# version 02
# baseline (speedup 1.0000x reference)
import sys
if '/opt/trn_rl_repo' not in sys.path:
    sys.path.insert(0, '/opt/trn_rl_repo')
import numpy as np
import ml_dtypes
import concourse.bass as bass
import concourse.mybir as mybir
from concourse.bass_utils import run_bass_kernel_spmd
from contextlib import ExitStack

F32 = mybir.dt.float32
BF16 = mybir.dt.bfloat16
I32 = mybir.dt.int32
AF = mybir.ActivationFunctionType
ALU = mybir.AluOpType
AX = mybir.AxisListType
BF = ml_dtypes.bfloat16

T = 4096
D = 1024
NEG = -1.0e9
LCAP0 = 1536
CAP0 = 10240
ZSLOT0 = 8 * LCAP0
LCAP1 = 1024
ZSLOT1 = 8 * LCAP1


class Seq:
    def __init__(self, nc, semg, semh, semc, semcc):
        self.nc = nc
        self.semg = semg
        self.semh = semh
        self.semc = semc
        self.semcc = semcc
        self.gval = 0
        self.hval = 0
        self.cval = 0
        self.ccval = 0
        self.streams = {}

    def _waits(self, e, gv, hv, cv, ccv):
        if gv:
            e.wait_ge(self.semg, gv)
        if hv:
            e.wait_ge(self.semh, hv)
        if cv:
            e.wait_ge(self.semc, cv)
        if ccv:
            e.wait_ge(self.semcc, ccv)

    def phase(self, eng, fn):
        gv, hv, cv, ccv = self.gval, self.hval, self.cval, self.ccval
        def emit(e, gv=gv, hv=hv, cv=cv, ccv=ccv, fn=fn):
            self._waits(e, gv, hv, cv, ccv)
            last = fn(e)
            last.then_inc(self.semc, 1)
        self.streams.setdefault(eng, []).append(emit)
        self.cval += 1

    def phase_dma(self, eng, fn, n):
        gv, hv, cv, ccv = self.gval, self.hval, self.cval, self.ccval
        sem = self.semg if eng == 'gpsimd' else self.semh
        def emit(e, gv=gv, hv=hv, cv=cv, ccv=ccv, fn=fn, n=n, sem=sem):
            self._waits(e, gv, hv, cv, ccv)
            insts = fn(e)
            assert len(insts) == n, f"dma phase expected {n} got {len(insts)}"
            for i in insts:
                i.then_inc(sem, 16)
        self.streams.setdefault(eng, []).append(emit)
        if eng == 'gpsimd':
            self.gval += 16 * n
        else:
            self.hval += 16 * n

    def phase_cc(self, fn):
        gv, hv, cv, ccv = self.gval, self.hval, self.cval, self.ccval
        def emit(e, gv=gv, hv=hv, cv=cv, ccv=ccv, fn=fn):
            self._waits(e, gv, hv, cv, ccv)
            fn(e).then_inc(self.semcc, 1)
        self.streams.setdefault('gpsimd', []).append(emit)
        self.ccval += 1

    def flush(self, block, final=False):
        fgv, fhv, fcv, fccv = self.gval, self.hval, self.cval, self.ccval
        streams = self.streams
        for eng in ('sync', 'tensor', 'vector', 'scalar', 'gpsimd'):
            def body(e, eng=eng):
                for emit in streams.get(eng, []):
                    emit(e)
                if final and eng == 'sync':
                    self._waits(e, fgv, fhv, fcv, fccv)
            getattr(block, eng)(body)
        self.streams = {}


class Section:
    counter = [0]
    gseq = [None]

    def __init__(self, nc):
        self.nc = nc
        self.ctx = ExitStack()
        self.sb = {}
        Section.counter[0] += 1
        self.uid = Section.counter[0]
        self.final = False

    def S(self, name, shape, dt=F32):
        t = self.ctx.enter_context(self.nc.sbuf_tensor(f"s{self.uid}_{name}", shape, dt))
        self.sb[name] = t
        return t

    def P(self, name, shape, dt=F32):
        t = self.ctx.enter_context(self.nc.psum_tensor(f"s{self.uid}_{name}", shape, dt))
        return t

    def begin(self):
        self.blk = self.ctx.enter_context(self.nc.Block())
        self.seq = Section.gseq[0]
        return self.seq

    def end(self):
        self.seq.flush(self.blk, final=self.final)
        self.ctx.close()


def emit_ln(sec, x_tiles, out_tiles, width, eps=1e-5):
    seq, sb = sec.seq, sec.sb
    nt = len(x_tiles)
    st = sb['lnstat']
    assert st.shape[1] >= 8*nt
    def vsum(e):
        last = None
        for i in range(nt):
            last = e.tensor_reduce(out=st[:, i:i+1], in_=x_tiles[i], axis=AX.X, op=ALU.add)
        return last
    seq.phase('vector', vsum)
    def ssq(e):
        last = None
        for i in range(nt):
            last = e.activation(out=sb['lnscr'][:, i, 0:width], in_=x_tiles[i], func=AF.Square,
                                accum_out=st[:, nt+i:nt+i+1])
        return last
    seq.phase('scalar', ssq)
    def v1(e):
        e.tensor_scalar_mul(st[:, 2*nt:3*nt], st[:, 0:nt], -1.0/width)
        return e.tensor_scalar_mul(st[:, 3*nt:4*nt], st[:, nt:2*nt], 1.0/width)
    seq.phase('vector', v1)
    seq.phase('vector', lambda e: e.tensor_tensor(out=st[:, 4*nt:5*nt], in0=st[:, 2*nt:3*nt], in1=st[:, 2*nt:3*nt], op=ALU.mult))
    seq.phase('vector', lambda e: e.tensor_tensor(out=st[:, 5*nt:6*nt], in0=st[:, 3*nt:4*nt], in1=st[:, 4*nt:5*nt], op=ALU.subtract))
    ec = 0 if eps == 1e-5 else 1
    seq.phase('scalar', lambda e: e.activation(out=st[:, 6*nt:7*nt], in_=st[:, 5*nt:6*nt], func=AF.Sqrt, bias=sb['scal'][:, ec:ec+1]))
    seq.phase('vector', lambda e: e.reciprocal(st[:, 7*nt:8*nt], st[:, 6*nt:7*nt]))
    def vapply(e):
        last = None
        for i in range(nt):
            last = e.tensor_scalar(out=out_tiles[i], in0=x_tiles[i],
                                   scalar1=st[:, 2*nt+i:2*nt+i+1], scalar2=st[:, 7*nt+i:7*nt+i+1],
                                   op0=ALU.add, op1=ALU.mult)
        return last
    seq.phase('vector', vapply)



def host_consts(core_id):
    C = {}
    C['IDB'] = np.eye(128, dtype=BF)
    C['IDF'] = np.eye(128, dtype=np.float32)
    C['ONESF'] = np.ones((128, 1), np.float32)
    C['ONESR'] = np.ones((1, 128), np.float32)
    NH, WINDOW = 16, 512
    slopes = np.array([(2.0**(-8.0/NH))**(h+1) for h in range(NH)], np.float32)
    i = np.arange(128)[:, None]; j = np.arange(640)[None, :]
    rel = (j - i - 512).astype(np.float32)
    valid = (rel <= 0) & (rel >= -WINDOW)
    C['RELB'] = np.where(valid[None], slopes[:, None, None]*rel[None], np.float32(NEG)).astype(BF)
    ii = np.arange(128)[:, None]; jj = np.arange(128)[None, :]
    C['DMASK'] = np.where(jj <= ii, 0.0, NEG).astype(np.float32)
    C['IOTA8'] = np.tile(np.arange(8, dtype=np.float32)[None, :], (128, 1))
    C['TRI128'] = (np.arange(128)[:, None] < np.arange(128)[None, :]).astype(np.float32)
    C['TRI16'] = (np.arange(16)[:, None] < np.arange(16)[None, :]).astype(np.float32)
    C['ALLONES'] = np.ones((128, 128), np.float32)
    C['SROWSEL'] = (np.arange(128)[:, None] % 16 == np.arange(16)[None, :]).astype(np.float32)
    C['CSEL'] = C['SROWSEL'].copy()
    sel = np.zeros((16, 16*128), np.float32)
    for s_ in range(16):
        sel[s_, s_*128:(s_+1)*128] = 1.0
    C['SELBANK'] = sel
    C['SCAL'] = np.tile(np.array([[1e-5, 1e-6, 0.0, 1.0]], np.float32), (128, 1))
    r = np.arange(128)
    C['CORETRI'] = (((r // 16)[:, None] < core_id) &
                    (r[:, None] % 16 == np.arange(16)[None, :])).astype(np.float32)
    C['CID'] = np.full((128, 1), float(core_id), np.float32)
    C['IDB64'] = np.vstack([np.eye(64, dtype=BF), np.eye(64, dtype=BF)])
    p = np.arange(128); t = p % 16
    f = np.arange(512); s = f % 16
    C['TMASKM'] = (t[:, None] <= s[None, :]).astype(BF)
    C['IND16'] = np.zeros((128, 8), BF)
    for g in range(4):
        for pp in range(2):
            C['IND16'][g*32+pp*16:(g*32+pp*16+16), g*2+pp] = 1
    C['IND16T'] = np.ascontiguousarray(C['IND16'].astype(np.float32).T)
    RESH = np.zeros((128, 32), np.float32)
    for jj_ in range(32):
        for pp in range(128):
            r_ = jj_*128 + pp
            pos, s_ = r_ // 16, r_ % 16
            RESH[pp, jj_] = (s_//2)*4096 + (s_ % 2)*2048 + pos
    C['RESHC'] = RESH
    RESH2 = np.zeros((128, 32), np.float32)
    for jj_ in range(32):
        for pp in range(128):
            r_ = jj_*128 + pp
            b_l, pos = r_ // 2048, r_ % 2048
            RESH2[pp, jj_] = (pos//256)*4096 + 256*b_l + (pos % 256)
    C['RESHC2'] = RESH2
    return C


def _asc(x):
    return np.ascontiguousarray(x)


def host_inputs(I, core_id):
    C = host_consts(core_id)
    x0 = np.asarray(I['L0_attn_x'])
    m = {
        "x0": _asc(x0[2*core_id:2*core_id+2].reshape(T, D)),
        "wqT": _asc(I['L0_attn_wq'].T), "wkT": _asc(I['L0_attn_wk'].T), "wvT": _asc(I['L0_attn_wv'].T),
        "qnw": _asc(np.asarray(I['L0_attn_q_norm_w']).reshape(8, 128).T),
        "knw": _asc(np.asarray(I['L0_attn_k_norm_w']).reshape(2, 128).T),
        "woT": _asc(I['L0_attn_wo'].T),
        "qkvT": _asc(I['L0_ffn_qkv_weight'].T),
        "W1REP": _asc(np.broadcast_to(I['L0_ffn_ln1_weight'], (128, D))),
        "B1REP": _asc(np.broadcast_to(I['L0_ffn_ln1_bias'], (128, D))),
        "prjT": _asc(I['L0_ffn_proj_weight'].T),
        "W2REP": _asc(np.broadcast_to(I['L0_ffn_ln2_weight'], (128, D))),
        "B2REP": _asc(np.broadcast_to(I['L0_ffn_ln2_bias'], (128, D))),
        "rw": _asc(I['L0_ffn_router_weight']),
        "w1": np.asarray(I['L0_ffn_moe_w1']), "w2": np.asarray(I['L0_ffn_moe_w2']),
        "Wq": _asc(I['L1_attn_Wq']), "Wkv": _asc(I['L1_attn_Wkv']), "Wo": _asc(I['L1_attn_Wo']),
        "centT": _asc(np.asarray(I['L1_ffn_centroids']).T),
        "w1T": _asc(np.asarray(I['L1_ffn_w1']).transpose(0, 2, 1)),
        "w2T": _asc(np.asarray(I['L1_ffn_w2']).transpose(0, 2, 1)),
        "LNG": _asc(np.asarray(I['L1_ffn_ln_g']).reshape(8, 8, 128).transpose(0, 2, 1)),
        "LNB": _asc(np.asarray(I['L1_ffn_ln_b']).reshape(8, 8, 128).transpose(0, 2, 1)),
        "B1C": _asc(np.asarray(I['L1_ffn_b1']).reshape(8, 32, 128).transpose(0, 2, 1)),
        "B2REP1": _asc(np.broadcast_to(np.asarray(I['L1_ffn_b2'])[:, None, :], (8, 128, D))),
    }
    for k in ('IDB', 'IDF', 'ONESF', 'ONESR', 'RELB', 'DMASK', 'IOTA8', 'TRI128', 'TRI16',
              'ALLONES', 'SROWSEL', 'CSEL', 'SELBANK', 'SCAL', 'CORETRI', 'CID',
              'TMASKM', 'IND16', 'IND16T', 'RESHC', 'RESHC2', 'IDB64'):
        m[k] = C[k]
    return m



def emit_s1_qkv0(sec, A):
    seq = sec.begin(); sb = sec.sb; S, P = sec.S, sec.P
    S('wq', [128, 8, 1024], BF16); S('wk', [128, 8, 256], BF16); S('wv', [128, 8, 256], BF16)
    S('qnw', [128, 8]); S('knw', [128, 2])
    S('idf', [128, 128]); S('onesf', [128, 1]); S('onesr', [1, 128]); S('idb', [128, 128], BF16)
    S('scal', [128, 4])
    S('xt', [128, 4, 1024]); S('ut', [128, 4, 1024])
    S('lnstat', [128, 32]); S('lnscr', [128, 4, 1024])
    S('ufm', [128, 8, 512], BF16)
    S('qraw', [128, 8, 512]); S('sq', [128, 512]); S('rstd1', [1, 512]); S('rstd2', [1, 512])
    S('outb', [128, 8, 512], BF16)
    S('vfm', [128, 2, 512], BF16)
    S('vtmS', [128, 4, 256], BF16)
    psA = P("psA", [128, 512]); psT = P("psT", [128, 512]); psR = P("psR", [1, 512])
    psB = P("psB", [128, 512]); psC = P("psC", [128, 256], BF16)

    seq.phase_dma('gpsimd', lambda e: [
        e.dma_start(out=sb['wq'][:], in_=A['wqT'].rearrange("(c p) o -> p c o", p=128)),
        e.dma_start(out=sb['wk'][:], in_=A['wkT'].rearrange("(c p) o -> p c o", p=128)),
        e.dma_start(out=sb['wv'][:], in_=A['wvT'].rearrange("(c p) o -> p c o", p=128)),
    ], 3)
    seq.phase_dma('sync', lambda e: [
        e.dma_start(out=sb['qnw'][:], in_=A['qnw']),
        e.dma_start(out=sb['knw'][:], in_=A['knw']),
        e.dma_start(out=sb['idf'][:], in_=A['IDF']),
        e.dma_start(out=sb['onesf'][:], in_=A['ONESF']),
        e.dma_start(out=sb['onesr'][:], in_=A['ONESR']),
        e.dma_start(out=sb['idb'][:], in_=A['IDB']),
        e.dma_start(out=sb['scal'][:], in_=A['SCAL']),
    ], 7)
    seq.phase('vector', lambda e: e.tensor_scalar_mul(sb['qnw'][:], sb['qnw'][:], 0.125))

    def rms_block(wname, nouts, nrm_name):
        for oc in range(nouts):
            def mm(e, oc=oc):
                last = None
                for fc in range(8):
                    last = e.matmul(psA[:], sb[wname][:, fc, oc*128:(oc+1)*128], sb['ufm'][:, fc, :],
                                    start=(fc == 0), stop=(fc == 7))
                return last
            seq.phase('tensor', mm)
            seq.phase('vector', lambda e, oc=oc: e.tensor_copy(out=sb['qraw'][:, oc, :], in_=psA[:]))
            seq.phase('scalar', lambda e, oc=oc: e.activation(out=sb['sq'][:], in_=sb['qraw'][:, oc, :], func=AF.Square))
            seq.phase('tensor', lambda e, oc=oc, n=nouts: e.matmul(psR[:], sb['onesf'][:], sb['sq'][:],
                                                                   start=(oc == 0), stop=(oc == n-1)))
        dim = nouts * 128
        seq.phase('vector', lambda e: e.tensor_scalar_mul(sb['rstd1'][:], psR[:], 1.0/dim))
        seq.phase('scalar', lambda e: e.activation(out=sb['rstd2'][:], in_=sb['rstd1'][:], func=AF.Sqrt, bias=sb['scal'][0:1, 1:2]))
        seq.phase('vector', lambda e: e.reciprocal(sb['rstd1'][:], sb['rstd2'][:]))
        seq.phase('tensor', lambda e: e.matmul(psB[:], sb['onesr'][:], sb['rstd1'][:], start=True, stop=True))
        def app(e):
            last = None
            for oc in range(nouts):
                last = e.scalar_tensor_tensor(out=sb['outb'][:, oc, :], in0=sb['qraw'][:, oc, :],
                                              scalar=sb[nrm_name][:, oc:oc+1], in1=psB[:],
                                              op0=ALU.mult, op1=ALU.mult)
            return last
        seq.phase('vector', app)

    for tc in range(8):
        seq.phase_dma('sync', lambda e, tc=tc: [
            e.dma_start(out=sb['xt'][:, i, :], in_=A['x0'][tc*512+i*128: tc*512+(i+1)*128, :]) for i in range(4)
        ], 4)
        emit_ln(sec, [sb['xt'][:, i, :] for i in range(4)], [sb['ut'][:, i, :] for i in range(4)], 1024)
        for fc in range(8):
            def tr(e, fc=fc):
                last = None
                for i in range(4):
                    last = e.transpose(psT[:, i*128:(i+1)*128], sb['ut'][:, i, fc*128:(fc+1)*128], sb['idf'][:])
                return last
            seq.phase('tensor', tr)
            seq.phase('vector', lambda e, fc=fc: e.tensor_copy(out=sb['ufm'][:, fc, :], in_=psT[:]))
        rms_block('wq', 8, 'qnw')
        seq.phase_dma('sync', lambda e, tc=tc: [
            e.dma_start(out=A['qf'][oc*128:(oc+1)*128, tc*512:(tc+1)*512], in_=sb['outb'][:, oc, :]) for oc in range(8)
        ], 8)
        rms_block('wk', 2, 'knw')
        seq.phase_dma('sync', lambda e, tc=tc: [
            e.dma_start(out=A['kf'][oc*128:(oc+1)*128, tc*512:(tc+1)*512], in_=sb['outb'][:, oc, :]) for oc in range(2)
        ], 2)
        for oc in range(2):
            def mmv(e, oc=oc):
                last = None
                for fc in range(8):
                    last = e.matmul(psA[:], sb['wv'][:, fc, oc*128:(oc+1)*128], sb['ufm'][:, fc, :],
                                    start=(fc == 0), stop=(fc == 7))
                return last
            seq.phase('tensor', mmv)
            seq.phase('vector', lambda e, oc=oc: e.tensor_copy(out=sb['vfm'][:, oc, :], in_=psA[:]))
        for i in range(4):
            def trv(e, i=i):
                e.transpose(psC[:, 0:128], sb['vfm'][:, 0, i*128:(i+1)*128], sb['idb'][:])
                return e.transpose(psC[:, 128:256], sb['vfm'][:, 1, i*128:(i+1)*128], sb['idb'][:])
            seq.phase('tensor', trv)
            seq.phase('vector', lambda e, i=i: e.tensor_copy(out=sb['vtmS'][:, i, :], in_=psC[:]))
        seq.phase_dma('sync', lambda e, tc=tc: [
            e.dma_start(out=A['vtm'][tc*512+i*128: tc*512+(i+1)*128, :], in_=sb['vtmS'][:, i, :]) for i in range(4)
        ], 4)
    sec.end()


def emit_s2_att0(sec, A):
    seq = sec.begin(); sb = sec.sb; S, P = sec.S, sec.P
    S('kg', [64, T], BF16)
    S('vtm', [128, 32, 256], BF16)
    S('rel', [128, 16, 640], BF16)
    S('wo', [128, 8, 1024], BF16)
    S('q', [64, 2048], BF16)
    S('sS', [128, 640]); S('sP', [128, 640], BF16); S('sPT', [128, 640], BF16)
    S('nm', [128, 1]); S('sums', [128, 1]); S('rs', [128, 1])
    S('aoS', [128, 64], BF16)
    S('aofm', [128, 8, T], BF16)
    S('x0t', [128, 1024]); S('x1t', [128, 1024])
    S('idb', [128, 128], BF16)
    psS = P("psS", [128, 640]); psPT = P("psPT", [128, 640], BF16)
    psO = P("psO", [128, 64]); psT2 = P("psT2", [64, 128], BF16)
    psPRJ = P("psPRJ", [128, 1024])
    seq.phase_dma('sync', lambda e: [
        e.dma_start(out=sb['vtm'][:], in_=A['vtm'].rearrange("(n p) f -> p n f", p=128)),
        e.dma_start(out=sb['rel'][:], in_=A['RELB'].rearrange("h p k -> p h k")),
        e.dma_start(out=sb['idb'][:], in_=A['IDB']),
    ], 3)
    seq.phase_dma('gpsimd', lambda e: [
        e.dma_start(out=sb['wo'][:], in_=A['woT'].rearrange("(c p) o -> p c o", p=128)),
    ], 1)
    for b in range(2):
        for h in range(16):
            g = h // 4
            fc_q = h // 2
            po_q = (h % 2) * 64
            if h % 4 == 0:
                seq.phase_dma('sync', lambda e, g=g: [
                    e.dma_start(out=sb['kg'][:], in_=A['kf'][g*64:(g+1)*64, :]),
                ], 1)
            seq.phase_dma('sync', lambda e, h=h, b=b: [
                e.dma_start(out=sb['q'][:], in_=A['qf'][h*64:(h+1)*64, b*2048:(b+1)*2048]),
            ], 1)
            for qt in range(16):
                off = max(0, 512 - 128*qt)
                kc0 = off // 128
                kstart = b*2048 + 128*qt - 512
                def qk(e, qt=qt, kc0=kc0, kstart=kstart):
                    last = None
                    for kc in range(kc0, 5):
                        ktile = (kstart + kc*128) // 128
                        last = e.matmul(psS[:, kc*128:(kc+1)*128],
                                        sb['q'][:, qt*128:(qt+1)*128],
                                        sb['kg'][:, ktile*128:(ktile+1)*128],
                                        start=True, stop=True)
                    return last
                seq.phase('tensor', qk)
                n = 640 - off
                seq.phase('vector', lambda e, h=h, off=off, n=n: e.scalar_tensor_tensor(
                    out=sb['sS'][:, 0:n], in0=psS[:, off:640], scalar=1.0,
                    in1=sb['rel'][:, h, off:640], op0=ALU.mult, op1=ALU.add))
                seq.phase('vector', lambda e, n=n: e.tensor_reduce(out=sb['nm'][:], in_=sb['sS'][:, 0:n], axis=AX.X, op=ALU.max, negate=True))
                seq.phase('scalar', lambda e, n=n: e.activation(out=sb['sP'][:, 0:n], in_=sb['sS'][:, 0:n], func=AF.Exp,
                                                                bias=sb['nm'][:, 0:1], accum_out=sb['sums'][:, 0:1]))
                def trp(e, kc0=kc0, off=off):
                    last = None
                    for kc in range(kc0, 5):
                        last = e.transpose(psPT[:, kc*128:(kc+1)*128], sb['sP'][:, kc*128-off:(kc+1)*128-off], sb['idb'][:])
                    return last
                seq.phase('tensor', trp)
                seq.phase('vector', lambda e, kc0=kc0: e.tensor_copy(out=sb['sPT'][:, kc0*128:640], in_=psPT[:, kc0*128:640]))
                def pv(e, g=g, kc0=kc0, kstart=kstart):
                    last = None
                    for kc in range(kc0, 5):
                        ktile = (kstart + kc*128) // 128
                        last = e.matmul(psO[:], sb['sPT'][:, kc*128:(kc+1)*128],
                                        sb['vtm'][:, ktile, g*64:(g+1)*64],
                                        start=(kc == kc0), stop=(kc == 4))
                    return last
                seq.phase('tensor', pv)
                seq.phase('vector', lambda e: e.reciprocal(sb['rs'][:], sb['sums'][:]))
                seq.phase('vector', lambda e: e.tensor_scalar(out=sb['aoS'][:], in0=psO[:], scalar1=sb['rs'][:, 0:1], scalar2=None, op0=ALU.mult))
                seq.phase('tensor', lambda e: e.transpose(psT2[:], sb['aoS'][:], sb['idb'][:]))
                seq.phase('vector', lambda e, b=b, qt=qt, fc_q=fc_q, po_q=po_q:
                          e.tensor_copy(out=sb['aofm'][po_q:po_q+64, fc_q, b*2048+qt*128: b*2048+(qt+1)*128], in_=psT2[:]))
    for tt in range(32):
        seq.phase_dma('sync', lambda e, tt=tt: [e.dma_start(out=sb['x0t'][:], in_=A['x0'][tt*128:(tt+1)*128, :])], 1)
        def prj(e, tt=tt):
            last = None
            for half in range(2):
                for fc in range(8):
                    last = e.matmul(psPRJ[:, half*512:(half+1)*512],
                                    sb['aofm'][:, fc, tt*128:(tt+1)*128],
                                    sb['wo'][:, fc, half*512:(half+1)*512],
                                    start=(fc == 0), stop=(fc == 7))
            return last
        seq.phase('tensor', prj)
        seq.phase('vector', lambda e: e.tensor_tensor(out=sb['x1t'][:], in0=sb['x0t'][:], in1=psPRJ[:], op=ALU.add))
        seq.phase_dma('sync', lambda e, tt=tt: [e.dma_start(out=A['x1loc'][tt*128:(tt+1)*128, :], in_=sb['x1t'][:])], 1)
    sec.end()


def emit_reshard(sec, A, src_loc, src_g, dst, base_name, coef):
    seq = sec.begin(); sb = sec.sb; S, P = sec.S, sec.P
    S('resh', [128, 32]); S('cid', [128, 1]); S('idxf', [128, 32]); S('idxi', [128, 32], I32)
    S('cids', [128, 1]); S('gt', [128, 1024])
    seq.phase_cc(lambda e: e.collective_compute(
        "AllGather", ALU.bypass, replica_groups=[list(range(8))],
        ins=[A[src_loc].opt()], outs=[A[src_g].opt()]))
    seq.phase_dma('sync', lambda e: [
        e.dma_start(out=sb['resh'][:], in_=A[base_name]),
        e.dma_start(out=sb['cid'][:], in_=A['CID']),
    ], 2)
    seq.phase('vector', lambda e: e.tensor_scalar_mul(sb['cids'][:], sb['cid'][:], float(coef)))
    seq.phase('vector', lambda e: e.tensor_scalar(out=sb['idxf'][:], in0=sb['resh'][:], scalar1=1.0,
                                                  scalar2=sb['cids'][:, 0:1], op0=ALU.mult, op1=ALU.add))
    seq.phase('vector', lambda e: e.tensor_copy(out=sb['idxi'][:], in_=sb['idxf'][:]))
    for j in range(32):
        seq.phase_dma('gpsimd', lambda e, j=j: [
            e.indirect_dma_start(out=sb['gt'][:], out_offset=None, in_=A[src_g],
                                 in_offset=bass.IndirectOffsetOnAxis(ap=sb['idxi'][:, j:j+1], axis=0)),
        ], 1)
        seq.phase_dma('sync', lambda e, j=j: [
            e.dma_start(out=A[dst][j*128:(j+1)*128, :], in_=sb['gt'][:]),
        ], 1)
    sec.end()


def emit_s4_ffn0a(sec, A):
    seq = sec.begin(); sb = sec.sb; S, P = sec.S, sec.P
    S('wk', [128, 8, 1536], BF16)
    S('idf', [128, 128]); S('scal', [128, 4])
    S('w1rep', [128, 1024]); S('b1rep', [128, 1024])
    S('xt', [128, 4, 1024]); S('ut', [128, 4, 1024]); S('lt', [128, 4, 1024])
    S('lnstat', [128, 32]); S('lnscr', [128, 4, 1024])
    S('ufm', [128, 8, 512], BF16)
    S('outb', [128, 12, 512], BF16)
    psA = P("psA", [128, 512]); psT = P("psT", [128, 512])
    seq.phase_dma('gpsimd', lambda e: [
        e.dma_start(out=sb['wk'][:], in_=A['qkvT'].rearrange("(c p) o -> p c o", p=128)),
    ], 1)
    seq.phase_dma('sync', lambda e: [
        e.dma_start(out=sb['idf'][:], in_=A['IDF']),
        e.dma_start(out=sb['scal'][:], in_=A['SCAL']),
        e.dma_start(out=sb['w1rep'][:], in_=A['W1REP']),
        e.dma_start(out=sb['b1rep'][:], in_=A['B1REP']),
    ], 4)
    for tc in range(8):
        seq.phase_dma('sync', lambda e, tc=tc: [
            e.dma_start(out=sb['xt'][:, i, :], in_=A['x1r'][tc*512+i*128: tc*512+(i+1)*128, :]) for i in range(4)
        ], 4)
        emit_ln(sec, [sb['xt'][:, i, :] for i in range(4)], [sb['ut'][:, i, :] for i in range(4)], 1024)
        seq.phase_dma('sync', lambda e, tc=tc: [
            e.dma_start(out=A['u1'][tc*512+i*128: tc*512+(i+1)*128, :], in_=sb['ut'][:, i, :]) for i in range(4)
        ], 4)
        def lw(e):
            last = None
            for i in range(4):
                last = e.tensor_tensor(out=sb['lt'][:, i, :], in0=sb['ut'][:, i, :], in1=sb['w1rep'][:], op=ALU.mult)
            return last
        seq.phase('vector', lw)
        def lb(e):
            last = None
            for i in range(4):
                last = e.tensor_tensor(out=sb['lt'][:, i, :], in0=sb['lt'][:, i, :], in1=sb['b1rep'][:], op=ALU.add)
            return last
        seq.phase('vector', lb)
        for fc in range(8):
            def tr(e, fc=fc):
                last = None
                for i in range(4):
                    last = e.transpose(psT[:, i*128:(i+1)*128], sb['lt'][:, i, fc*128:(fc+1)*128], sb['idf'][:])
                return last
            seq.phase('tensor', tr)
            seq.phase('vector', lambda e, fc=fc: e.tensor_copy(out=sb['ufm'][:, fc, :], in_=psT[:]))
        for oc in range(12):
            def mm(e, oc=oc):
                last = None
                for fc in range(8):
                    last = e.matmul(psA[:], sb['wk'][:, fc, oc*128:(oc+1)*128], sb['ufm'][:, fc, :],
                                    start=(fc == 0), stop=(fc == 7))
                return last
            seq.phase('tensor', mm)
            g, j = oc // 3, oc % 3
            if j < 2:
                seq.phase('vector', lambda e, oc=oc: e.tensor_scalar_mul(sb['outb'][:, oc, :], psA[:], 0.125))
            else:
                seq.phase('vector', lambda e, oc=oc: e.tensor_copy(out=sb['outb'][:, oc, :], in_=psA[:]))
        def st(e, tc=tc):
            ds = []
            for oc in range(12):
                g, j = oc // 3, oc % 3
                if j < 2:
                    ds.append(e.dma_start(out=A['qtf'][(g*256+j*128):(g*256+j*128+128), tc*512:(tc+1)*512],
                                          in_=sb['outb'][:, oc, :]))
                else:
                    ds.append(e.dma_start(out=A['ktf'][g*64:(g+1)*64, tc*512:(tc+1)*512],
                                          in_=sb['outb'][0:64, oc, :]))
                    ds.append(e.dma_start(out=A['vtf'][g*64:(g+1)*64, tc*512:(tc+1)*512],
                                          in_=sb['outb'][64:128, oc, :]))
            return ds
        seq.phase_dma('sync', st, 16)
    sec.end()


S5STAGE = [99]

def emit_s5_tiny(sec, A):
    seq = sec.begin(); sb = sec.sb; S, P = sec.S, sec.P
    S('idb', [128, 128], BF16)
    S('idb64', [128, 64], BF16)
    S('tmask', [128, 512], BF16)
    S('ind16', [128, 8], BF16)
    S('ind16t', [8, 128])
    S('vtfs', [128, 2, T], BF16)
    S('kbd', [128, 4, 1024], BF16)
    S('qmv', [128, 4, 2048], BF16)
    S('vmvT', [128, 4, 8, 64], BF16)
    S('sE', [128, 512], BF16)
    S('mE', [128, 512], BF16)
    S('rD', [8, 512])
    S('sP', [128, 512], BF16)
    S('pbd', [128, 4, 1024], BF16)
    S('aostage', [128, 2, 4096], BF16)
    psSC = P("psSC", [128, 512])
    psD = P("psD", [8, 512])
    psB = P("psB", [128, 512])
    psAO = [P(f"psAO{b4}", [128, 256]) for b4 in range(4)]
    psTv = P("psTv", [128, 512], BF16)
    seq.phase_dma('sync', lambda e: [
        e.dma_start(out=sb['idb'][:], in_=A['IDB']),
        e.dma_start(out=sb['idb64'][:], in_=A['IDB64']),
        e.dma_start(out=sb['tmask'][:], in_=A['TMASKM']),
        e.dma_start(out=sb['ind16'][:], in_=A['IND16']),
        e.dma_start(out=sb['ind16t'][:], in_=A['IND16T']),
    ], 5)
    seq.phase('vector', lambda e: e.memset(sb['pbd'][:], 0.0))
    seq.phase('vector', lambda e: e.memset(sb['kbd'][:], 0.0))
    seq.phase_dma('gpsimd', lambda e: [
        e.dma_start(out=sb['vtfs'][:], in_=A['vtf'].rearrange("(c p) t -> p c t", p=128)),
    ], 1)
    for ch in range(4):
        chp = ch * 32
        def ldk(e, chp=chp):
            ds = []
            for g in range(4):
                for pp in range(2):
                    ds.append(e.dma_start(
                        out=sb['kbd'][pp*64:(pp+1)*64, g, :].rearrange("p (a t) -> p a t", t=32)[:, :, pp*16:(pp+1)*16],
                        in_=A['ktf'][g*64:(g+1)*64, :].rearrange("p (a w) -> p a w", w=32)[:, chp:chp+32, pp*16:(pp+1)*16]))
            return ds
        seq.phase_dma('sync', ldk, 8)
        def ldq(e, chp=chp):
            ds = []
            for g in range(4):
                for qi in range(4):
                    for pp in range(2):
                        ds.append(e.dma_start(
                            out=sb['qmv'][pp*64:(pp+1)*64, g, :].rearrange("p (a x) -> p a x", x=64)[:, :, qi*16:(qi+1)*16],
                            in_=A['qtf'][g*256+qi*64:g*256+qi*64+64, :].rearrange("p (a w) -> p a w", w=32)[:, chp:chp+32, pp*16:(pp+1)*16]))
            return ds
        seq.phase_dma('gpsimd', ldq, 32)
        for g in range(4):
            def trv(e, g=g, ch=ch):
                last = None
                for tt in range(8):
                    last = e.transpose(psTv[:, tt*64:(tt+1)*64],
                                       sb['vtfs'][(g % 2)*64:(g % 2)*64+64, g//2, ch*1024+tt*128: ch*1024+(tt+1)*128],
                                       sb['idb64'][(g % 2)*64:(g % 2)*64+64, :])
                return last
            seq.phase('tensor', trv)
            seq.phase('vector', lambda e, g=g: e.tensor_copy(
                out=sb['vmvT'][:, g, :, :].rearrange("p a d -> p (a d)"), in_=psTv[:]))
        if S5STAGE[0] < 1:
            continue
        for sg in range(4):
            def qk(e, sg=sg):
                last = None
                for g in range(4):
                    for pi in range(8):
                        pr = sg*8 + pi
                        last = e.matmul(psSC[g*32:(g+1)*32, pi*64:(pi+1)*64],
                                        sb['kbd'][:, g, pr*32:(pr+1)*32],
                                        sb['qmv'][:, g, pr*64:(pr+1)*64],
                                        start=True, stop=True, tile_position=(0, g*32))
                return last
            seq.phase('tensor', qk)
            if S5STAGE[0] < 2:
                continue
            seq.phase('scalar', lambda e: e.activation(out=sb['sE'][:], in_=psSC[:], func=AF.Exp))
            seq.phase('vector', lambda e: e.tensor_tensor(out=sb['mE'][:], in0=sb['sE'][:], in1=sb['tmask'][:], op=ALU.mult))
            seq.phase('tensor', lambda e: e.matmul(psD[:], sb['ind16'][:], sb['mE'][:], start=True, stop=True))
            seq.phase('vector', lambda e: e.reciprocal(sb['rD'][:], psD[:]))
            seq.phase('tensor', lambda e: e.matmul(psB[:], sb['ind16t'][:], sb['rD'][:], start=True, stop=True))
            seq.phase('vector', lambda e: e.tensor_tensor(out=sb['sP'][:], in0=sb['mE'][:], in1=psB[:], op=ALU.mult))
            if S5STAGE[0] < 3:
                continue
            def pba(e):
                ds = []
                for g in range(4):
                    for pp in range(2):
                        for b4 in range(4):
                            ds.append(e.dma_start(
                                out=sb['pbd'][b4*32+pp*16: b4*32+pp*16+16, g, :]
                                    .rearrange("p (a x) -> p a x", x=128)[:, b4::4, pp*64:(pp+1)*64],
                                in_=sb['sP'][g*32+pp*16: g*32+pp*16+16, :]
                                    .rearrange("p (a x) -> p a x", x=64)[:, b4::4, :]))
                return ds
            seq.phase_dma('gpsimd', pba, 32)
            if S5STAGE[0] < 4:
                continue
            for rnd in range(2):
                def ao(e, sg=sg, rnd=rnd):
                    last = None
                    for b4 in range(4):
                        pi = rnd*4 + b4
                        tt = (sg*8 + pi) // 4
                        for g in range(4):
                            last = e.matmul(psAO[b4][(g % 2)*64:(g % 2)*64+64, (g//2)*128:(g//2)*128+128],
                                            sb['vmvT'][b4*32:(b4+1)*32, g, tt, :],
                                            sb['pbd'][b4*32:(b4+1)*32, g, pi*128:(pi+1)*128],
                                            start=True, stop=True, tile_position=(b4*32, (g % 2)*64))
                    return last
                seq.phase('tensor', ao)
                def ev(e, sg=sg, rnd=rnd):
                    last = None
                    for b4 in range(4):
                        pi = rnd*4 + b4
                        last = e.tensor_copy(
                            out=sb['aostage'][:, :, sg*1024+pi*128: sg*1024+(pi+1)*128],
                            in_=psAO[b4].rearrange("p (g x) -> p g x", x=128))
                    return last
                seq.phase('vector', ev)
        if S5STAGE[0] < 5:
            continue
        def afm(e, ch=ch):
            ds = []
            for g in range(4):
                for qi in range(4):
                    for pp in range(2):
                        src = sb['aostage'][(g % 2)*64:(g % 2)*64+64, g//2, :] \
                            .rearrange("p (a b x) -> p a b x", b=8, x=128)[:, :, :, pp*64+qi*16: pp*64+qi*16+16]
                        dst = A['aofm_t'][g*256+qi*64: g*256+qi*64+64, :] \
                            .rearrange("p (a b pz s) -> p a b pz s", b=8, pz=2, s=16)[:, ch*4:(ch+1)*4, :, pp, :]
                        ds.append(e.dma_start(out=dst, in_=src))
            return ds
        seq.phase_dma('gpsimd', afm, 32)
    sec.end()


def emit_s6_ffn0b(sec, A):
    seq = sec.begin(); sb = sec.sb; S, P = sec.S, sec.P
    S('aofm', [128, 8, T], BF16)
    S('pw', [128, 8, 1024], BF16)
    S('rw', [128, 8, 8], BF16)
    S('iota8', [128, 8]); S('srow', [128, 16])
    S('idf', [128, 128]); S('scal', [128, 4])
    S('w2rep', [128, 1024]); S('b2rep', [128, 1024])
    S('xt', [128, 1, 1024]); S('ut', [128, 1, 1024])
    S('lt', [128, 1024]); S('l2b', [128, 1024], BF16)
    S('l2fm', [128, 8, 128], BF16)
    S('lnstat', [128, 32]); S('lnscr', [128, 1, 1024])
    S('u1t', [128, 1024])
    S('lg', [128, 8]); S('mx', [128, 8]); S('oh1', [128, 8]); S('oh2', [128, 8])
    S('msk', [128, 8]); S('indt', [128, 8])
    S('e1c', [128, 8]); S('e2c', [128, 8])
    S('rt', [128, 8]); S('Rt', [16, 8])
    psPRJ = P("psPRJ", [128, 1024]); psT = P("psT", [128, 512])
    psLG = P("psLG", [128, 8]); psRT = P("psRT", [16, 8])
    seq.phase_dma('gpsimd', lambda e: [
        e.dma_start(out=sb['aofm'][:], in_=A['aofm_t'].rearrange("(c p) t -> p c t", p=128)),
        e.dma_start(out=sb['pw'][:], in_=A['prjT'].rearrange("(c p) o -> p c o", p=128)),
        e.dma_start(out=sb['rw'][:], in_=A['rw'].rearrange("(c p) o -> p c o", p=128)),
    ], 3)
    seq.phase_dma('sync', lambda e: [
        e.dma_start(out=sb['iota8'][:], in_=A['IOTA8']),
        e.dma_start(out=sb['srow'][:], in_=A['SROWSEL']),
        e.dma_start(out=sb['idf'][:], in_=A['IDF']),
        e.dma_start(out=sb['scal'][:], in_=A['SCAL']),
        e.dma_start(out=sb['w2rep'][:], in_=A['W2REP']),
        e.dma_start(out=sb['b2rep'][:], in_=A['B2REP']),
    ], 6)
    for tt in range(32):
        seq.phase_dma('sync', lambda e, tt=tt: [
            e.dma_start(out=sb['u1t'][:], in_=A['u1'][tt*128:(tt+1)*128, :]),
        ], 1)
        def prj(e, tt=tt):
            last = None
            for half in range(2):
                for fc in range(8):
                    last = e.matmul(psPRJ[:, half*512:(half+1)*512],
                                    sb['aofm'][:, fc, tt*128:(tt+1)*128],
                                    sb['pw'][:, fc, half*512:(half+1)*512],
                                    start=(fc == 0), stop=(fc == 7))
            return last
        seq.phase('tensor', prj)
        seq.phase('vector', lambda e: e.tensor_tensor(out=sb['xt'][:, 0, :], in0=sb['u1t'][:], in1=psPRJ[:], op=ALU.add))
        seq.phase_dma('sync', lambda e, tt=tt: [
            e.dma_start(out=A['h1'][tt*128:(tt+1)*128, :], in_=sb['xt'][:, 0, :]),
        ], 1)
        emit_ln(sec, [sb['xt'][:, 0, :]], [sb['ut'][:, 0, :]], 1024)
        seq.phase('vector', lambda e: e.tensor_tensor(out=sb['lt'][:], in0=sb['ut'][:, 0, :], in1=sb['w2rep'][:], op=ALU.mult))
        seq.phase('vector', lambda e: e.tensor_tensor(out=sb['lt'][:], in0=sb['lt'][:], in1=sb['b2rep'][:], op=ALU.add))
        seq.phase('vector', lambda e: e.tensor_copy(out=sb['l2b'][:], in_=sb['lt'][:]))
        seq.phase_dma('sync', lambda e, tt=tt: [
            e.dma_start(out=A['ln2tm'][tt*128:(tt+1)*128, :], in_=sb['l2b'][:]),
        ], 1)
        for fc in range(4):
            def tr(e, fc=fc):
                e.transpose(psT[:, 0:128], sb['lt'][:, (2*fc)*128:(2*fc+1)*128], sb['idf'][:])
                return e.transpose(psT[:, 128:256], sb['lt'][:, (2*fc+1)*128:(2*fc+2)*128], sb['idf'][:])
            seq.phase('tensor', tr)
            def ev(e, fc=fc):
                e.tensor_copy(out=sb['l2fm'][:, 2*fc, :], in_=psT[:, 0:128])
                return e.tensor_copy(out=sb['l2fm'][:, 2*fc+1, :], in_=psT[:, 128:256])
            seq.phase('vector', ev)
        def rmm(e):
            last = None
            for fc in range(8):
                last = e.matmul(psLG[:], sb['l2fm'][:, fc, :], sb['rw'][:, fc, :],
                                start=(fc == 0), stop=(fc == 7))
            return last
        seq.phase('tensor', rmm)
        seq.phase('vector', lambda e: e.tensor_copy(out=sb['lg'][:], in_=psLG[:]))
        seq.phase('vector', lambda e: e.max(out=sb['mx'][:], in_=sb['lg'][:]))
        seq.phase('vector', lambda e: e.tensor_scalar(out=sb['oh1'][:], in0=sb['lg'][:], scalar1=sb['mx'][:, 0:1], scalar2=None, op0=ALU.is_equal))
        seq.phase('vector', lambda e: e.scalar_tensor_tensor(out=sb['msk'][:], in0=sb['oh1'][:], scalar=NEG, in1=sb['lg'][:], op0=ALU.mult, op1=ALU.add))
        seq.phase('vector', lambda e: e.tensor_scalar(out=sb['oh2'][:], in0=sb['msk'][:], scalar1=sb['mx'][:, 1:2], scalar2=None, op0=ALU.is_equal))
        seq.phase('vector', lambda e: e.tensor_tensor(out=sb['indt'][:], in0=sb['oh1'][:], in1=sb['oh2'][:], op=ALU.add))
        def ec(e):
            e.tensor_tensor(out=sb['e1c'][:], in0=sb['oh1'][:], in1=sb['iota8'][:], op=ALU.mult)
            return e.tensor_tensor(out=sb['e2c'][:], in0=sb['oh2'][:], in1=sb['iota8'][:], op=ALU.mult)
        seq.phase('vector', ec)
        def er(e):
            e.tensor_reduce(out=sb['rt'][:, 0:1], in_=sb['e1c'][:], axis=AX.X, op=ALU.add)
            return e.tensor_reduce(out=sb['rt'][:, 1:2], in_=sb['e2c'][:], axis=AX.X, op=ALU.add)
        seq.phase('vector', er)
        seq.phase('vector', lambda e: e.tensor_tensor(out=sb['rt'][:, 4:5], in0=sb['mx'][:, 0:1], in1=sb['mx'][:, 1:2], op=ALU.subtract))
        seq.phase('scalar', lambda e: e.activation(out=sb['rt'][:, 2:3], in_=sb['rt'][:, 4:5], func=AF.Sigmoid))
        seq.phase('vector', lambda e: e.tensor_scalar(out=sb['rt'][:, 3:4], in0=sb['rt'][:, 2:3], scalar1=-1.0, scalar2=1.0, op0=ALU.mult, op1=ALU.add))
        seq.phase('tensor', lambda e, tt=tt: e.matmul(psRT[:], sb['srow'][:], sb['indt'][:],
                                                      start=(tt == 0), stop=(tt == 31)))
        seq.phase_dma('sync', lambda e, tt=tt: [
            e.dma_start(out=A['ind'][tt*128:(tt+1)*128, :], in_=sb['indt'][:]),
            e.dma_start(out=A['route'][tt*128:(tt+1)*128, :], in_=sb['rt'][:, 0:4]),
        ], 2)
    seq.phase('vector', lambda e: e.tensor_copy(out=sb['Rt'][:], in_=psRT[:]))
    seq.phase_dma('sync', lambda e: [e.dma_start(out=A['Rloc'], in_=sb['Rt'][:])], 1)
    sec.end()


def emit_s7_rgather(sec, A):
    seq = sec.begin()
    seq.phase_cc(lambda e: e.collective_compute(
        "AllGather", ALU.bypass, replica_groups=[list(range(8))],
        ins=[A['Rloc'].opt()], outs=[A['Rg'].opt()]))
    sec.end()


def emit_s8_disp(sec, A):
    seq = sec.begin(); sb = sec.sb; S, P = sec.S, sec.P
    for nm, shp in [('tri', [128, 128]), ('allones', [128, 128]), ('csel', [128, 16]),
                    ('coretri', [128, 16]), ('iota8', [128, 8]), ('onesf', [128, 1])]:
        S(nm, shp)
    S('tri16', [16, 16]); S('selbank', [16, 16*128]); S('onesr', [1, 128])
    S('Rall', [128, 8]); S('G', [16, 8]); S('SOFF', [16, 8])
    S('ind0', [128, 8]); S('ind1', [128, 8])
    S('oh1', [128, 8]); S('oh2', [128, 8])
    S('rt', [128, 4])
    S('lnt', [128, 1024], BF16)
    S('vm', [128, 8]); S('tmpA', [128, 8]); S('tmpB', [128, 8])
    S('vindt', [128, 8]); S('slotsS', [128, 8])
    S('tmpC', [128, 8]); S('tmpD', [128, 8])
    S('wt', [128, 24])
    S('ixt1', [128, 1], I32); S('ixt2', [128, 1], I32)
    S('pvt', [128, 2])
    S('rtA', [1, 8]); S('rtB', [1, 8]); S('rtS', [1, 8])
    psPF = P("psPF", [128, 8]); psSL = P("psSL", [128, 8]); psR2 = P("psR2", [1, 8])
    psG = P("psG", [16, 8]); psSO = P("psSO", [16, 8])
    seq.phase_dma('sync', lambda e: [
        e.dma_start(out=sb['tri'][:], in_=A['TRI128']),
        e.dma_start(out=sb['tri16'][:], in_=A['TRI16']),
        e.dma_start(out=sb['allones'][:], in_=A['ALLONES']),
        e.dma_start(out=sb['csel'][:], in_=A['CSEL']),
        e.dma_start(out=sb['coretri'][:], in_=A['CORETRI']),
        e.dma_start(out=sb['selbank'][:], in_=A['SELBANK']),
        e.dma_start(out=sb['iota8'][:], in_=A['IOTA8']),
        e.dma_start(out=sb['onesf'][:], in_=A['ONESF']),
        e.dma_start(out=sb['onesr'][:], in_=A['ONESR']),
        e.dma_start(out=sb['Rall'][:], in_=A['Rg']),
    ], 10)
    seq.phase('tensor', lambda e: e.matmul(psG[:], sb['csel'][:], sb['Rall'][:], start=True, stop=True))
    seq.phase('vector', lambda e: e.tensor_copy(out=sb['G'][:], in_=psG[:]))
    def soff(e):
        e.matmul(psSO[:], sb['tri16'][:], sb['G'][:], start=True, stop=False)
        return e.matmul(psSO[:], sb['coretri'][:], sb['Rall'][:], start=False, stop=True)
    seq.phase('tensor', soff)
    seq.phase('vector', lambda e: e.tensor_copy(out=sb['SOFF'][:], in_=psSO[:]))
    seq.phase('vector', lambda e: e.memset(sb['rtA'][:], 0.0))
    ind_r = A['ind'].rearrange("(p s) e -> s p e", s=16)
    rt_r = A['route'].rearrange("(p s) e -> s p e", s=16)
    ln_r = A['ln2tm'].rearrange("(p s) f -> s p f", s=16)
    for j in range(32):
        s_, pb_ = j // 2, j % 2
        cur = 'ind0' if j % 2 == 0 else 'ind1'
        prev = 'ind1' if j % 2 == 0 else 'ind0'
        rt_cur = 'rtA' if j % 2 == 0 else 'rtB'
        rt_nxt = 'rtB' if j % 2 == 0 else 'rtA'
        seq.phase_dma('sync', lambda e, s_=s_, pb_=pb_, cur=cur: [
            e.dma_start(out=sb[cur][:], in_=ind_r[s_, pb_*128:(pb_+1)*128, :]),
            e.dma_start(out=sb['rt'][:], in_=rt_r[s_, pb_*128:(pb_+1)*128, :]),
            e.dma_start(out=sb['lnt'][:], in_=ln_r[s_, pb_*128:(pb_+1)*128, :]),
        ], 3)
        def g1(e, s_=s_, pb_=pb_, cur=cur, prev=prev):
            e.matmul(psPF[:], sb['tri'][:], sb[cur][:], start=True, stop=False)
            if pb_ == 1:
                e.matmul(psPF[:], sb['allones'][:], sb[prev][:], start=False, stop=False)
            return e.matmul(psPF[:], sb['selbank'][:, s_*128:(s_+1)*128], sb['SOFF'][:],
                            start=False, stop=True)
        seq.phase('tensor', g1)
        seq.phase('vector', lambda e: e.tensor_scalar(out=sb['vm'][:], in0=psPF[:], scalar1=float(CAP0), scalar2=None, op0=ALU.is_lt))
        def v2(e):
            e.tensor_scalar(out=sb['oh1'][:], in0=sb['iota8'][:], scalar1=sb['rt'][:, 0:1], scalar2=None, op0=ALU.is_equal)
            return e.tensor_scalar(out=sb['oh2'][:], in0=sb['iota8'][:], scalar1=sb['rt'][:, 1:2], scalar2=None, op0=ALU.is_equal)
        seq.phase('vector', v2)
        def v3(e):
            e.tensor_tensor(out=sb['tmpA'][:], in0=sb['vm'][:], in1=sb['oh1'][:], op=ALU.mult)
            return e.tensor_tensor(out=sb['tmpB'][:], in0=sb['vm'][:], in1=sb['oh2'][:], op=ALU.mult)
        seq.phase('vector', v3)
        def v4(e):
            e.tensor_reduce(out=sb['wt'][:, 4:5], in_=sb['tmpA'][:], axis=AX.X, op=ALU.add)
            return e.tensor_reduce(out=sb['wt'][:, 5:6], in_=sb['tmpB'][:], axis=AX.X, op=ALU.add)
        seq.phase('vector', v4)
        def v5(e):
            e.tensor_scalar(out=sb['tmpA'][:], in0=sb['oh1'][:], scalar1=sb['wt'][:, 4:5], scalar2=None, op0=ALU.mult)
            return e.tensor_scalar(out=sb['tmpB'][:], in0=sb['oh2'][:], scalar1=sb['wt'][:, 5:6], scalar2=None, op0=ALU.mult)
        seq.phase('vector', v5)
        seq.phase('vector', lambda e: e.tensor_tensor(out=sb['vindt'][:], in0=sb['tmpA'][:], in1=sb['tmpB'][:], op=ALU.add))
        def g2(e, rt_cur=rt_cur):
            e.matmul(psSL[:], sb['onesr'][:], sb[rt_cur][:], start=True, stop=False)
            return e.matmul(psSL[:], sb['tri'][:], sb['vindt'][:], start=False, stop=True)
        seq.phase('tensor', g2)
        seq.phase('tensor', lambda e: e.matmul(psR2[:], sb['onesf'][:], sb['vindt'][:], start=True, stop=True))
        def v7(e):
            e.tensor_copy(out=sb['slotsS'][:], in_=psSL[:])
            return e.tensor_copy(out=sb['rtS'][:], in_=psR2[:])
        seq.phase('vector', v7)
        seq.phase('vector', lambda e, rt_cur=rt_cur, rt_nxt=rt_nxt: e.tensor_tensor(out=sb[rt_nxt][:], in0=sb[rt_cur][:], in1=sb['rtS'][:], op=ALU.add))
        def v9a(e):
            e.tensor_tensor(out=sb['tmpC'][:], in0=sb['slotsS'][:], in1=sb['tmpA'][:], op=ALU.mult)
            return e.tensor_tensor(out=sb['tmpD'][:], in0=sb['slotsS'][:], in1=sb['tmpB'][:], op=ALU.mult)
        seq.phase('vector', v9a)
        def v9b(e):
            e.tensor_reduce(out=sb['wt'][:, 6:7], in_=sb['tmpC'][:], axis=AX.X, op=ALU.add)
            return e.tensor_reduce(out=sb['wt'][:, 7:8], in_=sb['tmpD'][:], axis=AX.X, op=ALU.add)
        seq.phase('vector', v9b)
        def v10(e):
            e.tensor_scalar(out=sb['wt'][:, 8:9], in0=sb['wt'][:, 6:7], scalar1=float(LCAP0), scalar2=None, op0=ALU.is_lt)
            return e.tensor_scalar(out=sb['wt'][:, 9:10], in0=sb['wt'][:, 7:8], scalar1=float(LCAP0), scalar2=None, op0=ALU.is_lt)
        seq.phase('vector', v10)
        def v11(e):
            e.tensor_tensor(out=sb['wt'][:, 10:11], in0=sb['wt'][:, 4:5], in1=sb['wt'][:, 8:9], op=ALU.mult)
            return e.tensor_tensor(out=sb['wt'][:, 11:12], in0=sb['wt'][:, 5:6], in1=sb['wt'][:, 9:10], op=ALU.mult)
        seq.phase('vector', v11)
        def v12(e):
            e.tensor_scalar(out=sb['wt'][:, 12:13], in0=sb['rt'][:, 0:1], scalar1=float(LCAP0), scalar2=sb['wt'][:, 6:7], op0=ALU.mult, op1=ALU.add)
            return e.tensor_scalar(out=sb['wt'][:, 13:14], in0=sb['rt'][:, 1:2], scalar1=float(LCAP0), scalar2=sb['wt'][:, 7:8], op0=ALU.mult, op1=ALU.add)
        seq.phase('vector', v12)
        def v13(e):
            e.tensor_tensor(out=sb['wt'][:, 14:15], in0=sb['wt'][:, 12:13], in1=sb['wt'][:, 10:11], op=ALU.mult)
            return e.tensor_tensor(out=sb['wt'][:, 15:16], in0=sb['wt'][:, 13:14], in1=sb['wt'][:, 11:12], op=ALU.mult)
        seq.phase('vector', v13)
        def v14(e):
            e.tensor_scalar(out=sb['wt'][:, 16:17], in0=sb['wt'][:, 10:11], scalar1=-float(ZSLOT0), scalar2=float(ZSLOT0), op0=ALU.mult, op1=ALU.add)
            return e.tensor_scalar(out=sb['wt'][:, 17:18], in0=sb['wt'][:, 11:12], scalar1=-float(ZSLOT0), scalar2=float(ZSLOT0), op0=ALU.mult, op1=ALU.add)
        seq.phase('vector', v14)
        def v15(e):
            e.tensor_tensor(out=sb['wt'][:, 18:19], in0=sb['wt'][:, 14:15], in1=sb['wt'][:, 16:17], op=ALU.add)
            return e.tensor_tensor(out=sb['wt'][:, 19:20], in0=sb['wt'][:, 15:16], in1=sb['wt'][:, 17:18], op=ALU.add)
        seq.phase('vector', v15)
        def v16(e):
            e.tensor_copy(out=sb['ixt1'][:], in_=sb['wt'][:, 18:19])
            return e.tensor_copy(out=sb['ixt2'][:], in_=sb['wt'][:, 19:20])
        seq.phase('vector', v16)
        def v17(e):
            e.tensor_tensor(out=sb['pvt'][:, 0:1], in0=sb['rt'][:, 2:3], in1=sb['wt'][:, 10:11], op=ALU.mult)
            return e.tensor_tensor(out=sb['pvt'][:, 1:2], in0=sb['rt'][:, 3:4], in1=sb['wt'][:, 11:12], op=ALU.mult)
        seq.phase('vector', v17)
        seq.phase_dma('sync', lambda e, j=j: [
            e.dma_start(out=A['dst1'][j*128:(j+1)*128, :], in_=sb['ixt1'][:]),
            e.dma_start(out=A['dst2'][j*128:(j+1)*128, :], in_=sb['ixt2'][:]),
            e.dma_start(out=A['pv'][j*128:(j+1)*128, :], in_=sb['pvt'][:]),
        ], 3)
        seq.phase_dma('gpsimd', lambda e: [
            e.indirect_dma_start(out=A['bucket0'], out_offset=bass.IndirectOffsetOnAxis(ap=sb['ixt1'][:], axis=0),
                                 in_=sb['lnt'][:], in_offset=None)], 1)
        seq.phase_dma('gpsimd', lambda e: [
            e.indirect_dma_start(out=A['bucket0'], out_offset=bass.IndirectOffsetOnAxis(ap=sb['ixt2'][:], axis=0),
                                 in_=sb['lnt'][:], in_offset=None)], 1)
    sec.end()


def emit_s9_moeffn(sec, A):
    seq = sec.begin(); sb = sec.sb; S, P = sec.S, sec.P
    S('w1', [128, 8, 2048], BF16)
    S('w2', [128, 16, 1024], BF16)
    S('xb', [128, 8, LCAP0], BF16)
    S('fc1', [128, 16, LCAP0], BF16)
    S('o2', [128, 1024])
    S('zz', [1, 1024])
    psF = P("psF", [128, 512])
    psO2 = P("psO2", [128, 1024])
    seq.phase('vector', lambda e: e.memset(sb['zz'][:], 0.0))
    seq.phase_dma('sync', lambda e: [e.dma_start(out=A['fc2buf'][ZSLOT0:ZSLOT0+1, :], in_=sb['zz'][:])], 1)
    for ex in range(8):
        seq.phase_dma('gpsimd', lambda e, ex=ex: [
            e.dma_start(out=sb['w1'][:], in_=A['w1'][ex].rearrange("(c p) f -> p c f", p=128)),
            e.dma_start(out=sb['w2'][:], in_=A['w2'][ex].rearrange("(c p) f -> p c f", p=128)),
        ], 2)
        seq.phase_dma('sync', lambda e, ex=ex: [
            e.dma_start(out=sb['xb'][:, fc, :], in_=A['bucket0'][ex*LCAP0:(ex+1)*LCAP0, fc*128:(fc+1)*128],
                        transpose=True) for fc in range(8)
        ], 8)
        for Fc in range(16):
            for sc in range(LCAP0 // 512):
                def mm(e, Fc=Fc, sc=sc):
                    last = None
                    for fc in range(8):
                        last = e.matmul(psF[:], sb['w1'][:, fc, Fc*128:(Fc+1)*128],
                                        sb['xb'][:, fc, sc*512:(sc+1)*512],
                                        start=(fc == 0), stop=(fc == 7))
                    return last
                seq.phase('tensor', mm)
                seq.phase('scalar', lambda e, Fc=Fc, sc=sc: e.activation(
                    out=sb['fc1'][:, Fc, sc*512:(sc+1)*512], in_=psF[:], func=AF.Gelu_apprx_tanh))
        for sc2 in range(LCAP0 // 128):
            def mm2(e, sc2=sc2):
                last = None
                for half in range(2):
                    for Fc in range(16):
                        last = e.matmul(psO2[:, half*512:(half+1)*512],
                                        sb['fc1'][:, Fc, sc2*128:(sc2+1)*128],
                                        sb['w2'][:, Fc, half*512:(half+1)*512],
                                        start=(Fc == 0), stop=(Fc == 15))
                return last
            seq.phase('tensor', mm2)
            seq.phase('vector', lambda e: e.tensor_copy(out=sb['o2'][:], in_=psO2[:]))
            seq.phase_dma('sync', lambda e, ex=ex, sc2=sc2: [
                e.dma_start(out=A['fc2buf'][ex*LCAP0+sc2*128: ex*LCAP0+(sc2+1)*128, :], in_=sb['o2'][:]),
            ], 1)
    sec.end()


def emit_s10_ret0(sec, A):
    seq = sec.begin(); sb = sec.sb; S, P = sec.S, sec.P
    S('ix1', [128, 1], I32); S('ix2', [128, 1], I32); S('pv', [128, 2])
    S('g1', [128, 1024]); S('g2', [128, 1024]); S('g2s', [128, 1024])
    S('x1t', [128, 1024]); S('h1t', [128, 1024])
    S('acc', [128, 1024]); S('acc2', [128, 1024]); S('moe', [128, 1024])
    x1_r = A['x1r'].rearrange("(p s) f -> s p f", s=16)
    h1_rr = A['h1'].rearrange("(p s) f -> s p f", s=16)
    for j in range(32):
        s_, pb_ = j // 2, j % 2
        seq.phase_dma('sync', lambda e, j=j, s_=s_, pb_=pb_: [
            e.dma_start(out=sb['ix1'][:], in_=A['dst1'][j*128:(j+1)*128, :]),
            e.dma_start(out=sb['ix2'][:], in_=A['dst2'][j*128:(j+1)*128, :]),
            e.dma_start(out=sb['pv'][:], in_=A['pv'][j*128:(j+1)*128, :]),
            e.dma_start(out=sb['x1t'][:], in_=x1_r[s_, pb_*128:(pb_+1)*128, :]),
            e.dma_start(out=sb['h1t'][:], in_=h1_rr[s_, pb_*128:(pb_+1)*128, :]),
        ], 5)
        seq.phase_dma('gpsimd', lambda e: [
            e.indirect_dma_start(out=sb['g1'][:], out_offset=None, in_=A['fc2buf'],
                                 in_offset=bass.IndirectOffsetOnAxis(ap=sb['ix1'][:], axis=0))], 1)
        seq.phase_dma('gpsimd', lambda e: [
            e.indirect_dma_start(out=sb['g2'][:], out_offset=None, in_=A['fc2buf'],
                                 in_offset=bass.IndirectOffsetOnAxis(ap=sb['ix2'][:], axis=0))], 1)
        seq.phase('vector', lambda e: e.tensor_scalar(out=sb['g2s'][:], in0=sb['g2'][:], scalar1=sb['pv'][:, 1:2], scalar2=None, op0=ALU.mult))
        seq.phase('vector', lambda e: e.scalar_tensor_tensor(out=sb['moe'][:], in0=sb['g1'][:], scalar=sb['pv'][:, 0:1], in1=sb['g2s'][:], op0=ALU.mult, op1=ALU.add))
        seq.phase('vector', lambda e: e.tensor_tensor(out=sb['acc'][:], in0=sb['x1t'][:], in1=sb['h1t'][:], op=ALU.add))
        seq.phase('vector', lambda e: e.tensor_tensor(out=sb['acc2'][:], in0=sb['acc'][:], in1=sb['moe'][:], op=ALU.add))
        seq.phase_dma('sync', lambda e, j=j: [
            e.dma_start(out=A['x2loc'][j*128:(j+1)*128, :], in_=sb['acc2'][:]),
        ], 1)
    sec.end()


def emit_s12_qkv2(sec, A):
    seq = sec.begin(); sb = sec.sb; S, P = sec.S, sec.P
    S('wq', [128, 8, 1024], BF16)
    S('wkv', [128, 8, 2048], BF16)
    S('idf', [128, 128]); S('idb', [128, 128], BF16); S('scal', [128, 4])
    S('xt', [128, 4, 1024]); S('ut', [128, 4, 1024])
    S('lnstat', [128, 32]); S('lnscr', [128, 4, 1024])
    S('ufm', [128, 8, 512], BF16)
    S('outb', [128, 8, 512], BF16)
    S('vfm', [128, 8, 512], BF16)
    S('vtmS', [128, 4, 1024], BF16)
    psA = P("psA", [128, 512]); psT = P("psT", [128, 512]); psC = P("psC", [128, 1024], BF16)
    seq.phase_dma('gpsimd', lambda e: [
        e.dma_start(out=sb['wq'][:], in_=A['Wq'].rearrange("(c p) o -> p c o", p=128)),
        e.dma_start(out=sb['wkv'][:], in_=A['Wkv'].rearrange("(c p) o -> p c o", p=128)),
    ], 2)
    seq.phase_dma('sync', lambda e: [
        e.dma_start(out=sb['idf'][:], in_=A['IDF']),
        e.dma_start(out=sb['idb'][:], in_=A['IDB']),
        e.dma_start(out=sb['scal'][:], in_=A['SCAL']),
    ], 3)
    for tc in range(8):
        seq.phase_dma('sync', lambda e, tc=tc: [
            e.dma_start(out=sb['xt'][:, i, :], in_=A['x2b'][tc*512+i*128: tc*512+(i+1)*128, :]) for i in range(4)
        ], 4)
        emit_ln(sec, [sb['xt'][:, i, :] for i in range(4)], [sb['ut'][:, i, :] for i in range(4)], 1024)
        for fc in range(8):
            def tr(e, fc=fc):
                last = None
                for i in range(4):
                    last = e.transpose(psT[:, i*128:(i+1)*128], sb['ut'][:, i, fc*128:(fc+1)*128], sb['idf'][:])
                return last
            seq.phase('tensor', tr)
            seq.phase('vector', lambda e, fc=fc: e.tensor_copy(out=sb['ufm'][:, fc, :], in_=psT[:]))
        for oc in range(8):
            def mm(e, oc=oc):
                last = None
                for fc in range(8):
                    last = e.matmul(psA[:], sb['wq'][:, fc, oc*128:(oc+1)*128], sb['ufm'][:, fc, :],
                                    start=(fc == 0), stop=(fc == 7))
                return last
            seq.phase('tensor', mm)
            seq.phase('vector', lambda e, oc=oc: e.tensor_scalar_mul(sb['outb'][:, oc, :], psA[:], 0.125))
        seq.phase_dma('sync', lambda e, tc=tc: [
            e.dma_start(out=A['q1f'][oc*128:(oc+1)*128, tc*512:(tc+1)*512], in_=sb['outb'][:, oc, :]) for oc in range(8)
        ], 8)
        for oc in range(8):
            def mm(e, oc=oc):
                last = None
                for fc in range(8):
                    last = e.matmul(psA[:], sb['wkv'][:, fc, oc*128:(oc+1)*128], sb['ufm'][:, fc, :],
                                    start=(fc == 0), stop=(fc == 7))
                return last
            seq.phase('tensor', mm)
            seq.phase('vector', lambda e, oc=oc: e.tensor_copy(out=sb['outb'][:, oc, :], in_=psA[:]))
        seq.phase_dma('sync', lambda e, tc=tc: [
            e.dma_start(out=A['k1f'][oc*128:(oc+1)*128, tc*512:(tc+1)*512], in_=sb['outb'][:, oc, :]) for oc in range(8)
        ], 8)
        for oc in range(8):
            def mm(e, oc=oc):
                last = None
                for fc in range(8):
                    last = e.matmul(psA[:], sb['wkv'][:, fc, 1024+oc*128:1024+(oc+1)*128], sb['ufm'][:, fc, :],
                                    start=(fc == 0), stop=(fc == 7))
                return last
            seq.phase('tensor', mm)
            seq.phase('vector', lambda e, oc=oc: e.tensor_copy(out=sb['vfm'][:, oc, :], in_=psA[:]))
        for i in range(4):
            def trv(e, i=i):
                last = None
                for oc in range(8):
                    last = e.transpose(psC[:, oc*128:(oc+1)*128], sb['vfm'][:, oc, i*128:(i+1)*128], sb['idb'][:])
                return last
            seq.phase('tensor', trv)
            seq.phase('vector', lambda e, i=i: e.tensor_copy(out=sb['vtmS'][:, i, :], in_=psC[:]))
        seq.phase_dma('sync', lambda e, tc=tc: [
            e.dma_start(out=A['v1tm'][tc*512+i*128: tc*512+(i+1)*128, :], in_=sb['vtmS'][:, i, :]) for i in range(4)
        ], 4)
    sec.end()


def emit_s13_att1(sec, A):
    seq = sec.begin(); sb = sec.sb; S, P = sec.S, sec.P
    S('vtm', [128, 32, 1024], BF16)
    S('wo', [128, 8, 1024], BF16)
    S('dmask', [128, 128]); S('idb', [128, 128], BF16)
    S('q', [64, 2048], BF16); S('kg', [64, 2048], BF16)
    S('sS', [128, 2048]); S('sP', [128, 2048], BF16); S('sPT', [128, 2048], BF16)
    S('nm', [128, 1]); S('sums', [128, 1]); S('rs', [128, 1])
    S('aoS', [128, 64], BF16)
    S('aofm', [128, 8, T], BF16)
    S('x2t', [128, 1024]); S('x3t', [128, 1024])
    psS = P("psS", [128, 2048]); psPT = P("psPT", [128, 1024], BF16)
    psO = P("psO", [128, 64]); psPRJ = P("psPRJ", [128, 1024])
    seq.phase_dma('sync', lambda e: [
        e.dma_start(out=sb['vtm'][:], in_=A['v1tm'].rearrange("(n p) f -> p n f", p=128)),
        e.dma_start(out=sb['dmask'][:], in_=A['DMASK']),
        e.dma_start(out=sb['idb'][:], in_=A['IDB']),
    ], 3)
    seq.phase_dma('gpsimd', lambda e: [
        e.dma_start(out=sb['wo'][:], in_=A['Wo'].rearrange("(c p) o -> p c o", p=128)),
    ], 1)
    for b in range(2):
        for h in range(16):
            seq.phase_dma('sync', lambda e, h=h, b=b: [
                e.dma_start(out=sb['q'][:], in_=A['q1f'][h*64:(h+1)*64, b*2048:(b+1)*2048]),
                e.dma_start(out=sb['kg'][:], in_=A['k1f'][h*64:(h+1)*64, b*2048:(b+1)*2048]),
            ], 2)
            for qt in range(16):
                nk = (qt + 1) * 128
                def qk(e, qt=qt):
                    last = None
                    for kc in range(qt + 1):
                        last = e.matmul(psS[:, kc*128:(kc+1)*128],
                                        sb['q'][:, qt*128:(qt+1)*128],
                                        sb['kg'][:, kc*128:(kc+1)*128],
                                        start=True, stop=True)
                    return last
                seq.phase('tensor', qk)
                def evsc(e, qt=qt, nk=nk):
                    if qt > 0:
                        e.tensor_copy(out=sb['sS'][:, 0:qt*128], in_=psS[:, 0:qt*128])
                    return e.scalar_tensor_tensor(out=sb['sS'][:, qt*128:nk], in0=psS[:, qt*128:nk],
                                                  scalar=1.0, in1=sb['dmask'][:], op0=ALU.mult, op1=ALU.add)
                seq.phase('vector', evsc)
                seq.phase('vector', lambda e, nk=nk: e.tensor_reduce(out=sb['nm'][:], in_=sb['sS'][:, 0:nk], axis=AX.X, op=ALU.max, negate=True))
                seq.phase('scalar', lambda e, nk=nk: e.activation(out=sb['sP'][:, 0:nk], in_=sb['sS'][:, 0:nk], func=AF.Exp,
                                                                  bias=sb['nm'][:, 0:1], accum_out=sb['sums'][:, 0:1]))
                for half in range(2):
                    kcs = list(range(half*8, min(qt+1, half*8+8)))
                    if len(kcs) == 0:
                        continue
                    def trp(e, kcs=kcs, half=half):
                        last = None
                        for kc in kcs:
                            last = e.transpose(psPT[:, (kc-half*8)*128:(kc-half*8+1)*128],
                                               sb['sP'][:, kc*128:(kc+1)*128], sb['idb'][:])
                        return last
                    seq.phase('tensor', trp)
                    seq.phase('vector', lambda e, kcs=kcs, half=half: e.tensor_copy(
                        out=sb['sPT'][:, kcs[0]*128: (kcs[-1]+1)*128],
                        in_=psPT[:, (kcs[0]-half*8)*128:(kcs[-1]-half*8+1)*128]))
                def pv(e, b=b, h=h, qt=qt):
                    last = None
                    for kc in range(qt + 1):
                        ktile = b*16 + kc
                        last = e.matmul(psO[:], sb['sPT'][:, kc*128:(kc+1)*128],
                                        sb['vtm'][:, ktile, h*64:(h+1)*64],
                                        start=(kc == 0), stop=(kc == qt))
                    return last
                seq.phase('tensor', pv)
                seq.phase('vector', lambda e: e.reciprocal(sb['rs'][:], sb['sums'][:]))
                seq.phase('vector', lambda e: e.tensor_scalar(out=sb['aoS'][:], in0=psO[:], scalar1=sb['rs'][:, 0:1], scalar2=None, op0=ALU.mult))
                seq.phase('tensor', lambda e: e.transpose(psPT[0:64, 0:128], sb['aoS'][:], sb['idb'][:]))
                seq.phase('vector', lambda e, b=b, h=h, qt=qt: e.tensor_copy(
                    out=sb['aofm'][(h % 2)*64:(h % 2)*64+64, h//2, b*2048+qt*128: b*2048+(qt+1)*128],
                    in_=psPT[0:64, 0:128]))
    for tt in range(32):
        seq.phase_dma('sync', lambda e, tt=tt: [e.dma_start(out=sb['x2t'][:], in_=A['x2b'][tt*128:(tt+1)*128, :])], 1)
        def prj(e, tt=tt):
            last = None
            for half in range(2):
                for fc in range(8):
                    last = e.matmul(psPRJ[:, half*512:(half+1)*512],
                                    sb['aofm'][:, fc, tt*128:(tt+1)*128],
                                    sb['wo'][:, fc, half*512:(half+1)*512],
                                    start=(fc == 0), stop=(fc == 7))
            return last
        seq.phase('tensor', prj)
        seq.phase('vector', lambda e: e.tensor_tensor(out=sb['x3t'][:], in0=sb['x2t'][:], in1=psPRJ[:], op=ALU.add))
        seq.phase_dma('sync', lambda e, tt=tt: [e.dma_start(out=A['x3'][tt*128:(tt+1)*128, :], in_=sb['x3t'][:])], 1)
    sec.end()


def emit_s14_ffn1a(sec, A):
    seq = sec.begin(); sb = sec.sb; S, P = sec.S, sec.P
    S('centT', [128, 8, 8], BF16)
    S('tri', [128, 128]); S('iota8', [128, 8]); S('onesf', [128, 1]); S('onesr', [1, 128])
    S('idf', [128, 128]); S('scal', [128, 4])
    S('xt', [128, 1, 1024]); S('ut', [128, 1, 1024]); S('nt', [128, 1, 1024])
    S('ntb', [128, 1024], BF16)
    S('lnstat', [128, 32]); S('lnscr', [128, 1, 1024])
    S('ufm', [128, 8, 128], BF16)
    S('af', [128, 8]); S('mx', [128, 8]); S('oh', [128, 8]); S('ec', [128, 8])
    S('slotsS', [128, 8]); S('tmpC', [128, 8])
    S('wt', [128, 16])
    S('ix', [128, 1], I32)
    S('rtA', [1, 8]); S('rtB', [1, 8]); S('rtS', [1, 8])
    psT = P("psT", [128, 512]); psAF = P("psAF", [128, 8])
    psSL = P("psSL", [128, 8]); psR2 = P("psR2", [1, 8])
    seq.phase_dma('gpsimd', lambda e: [
        e.dma_start(out=sb['centT'][:], in_=A['centT'].rearrange("(c p) o -> p c o", p=128)),
    ], 1)
    seq.phase_dma('sync', lambda e: [
        e.dma_start(out=sb['tri'][:], in_=A['TRI128']),
        e.dma_start(out=sb['iota8'][:], in_=A['IOTA8']),
        e.dma_start(out=sb['onesf'][:], in_=A['ONESF']),
        e.dma_start(out=sb['onesr'][:], in_=A['ONESR']),
        e.dma_start(out=sb['idf'][:], in_=A['IDF']),
        e.dma_start(out=sb['scal'][:], in_=A['SCAL']),
    ], 6)
    seq.phase('vector', lambda e: e.memset(sb['rtA'][:], 0.0))
    for j in range(32):
        cur_rt = 'rtA' if j % 2 == 0 else 'rtB'
        nxt_rt = 'rtB' if j % 2 == 0 else 'rtA'
        seq.phase_dma('sync', lambda e, j=j: [
            e.dma_start(out=sb['xt'][:, 0, :], in_=A['x3'][j*128:(j+1)*128, :]),
        ], 1)
        emit_ln(sec, [sb['xt'][:, 0, :]], [sb['ut'][:, 0, :]], 1024)
        seq.phase_dma('sync', lambda e, j=j: [
            e.dma_start(out=A['u3'][j*128:(j+1)*128, :], in_=sb['ut'][:, 0, :]),
        ], 1)
        emit_ln(sec, [sb['ut'][:, 0, :]], [sb['nt'][:, 0, :]], 1024)
        seq.phase('vector', lambda e: e.tensor_copy(out=sb['ntb'][:], in_=sb['nt'][:, 0, :]))
        for fc in range(8):
            seq.phase('tensor', lambda e, fc=fc: e.transpose(psT[:, 0:128], sb['ut'][:, 0, fc*128:(fc+1)*128], sb['idf'][:]))
            seq.phase('vector', lambda e, fc=fc: e.tensor_copy(out=sb['ufm'][:, fc, :], in_=psT[:, 0:128]))
        def afm(e):
            last = None
            for fc in range(8):
                last = e.matmul(psAF[:], sb['ufm'][:, fc, :], sb['centT'][:, fc, :],
                                start=(fc == 0), stop=(fc == 7))
            return last
        seq.phase('tensor', afm)
        seq.phase('vector', lambda e: e.tensor_copy(out=sb['af'][:], in_=psAF[:]))
        seq.phase('vector', lambda e: e.max(out=sb['mx'][:], in_=sb['af'][:]))
        seq.phase('scalar', lambda e: e.activation(out=sb['wt'][:, 0:1], in_=sb['mx'][:, 0:1], func=AF.Sigmoid))
        seq.phase('vector', lambda e: e.tensor_scalar(out=sb['oh'][:], in0=sb['af'][:], scalar1=sb['mx'][:, 0:1], scalar2=None, op0=ALU.is_equal))
        seq.phase('vector', lambda e: e.tensor_tensor(out=sb['ec'][:], in0=sb['oh'][:], in1=sb['iota8'][:], op=ALU.mult))
        seq.phase('vector', lambda e: e.tensor_reduce(out=sb['wt'][:, 1:2], in_=sb['ec'][:], axis=AX.X, op=ALU.add))
        def g2(e, cur_rt=cur_rt):
            e.matmul(psSL[:], sb['onesr'][:], sb[cur_rt][:], start=True, stop=False)
            return e.matmul(psSL[:], sb['tri'][:], sb['oh'][:], start=False, stop=True)
        seq.phase('tensor', g2)
        seq.phase('tensor', lambda e: e.matmul(psR2[:], sb['onesf'][:], sb['oh'][:], start=True, stop=True))
        def v7(e):
            e.tensor_copy(out=sb['slotsS'][:], in_=psSL[:])
            return e.tensor_copy(out=sb['rtS'][:], in_=psR2[:])
        seq.phase('vector', v7)
        seq.phase('vector', lambda e, cur_rt=cur_rt, nxt_rt=nxt_rt: e.tensor_tensor(out=sb[nxt_rt][:], in0=sb[cur_rt][:], in1=sb['rtS'][:], op=ALU.add))
        seq.phase('vector', lambda e: e.tensor_tensor(out=sb['tmpC'][:], in0=sb['slotsS'][:], in1=sb['oh'][:], op=ALU.mult))
        seq.phase('vector', lambda e: e.tensor_reduce(out=sb['wt'][:, 2:3], in_=sb['tmpC'][:], axis=AX.X, op=ALU.add))
        seq.phase('vector', lambda e: e.tensor_scalar(out=sb['wt'][:, 3:4], in0=sb['wt'][:, 2:3], scalar1=float(LCAP1), scalar2=None, op0=ALU.is_lt))
        seq.phase('vector', lambda e: e.tensor_scalar(out=sb['wt'][:, 4:5], in0=sb['wt'][:, 1:2], scalar1=float(LCAP1), scalar2=sb['wt'][:, 2:3], op0=ALU.mult, op1=ALU.add))
        seq.phase('vector', lambda e: e.tensor_tensor(out=sb['wt'][:, 5:6], in0=sb['wt'][:, 4:5], in1=sb['wt'][:, 3:4], op=ALU.mult))
        seq.phase('vector', lambda e: e.tensor_scalar(out=sb['wt'][:, 6:7], in0=sb['wt'][:, 3:4], scalar1=-float(ZSLOT1), scalar2=float(ZSLOT1), op0=ALU.mult, op1=ALU.add))
        seq.phase('vector', lambda e: e.tensor_tensor(out=sb['wt'][:, 7:8], in0=sb['wt'][:, 5:6], in1=sb['wt'][:, 6:7], op=ALU.add))
        seq.phase('vector', lambda e: e.tensor_copy(out=sb['ix'][:], in_=sb['wt'][:, 7:8]))
        seq.phase_dma('sync', lambda e, j=j: [
            e.dma_start(out=A['alpha'][j*128:(j+1)*128, :], in_=sb['wt'][:, 0:1]),
            e.dma_start(out=A['dsti1'][j*128:(j+1)*128, :], in_=sb['ix'][:]),
        ], 2)
        seq.phase_dma('gpsimd', lambda e: [
            e.indirect_dma_start(out=A['bucket1'], out_offset=bass.IndirectOffsetOnAxis(ap=sb['ix'][:, 0:1], axis=0),
                                 in_=sb['ntb'][:], in_offset=None)], 1)
    sec.end()


def emit_s15_ffn1b(sec, A):
    seq = sec.begin(); sb = sec.sb; S, P = sec.S, sec.P
    S('w1', [128, 8, 4096], BF16)
    S('w2', [128, 32, 1024], BF16)
    S('lng', [128, 8]); S('lnb', [128, 8]); S('b1c', [128, 32]); S('b2r', [128, 1024])
    S('xbe', [128, 8, LCAP1], BF16)
    S('xg', [128, 8, 256], BF16)
    S('fc1', [128, 32, 256], BF16)
    S('o2', [128, 1024])
    S('zz', [1, 1024])
    psF = P("psF", [128, 256]); psO2 = P("psO2", [128, 1024])
    seq.phase('vector', lambda e: e.memset(sb['zz'][:], 0.0))
    seq.phase_dma('sync', lambda e: [e.dma_start(out=A['fc2buf1'][ZSLOT1:ZSLOT1+1, :], in_=sb['zz'][:])], 1)
    for ex in range(8):
        seq.phase_dma('gpsimd', lambda e, ex=ex: [
            e.dma_start(out=sb['w1'][:], in_=A['w1T'][ex].rearrange("(c p) f -> p c f", p=128)),
            e.dma_start(out=sb['w2'][:], in_=A['w2T'][ex].rearrange("(c p) f -> p c f", p=128)),
        ], 2)
        seq.phase_dma('sync', lambda e, ex=ex: [
            e.dma_start(out=sb['lng'][:], in_=A['LNG'][ex]),
            e.dma_start(out=sb['lnb'][:], in_=A['LNB'][ex]),
            e.dma_start(out=sb['b1c'][:], in_=A['B1C'][ex]),
            e.dma_start(out=sb['b2r'][:], in_=A['B2REP1'][ex]),
        ], 4)
        seq.phase_dma('sync', lambda e, ex=ex: [
            e.dma_start(out=sb['xbe'][:, fc, :], in_=A['bucket1'][ex*LCAP1:(ex+1)*LCAP1, fc*128:(fc+1)*128],
                        transpose=True) for fc in range(8)
        ], 8)
        for sc in range(LCAP1 // 256):
            def gb(e, sc=sc):
                last = None
                for fc in range(8):
                    last = e.tensor_scalar(out=sb['xg'][:, fc, :], in0=sb['xbe'][:, fc, sc*256:(sc+1)*256],
                                           scalar1=sb['lng'][:, fc:fc+1], scalar2=sb['lnb'][:, fc:fc+1],
                                           op0=ALU.mult, op1=ALU.add)
                return last
            seq.phase('vector', gb)
            for Fc in range(32):
                def mm(e, Fc=Fc):
                    last = None
                    for fc in range(8):
                        last = e.matmul(psF[:], sb['w1'][:, fc, Fc*128:(Fc+1)*128],
                                        sb['xg'][:, fc, :],
                                        start=(fc == 0), stop=(fc == 7))
                    return last
                seq.phase('tensor', mm)
                seq.phase('scalar', lambda e, Fc=Fc: e.activation(
                    out=sb['fc1'][:, Fc, :], in_=psF[:], func=AF.Relu, bias=sb['b1c'][:, Fc:Fc+1]))
            for sc2 in range(2):
                def mm2(e, sc2=sc2):
                    last = None
                    for half in range(2):
                        for Fc in range(32):
                            last = e.matmul(psO2[:, half*512:(half+1)*512],
                                            sb['fc1'][:, Fc, sc2*128:(sc2+1)*128],
                                            sb['w2'][:, Fc, half*512:(half+1)*512],
                                            start=(Fc == 0), stop=(Fc == 31))
                    return last
                seq.phase('tensor', mm2)
                seq.phase('vector', lambda e: e.tensor_tensor(out=sb['o2'][:], in0=psO2[:], in1=sb['b2r'][:], op=ALU.add))
                seq.phase_dma('sync', lambda e, ex=ex, sc=sc, sc2=sc2: [
                    e.dma_start(out=A['fc2buf1'][ex*LCAP1+sc*256+sc2*128: ex*LCAP1+sc*256+(sc2+1)*128, :], in_=sb['o2'][:]),
                ], 1)
    sec.end()


def emit_s16_ret1(sec, A):
    seq = sec.begin(); sb = sec.sb; S, P = sec.S, sec.P
    S('ix', [128, 1], I32); S('al', [128, 1])
    S('g1', [128, 1024]); S('gs', [128, 1024])
    S('x3t', [128, 1024]); S('u3t', [128, 1024])
    S('a1', [128, 1024]); S('a2', [128, 1024])
    for j in range(32):
        seq.phase_dma('sync', lambda e, j=j: [
            e.dma_start(out=sb['ix'][:], in_=A['dsti1'][j*128:(j+1)*128, :]),
            e.dma_start(out=sb['al'][:], in_=A['alpha'][j*128:(j+1)*128, :]),
            e.dma_start(out=sb['x3t'][:], in_=A['x3'][j*128:(j+1)*128, :]),
            e.dma_start(out=sb['u3t'][:], in_=A['u3'][j*128:(j+1)*128, :]),
        ], 4)
        seq.phase_dma('gpsimd', lambda e: [
            e.indirect_dma_start(out=sb['g1'][:], out_offset=None, in_=A['fc2buf1'],
                                 in_offset=bass.IndirectOffsetOnAxis(ap=sb['ix'][:, 0:1], axis=0))], 1)
        seq.phase('vector', lambda e: e.tensor_scalar(out=sb['gs'][:], in0=sb['g1'][:], scalar1=sb['al'][:, 0:1], scalar2=None, op0=ALU.mult))
        seq.phase('vector', lambda e: e.tensor_tensor(out=sb['a1'][:], in0=sb['x3t'][:], in1=sb['u3t'][:], op=ALU.add))
        seq.phase('vector', lambda e: e.tensor_tensor(out=sb['a2'][:], in0=sb['a1'][:], in1=sb['gs'][:], op=ALU.add))
        seq.phase_dma('sync', lambda e, j=j: [
            e.dma_start(out=A['out'][j*128:(j+1)*128, :], in_=sb['a2'][:]),
        ], 1)
    sec.end()



def build_fused(upto=99, dbg=(), lo=0, ext_in=()):
    nc = bass.Bass("TRN2", target_bir_lowering=False, debug=False)
    Section.counter[0] = 0
    A = {}
    ins_spec = [
        ("x0", [T, D], F32), ("wqT", [D, D], F32), ("wkT", [D, 256], F32), ("wvT", [D, 256], F32),
        ("qnw", [128, 8], F32), ("knw", [128, 2], F32), ("woT", [D, D], F32),
        ("qkvT", [D, 1536], F32), ("W1REP", [128, D], F32), ("B1REP", [128, D], F32),
        ("prjT", [D, D], F32), ("W2REP", [128, D], F32), ("B2REP", [128, D], F32),
        ("rw", [D, 8], F32), ("w1", [8, D, 2048], F32), ("w2", [8, 2048, D], F32),
        ("Wq", [D, D], F32), ("Wkv", [D, 2*D], F32), ("Wo", [D, D], F32),
        ("centT", [D, 8], F32), ("w1T", [8, D, 4096], F32), ("w2T", [8, 4096, D], F32),
        ("LNG", [8, 128, 8], F32), ("LNB", [8, 128, 8], F32), ("B1C", [8, 128, 32], F32),
        ("B2REP1", [8, 128, D], F32),
        ("IDB", [128, 128], BF16), ("IDF", [128, 128], F32), ("IDB64", [128, 64], BF16),
        ("ONESF", [128, 1], F32), ("ONESR", [1, 128], F32),
        ("RELB", [16, 128, 640], BF16), ("DMASK", [128, 128], F32),
        ("IOTA8", [128, 8], F32), ("TRI128", [128, 128], F32), ("TRI16", [16, 16], F32),
        ("ALLONES", [128, 128], F32), ("SROWSEL", [128, 16], F32), ("CSEL", [128, 16], F32),
        ("SELBANK", [16, 16*128], F32), ("SCAL", [128, 4], F32),
        ("CORETRI", [128, 16], F32), ("CID", [128, 1], F32),
        ("TMASKM", [128, 512], BF16), ("IND16", [128, 8], BF16), ("IND16T", [8, 128], F32),
        ("RESHC", [128, 32], F32), ("RESHC2", [128, 32], F32),
    ]
    for nm, shp, dt in ins_spec:
        A[nm] = nc.dram_tensor(nm, shp, dt, kind="ExternalInput").ap()
    mid_spec = [
        ("qf", [D, T], BF16), ("kf", [256, T], BF16), ("vtm", [T, 256], BF16),
        ("x1loc", [T, D], F32), ("x1g", [8*T, D], F32), ("x1r", [T, D], F32),
        ("u1", [T, D], F32), ("qtf", [D, T], BF16), ("ktf", [256, T], BF16), ("vtf", [256, T], BF16),
        ("aofm_t", [D, T], BF16),
        ("h1", [T, D], F32), ("ln2tm", [T, D], BF16), ("ind", [T, 8], F32), ("route", [T, 4], F32),
        ("Rloc", [16, 8], F32), ("Rg", [128, 8], F32),
        ("dst1", [T, 1], I32), ("dst2", [T, 1], I32), ("pv", [T, 2], F32),
        ("bucket0", [8*LCAP0+1, D], BF16), ("fc2buf", [8*LCAP0+1, D], F32),
        ("x2loc", [T, D], F32), ("x2g", [8*T, D], F32), ("x2b", [T, D], F32),
        ("q1f", [D, T], BF16), ("k1f", [D, T], BF16), ("v1tm", [T, D], BF16),
        ("x3", [T, D], F32), ("u3", [T, D], F32), ("alpha", [T, 1], F32), ("dsti1", [T, 1], I32),
        ("bucket1", [8*LCAP1+1, D], BF16), ("fc2buf1", [8*LCAP1+1, D], F32),
    ]
    for nm, shp, dt in mid_spec:
        kind = "ExternalOutput" if nm in dbg else ("ExternalInput" if nm in ext_in else "Internal")
        A[nm] = nc.dram_tensor(nm, shp, dt, kind=kind).ap()
    A['out'] = nc.dram_tensor("out", [T, D], F32, kind="ExternalOutput").ap()

    gctx = ExitStack()
    semg = gctx.enter_context(nc.semaphore("Gg"))
    semh = gctx.enter_context(nc.semaphore("Gh"))
    semc = gctx.enter_context(nc.semaphore("Gc"))
    semcc = gctx.enter_context(nc.semaphore("Gcc"))
    Section.gseq[0] = Seq(nc, semg, semh, semc, semcc)
    nc._keepalive = gctx
    sections = [
        lambda sec: emit_s1_qkv0(sec, A),
        lambda sec: emit_s2_att0(sec, A),
        lambda sec: emit_reshard(sec, A, 'x1loc', 'x1g', 'x1r', 'RESHC', 256),
        lambda sec: emit_s4_ffn0a(sec, A),
        lambda sec: emit_s5_tiny(sec, A),
        lambda sec: emit_s6_ffn0b(sec, A),
        lambda sec: emit_s7_rgather(sec, A),
        lambda sec: emit_s8_disp(sec, A),
        lambda sec: emit_s9_moeffn(sec, A),
        lambda sec: emit_s10_ret0(sec, A),
        lambda sec: emit_reshard(sec, A, 'x2loc', 'x2g', 'x2b', 'RESHC2', 512),
        lambda sec: emit_s12_qkv2(sec, A),
        lambda sec: emit_s13_att1(sec, A),
        lambda sec: emit_s14_ffn1a(sec, A),
        lambda sec: emit_s15_ffn1b(sec, A),
        lambda sec: emit_s16_ret1(sec, A),
    ]
    nsec = min(upto, len(sections))
    for i, emit in enumerate(sections):
        if i >= upto or i < lo:
            continue
        sec = Section(nc)
        sec.final = (i == nsec - 1)
        emit(sec)
    return nc


def kernel(**inputs):
    nc = build_fused()
    in_maps = [host_inputs(inputs, c) for c in range(8)]
    res = run_bass_kernel_spmd(nc, in_maps, core_ids=list(range(8)))
    out = np.empty((16, 2048, D), np.float32)
    for c in range(8):
        out[2*c:2*c+2] = np.asarray(res.results[c]["out"]).reshape(2, 2048, D)
    return out


# version 03
# speedup vs baseline: 1.1497x; 1.1497x over previous
import sys
if '/opt/trn_rl_repo' not in sys.path:
    sys.path.insert(0, '/opt/trn_rl_repo')
import numpy as np
import ml_dtypes
import concourse.bass as bass
import concourse.mybir as mybir
from concourse.bass_utils import run_bass_kernel_spmd
from contextlib import ExitStack

F32 = mybir.dt.float32
BF16 = mybir.dt.bfloat16
I32 = mybir.dt.int32
AF = mybir.ActivationFunctionType
ALU = mybir.AluOpType
AX = mybir.AxisListType
BF = ml_dtypes.bfloat16

T = 4096
D = 1024
NEG = -1.0e9
LCAP0 = 1536
CAP0 = 10240
ZSLOT0 = 8 * LCAP0
LCAP1 = 1024
ZSLOT1 = 8 * LCAP1


class Seq:
    ENG2K = {'tensor': 't', 'vector': 'v', 'scalar': 's', 'gpsimd': 'p', 'sync': 'y'}

    def __init__(self, nc, sems):
        self.nc = nc
        self.sems = sems
        self.vals = {k: 0 for k in sems}
        self.streams = {}

    def _now(self):
        return dict(self.vals)

    def _fold(self, deps):
        if deps is None:
            return self._now()
        w = {}
        for d in deps:
            if d is None:
                continue
            k, v = d
            w[k] = max(w.get(k, 0), v)
        return w

    def _waits(self, e, w):
        for k, v in w.items():
            if v:
                e.wait_ge(self.sems[k], v)

    def phase(self, eng, fn, deps=None):
        w = self._fold(deps)
        k = Seq.ENG2K[eng]
        sem = self.sems[k]
        def emit(e, w=w, fn=fn, sem=sem):
            self._waits(e, w)
            last = fn(e)
            last.then_inc(sem, 1)
        self.streams.setdefault(eng, []).append(emit)
        self.vals[k] += 1
        return (k, self.vals[k])

    def phase_dma(self, eng, fn, n, deps=None):
        w = self._fold(deps)
        k = 'g' if eng == 'gpsimd' else 'h'
        w[k] = max(w.get(k, 0), self.vals[k])
        sem = self.sems[k]
        def emit(e, w=w, fn=fn, n=n, sem=sem):
            self._waits(e, w)
            insts = fn(e)
            assert len(insts) == n, f"dma phase expected {n} got {len(insts)}"
            for i in insts:
                i.then_inc(sem, 16)
        self.streams.setdefault(eng, []).append(emit)
        self.vals[k] += 16 * n
        return (k, self.vals[k])

    def phase_cc(self, fn, deps=None):
        w = self._fold(deps)
        sem = self.sems['cc']
        def emit(e, w=w, fn=fn, sem=sem):
            self._waits(e, w)
            fn(e).then_inc(sem, 1)
        self.streams.setdefault('gpsimd', []).append(emit)
        self.vals['cc'] += 1
        return ('cc', self.vals['cc'])

    def flush(self, block, final=False):
        fin = self._now()
        streams = self.streams
        for eng in ('sync', 'tensor', 'vector', 'scalar', 'gpsimd'):
            def body(e, eng=eng):
                for emit in streams.get(eng, []):
                    emit(e)
                if final and eng == 'sync':
                    self._waits(e, fin)
            getattr(block, eng)(body)
        self.streams = {}


class Section:
    counter = [0]
    gseq = [None]

    def __init__(self, nc):
        self.nc = nc
        self.ctx = ExitStack()
        self.sb = {}
        Section.counter[0] += 1
        self.uid = Section.counter[0]
        self.final = False

    def S(self, name, shape, dt=F32):
        t = self.ctx.enter_context(self.nc.sbuf_tensor(f"s{self.uid}_{name}", shape, dt))
        self.sb[name] = t
        return t

    def P(self, name, shape, dt=F32):
        t = self.ctx.enter_context(self.nc.psum_tensor(f"s{self.uid}_{name}", shape, dt))
        return t

    def begin(self):
        self.blk = self.ctx.enter_context(self.nc.Block())
        self.seq = Section.gseq[0]
        return self.seq

    def end(self):
        self.seq.flush(self.blk, final=self.final)
        self.ctx.close()


def emit_ln(sec, x_tiles, out_tiles, width, eps=1e-5):
    seq, sb = sec.seq, sec.sb
    nt = len(x_tiles)
    st = sb['lnstat']
    assert st.shape[1] >= 8*nt
    def vsum(e):
        last = None
        for i in range(nt):
            last = e.tensor_reduce(out=st[:, i:i+1], in_=x_tiles[i], axis=AX.X, op=ALU.add)
        return last
    seq.phase('vector', vsum)
    def ssq(e):
        last = None
        for i in range(nt):
            last = e.activation(out=sb['lnscr'][:, i, 0:width], in_=x_tiles[i], func=AF.Square,
                                accum_out=st[:, nt+i:nt+i+1])
        return last
    seq.phase('scalar', ssq)
    def v1(e):
        e.tensor_scalar_mul(st[:, 2*nt:3*nt], st[:, 0:nt], -1.0/width)
        return e.tensor_scalar_mul(st[:, 3*nt:4*nt], st[:, nt:2*nt], 1.0/width)
    seq.phase('vector', v1)
    seq.phase('vector', lambda e: e.tensor_tensor(out=st[:, 4*nt:5*nt], in0=st[:, 2*nt:3*nt], in1=st[:, 2*nt:3*nt], op=ALU.mult))
    seq.phase('vector', lambda e: e.tensor_tensor(out=st[:, 5*nt:6*nt], in0=st[:, 3*nt:4*nt], in1=st[:, 4*nt:5*nt], op=ALU.subtract))
    ec = 0 if eps == 1e-5 else 1
    seq.phase('scalar', lambda e: e.activation(out=st[:, 6*nt:7*nt], in_=st[:, 5*nt:6*nt], func=AF.Sqrt, bias=sb['scal'][:, ec:ec+1]))
    seq.phase('vector', lambda e: e.reciprocal(st[:, 7*nt:8*nt], st[:, 6*nt:7*nt]))
    def vapply(e):
        last = None
        for i in range(nt):
            last = e.tensor_scalar(out=out_tiles[i], in0=x_tiles[i],
                                   scalar1=st[:, 2*nt+i:2*nt+i+1], scalar2=st[:, 7*nt+i:7*nt+i+1],
                                   op0=ALU.add, op1=ALU.mult)
        return last
    seq.phase('vector', vapply)



def host_consts(core_id):
    C = {}
    C['IDB'] = np.eye(128, dtype=BF)
    C['IDF'] = np.eye(128, dtype=np.float32)
    C['ONESF'] = np.ones((128, 1), np.float32)
    C['ONESR'] = np.ones((1, 128), np.float32)
    NH, WINDOW = 16, 512
    slopes = np.array([(2.0**(-8.0/NH))**(h+1) for h in range(NH)], np.float32)
    i = np.arange(128)[:, None]; j = np.arange(640)[None, :]
    rel = (j - i - 512).astype(np.float32)
    valid = (rel <= 0) & (rel >= -WINDOW)
    C['RELB'] = np.where(valid[None], slopes[:, None, None]*rel[None], np.float32(NEG)).astype(BF)
    ii = np.arange(128)[:, None]; jj = np.arange(128)[None, :]
    C['DMASK'] = np.where(jj <= ii, 0.0, NEG).astype(np.float32)
    C['IOTA8'] = np.tile(np.arange(8, dtype=np.float32)[None, :], (128, 1))
    C['TRI128'] = (np.arange(128)[:, None] < np.arange(128)[None, :]).astype(np.float32)
    C['TRI16'] = (np.arange(16)[:, None] < np.arange(16)[None, :]).astype(np.float32)
    C['ALLONES'] = np.ones((128, 128), np.float32)
    C['SROWSEL'] = (np.arange(128)[:, None] % 16 == np.arange(16)[None, :]).astype(np.float32)
    C['CSEL'] = C['SROWSEL'].copy()
    sel = np.zeros((16, 16*128), np.float32)
    for s_ in range(16):
        sel[s_, s_*128:(s_+1)*128] = 1.0
    C['SELBANK'] = sel
    C['SCAL'] = np.tile(np.array([[1e-5, 1e-6, 0.0, 1.0]], np.float32), (128, 1))
    r = np.arange(128)
    C['CORETRI'] = (((r // 16)[:, None] < core_id) &
                    (r[:, None] % 16 == np.arange(16)[None, :])).astype(np.float32)
    C['CID'] = np.full((128, 1), float(core_id), np.float32)
    C['IDB64'] = np.vstack([np.eye(64, dtype=BF), np.eye(64, dtype=BF)])
    p = np.arange(128); t = p % 16
    f = np.arange(512); s = f % 16
    C['TMASKM'] = (t[:, None] <= s[None, :]).astype(BF)
    C['IND16'] = np.zeros((128, 8), BF)
    for g in range(4):
        for pp in range(2):
            C['IND16'][g*32+pp*16:(g*32+pp*16+16), g*2+pp] = 1
    C['IND16T'] = np.ascontiguousarray(C['IND16'].astype(np.float32).T)
    RESH = np.zeros((128, 32), np.float32)
    for jj_ in range(32):
        for pp in range(128):
            r_ = jj_*128 + pp
            pos, s_ = r_ // 16, r_ % 16
            RESH[pp, jj_] = (s_//2)*4096 + (s_ % 2)*2048 + pos
    C['RESHC'] = RESH
    RESH2 = np.zeros((128, 32), np.float32)
    for jj_ in range(32):
        for pp in range(128):
            r_ = jj_*128 + pp
            b_l, pos = r_ // 2048, r_ % 2048
            RESH2[pp, jj_] = (pos//256)*4096 + 256*b_l + (pos % 256)
    C['RESHC2'] = RESH2
    return C


def _asc(x):
    return np.ascontiguousarray(x)


def host_inputs(I, core_id):
    C = host_consts(core_id)
    x0 = np.asarray(I['L0_attn_x'])
    m = {
        "x0": _asc(x0[2*core_id:2*core_id+2].reshape(T, D)),
        "wqT": _asc(I['L0_attn_wq'].T), "wkT": _asc(I['L0_attn_wk'].T), "wvT": _asc(I['L0_attn_wv'].T),
        "qnw": _asc(np.asarray(I['L0_attn_q_norm_w']).reshape(8, 128).T),
        "knw": _asc(np.asarray(I['L0_attn_k_norm_w']).reshape(2, 128).T),
        "woT": _asc(I['L0_attn_wo'].T),
        "qkvT": _asc(I['L0_ffn_qkv_weight'].T),
        "W1REP": _asc(np.broadcast_to(I['L0_ffn_ln1_weight'], (128, D))),
        "B1REP": _asc(np.broadcast_to(I['L0_ffn_ln1_bias'], (128, D))),
        "prjT": _asc(I['L0_ffn_proj_weight'].T),
        "W2REP": _asc(np.broadcast_to(I['L0_ffn_ln2_weight'], (128, D))),
        "B2REP": _asc(np.broadcast_to(I['L0_ffn_ln2_bias'], (128, D))),
        "rw": _asc(I['L0_ffn_router_weight']),
        "w1": np.asarray(I['L0_ffn_moe_w1']), "w2": np.asarray(I['L0_ffn_moe_w2']),
        "Wq": _asc(I['L1_attn_Wq']), "Wkv": _asc(I['L1_attn_Wkv']), "Wo": _asc(I['L1_attn_Wo']),
        "centT": _asc(np.asarray(I['L1_ffn_centroids']).T),
        "w1T": _asc(np.asarray(I['L1_ffn_w1']).transpose(0, 2, 1)),
        "w2T": _asc(np.asarray(I['L1_ffn_w2']).transpose(0, 2, 1)),
        "LNG": _asc(np.asarray(I['L1_ffn_ln_g']).reshape(8, 8, 128).transpose(0, 2, 1)),
        "LNB": _asc(np.asarray(I['L1_ffn_ln_b']).reshape(8, 8, 128).transpose(0, 2, 1)),
        "B1C": _asc(np.asarray(I['L1_ffn_b1']).reshape(8, 32, 128).transpose(0, 2, 1)),
        "B2REP1": _asc(np.broadcast_to(np.asarray(I['L1_ffn_b2'])[:, None, :], (8, 128, D))),
    }
    for k in ('IDB', 'IDF', 'ONESF', 'ONESR', 'RELB', 'DMASK', 'IOTA8', 'TRI128', 'TRI16',
              'ALLONES', 'SROWSEL', 'CSEL', 'SELBANK', 'SCAL', 'CORETRI', 'CID',
              'TMASKM', 'IND16', 'IND16T', 'RESHC', 'RESHC2', 'IDB64'):
        m[k] = C[k]
    return m



def emit_s1_qkv0(sec, A):
    seq = sec.begin(); sb = sec.sb; S, P = sec.S, sec.P
    S('wq', [128, 8, 1024], BF16); S('wk', [128, 8, 256], BF16); S('wv', [128, 8, 256], BF16)
    S('qnw', [128, 8]); S('knw', [128, 2])
    S('idf', [128, 128]); S('onesf', [128, 1]); S('onesr', [1, 128]); S('idb', [128, 128], BF16)
    S('scal', [128, 4])
    S('xt', [128, 4, 1024]); S('ut', [128, 4, 1024])
    S('lnstat', [128, 32]); S('lnscr', [128, 4, 1024])
    S('ufm', [128, 8, 512], BF16)
    S('qraw', [128, 8, 512]); S('sq', [128, 512]); S('rstd1', [1, 512]); S('rstd2', [1, 512])
    S('outb', [128, 8, 512], BF16)
    S('vfm', [128, 2, 512], BF16)
    S('vtmS', [128, 4, 256], BF16)
    psA = P("psA", [128, 512]); psT = P("psT", [128, 512]); psR = P("psR", [1, 512])
    psB = P("psB", [128, 512]); psC = P("psC", [128, 256], BF16)

    seq.phase_dma('gpsimd', lambda e: [
        e.dma_start(out=sb['wq'][:], in_=A['wqT'].rearrange("(c p) o -> p c o", p=128)),
        e.dma_start(out=sb['wk'][:], in_=A['wkT'].rearrange("(c p) o -> p c o", p=128)),
        e.dma_start(out=sb['wv'][:], in_=A['wvT'].rearrange("(c p) o -> p c o", p=128)),
    ], 3)
    seq.phase_dma('sync', lambda e: [
        e.dma_start(out=sb['qnw'][:], in_=A['qnw']),
        e.dma_start(out=sb['knw'][:], in_=A['knw']),
        e.dma_start(out=sb['idf'][:], in_=A['IDF']),
        e.dma_start(out=sb['onesf'][:], in_=A['ONESF']),
        e.dma_start(out=sb['onesr'][:], in_=A['ONESR']),
        e.dma_start(out=sb['idb'][:], in_=A['IDB']),
        e.dma_start(out=sb['scal'][:], in_=A['SCAL']),
    ], 7)
    seq.phase('vector', lambda e: e.tensor_scalar_mul(sb['qnw'][:], sb['qnw'][:], 0.125))

    def rms_block(wname, nouts, nrm_name):
        for oc in range(nouts):
            def mm(e, oc=oc):
                last = None
                for fc in range(8):
                    last = e.matmul(psA[:], sb[wname][:, fc, oc*128:(oc+1)*128], sb['ufm'][:, fc, :],
                                    start=(fc == 0), stop=(fc == 7))
                return last
            seq.phase('tensor', mm)
            seq.phase('vector', lambda e, oc=oc: e.tensor_copy(out=sb['qraw'][:, oc, :], in_=psA[:]))
            seq.phase('scalar', lambda e, oc=oc: e.activation(out=sb['sq'][:], in_=sb['qraw'][:, oc, :], func=AF.Square))
            seq.phase('tensor', lambda e, oc=oc, n=nouts: e.matmul(psR[:], sb['onesf'][:], sb['sq'][:],
                                                                   start=(oc == 0), stop=(oc == n-1)))
        dim = nouts * 128
        seq.phase('vector', lambda e: e.tensor_scalar_mul(sb['rstd1'][:], psR[:], 1.0/dim))
        seq.phase('scalar', lambda e: e.activation(out=sb['rstd2'][:], in_=sb['rstd1'][:], func=AF.Sqrt, bias=sb['scal'][0:1, 1:2]))
        seq.phase('vector', lambda e: e.reciprocal(sb['rstd1'][:], sb['rstd2'][:]))
        seq.phase('tensor', lambda e: e.matmul(psB[:], sb['onesr'][:], sb['rstd1'][:], start=True, stop=True))
        def app(e):
            last = None
            for oc in range(nouts):
                last = e.scalar_tensor_tensor(out=sb['outb'][:, oc, :], in0=sb['qraw'][:, oc, :],
                                              scalar=sb[nrm_name][:, oc:oc+1], in1=psB[:],
                                              op0=ALU.mult, op1=ALU.mult)
            return last
        seq.phase('vector', app)

    for tc in range(8):
        seq.phase_dma('sync', lambda e, tc=tc: [
            e.dma_start(out=sb['xt'][:, i, :], in_=A['x0'][tc*512+i*128: tc*512+(i+1)*128, :]) for i in range(4)
        ], 4)
        emit_ln(sec, [sb['xt'][:, i, :] for i in range(4)], [sb['ut'][:, i, :] for i in range(4)], 1024)
        for fc in range(8):
            def tr(e, fc=fc):
                last = None
                for i in range(4):
                    last = e.transpose(psT[:, i*128:(i+1)*128], sb['ut'][:, i, fc*128:(fc+1)*128], sb['idf'][:])
                return last
            seq.phase('tensor', tr)
            seq.phase('vector', lambda e, fc=fc: e.tensor_copy(out=sb['ufm'][:, fc, :], in_=psT[:]))
        rms_block('wq', 8, 'qnw')
        seq.phase_dma('sync', lambda e, tc=tc: [
            e.dma_start(out=A['qf'][oc*128:(oc+1)*128, tc*512:(tc+1)*512], in_=sb['outb'][:, oc, :]) for oc in range(8)
        ], 8)
        rms_block('wk', 2, 'knw')
        seq.phase_dma('sync', lambda e, tc=tc: [
            e.dma_start(out=A['kf'][oc*128:(oc+1)*128, tc*512:(tc+1)*512], in_=sb['outb'][:, oc, :]) for oc in range(2)
        ], 2)
        for oc in range(2):
            def mmv(e, oc=oc):
                last = None
                for fc in range(8):
                    last = e.matmul(psA[:], sb['wv'][:, fc, oc*128:(oc+1)*128], sb['ufm'][:, fc, :],
                                    start=(fc == 0), stop=(fc == 7))
                return last
            seq.phase('tensor', mmv)
            seq.phase('vector', lambda e, oc=oc: e.tensor_copy(out=sb['vfm'][:, oc, :], in_=psA[:]))
        for i in range(4):
            def trv(e, i=i):
                e.transpose(psC[:, 0:128], sb['vfm'][:, 0, i*128:(i+1)*128], sb['idb'][:])
                return e.transpose(psC[:, 128:256], sb['vfm'][:, 1, i*128:(i+1)*128], sb['idb'][:])
            seq.phase('tensor', trv)
            seq.phase('vector', lambda e, i=i: e.tensor_copy(out=sb['vtmS'][:, i, :], in_=psC[:]))
        seq.phase_dma('sync', lambda e, tc=tc: [
            e.dma_start(out=A['vtm'][tc*512+i*128: tc*512+(i+1)*128, :], in_=sb['vtmS'][:, i, :]) for i in range(4)
        ], 4)
    sec.end()


def emit_s2_att0(sec, A):
    seq = sec.begin(); sb = sec.sb; S, P = sec.S, sec.P
    S('kg', [64, 2, T], BF16)
    S('vtm', [128, 32, 256], BF16)
    S('rel', [128, 16, 640], BF16)
    S('wo', [128, 8, 1024], BF16)
    S('q', [64, 2, 2048], BF16)
    S('sS', [128, 640]); S('sP', [128, 640], BF16); S('sPT', [128, 640], BF16)
    S('nm', [128, 1]); S('sums', [128, 1]); S('rs', [128, 1])
    S('aoS', [128, 64], BF16)
    S('aofm', [128, 8, T], BF16)
    S('x0t', [128, 1024]); S('x1t', [128, 1024])
    S('idb', [128, 128], BF16)
    psS = P("psS", [128, 640]); psPT = P("psPT", [128, 640], BF16)
    psO = P("psO", [128, 64]); psT2 = P("psT2", [64, 128], BF16)
    psPRJ = P("psPRJ", [128, 1024])
    seq.phase_dma('sync', lambda e: [
        e.dma_start(out=sb['vtm'][:], in_=A['vtm'].rearrange("(n p) f -> p n f", p=128)),
        e.dma_start(out=sb['rel'][:], in_=A['RELB'].rearrange("h p k -> p h k")),
        e.dma_start(out=sb['idb'][:], in_=A['IDB']),
    ], 3)
    seq.phase_dma('gpsimd', lambda e: [
        e.dma_start(out=sb['wo'][:], in_=A['woT'].rearrange("(c p) o -> p c o", p=128)),
    ], 1)

    iters = [(b, h, qt) for b in range(2) for h in range(16) for qt in range(16)]
    t_ldq = {}
    t_ldk = {}
    t_lastqk = {}
    t = {'bias': None, 'exp': None, 'nm': None, 'trp': None, 'cpy': None,
         'pv': None, 'rs': None, 'aoS': None, 'tr2': None, 'cp2': None}

    def emit_ldk(bg):
        b, g = bg
        par = g % 2
        prev = (b, g - 2) if g >= 2 else (b - 1, g + 2)
        dep = t_lastqk.get(('g',) + prev, None)
        t_ldk[bg] = seq.phase_dma('sync', lambda e, g=g, par=par: [
            e.dma_start(out=sb['kg'][:, par, :], in_=A['kf'][g*64:(g+1)*64, :]),
        ], 1, deps=[dep] if dep else None)

    def emit_ldq(bh):
        b, h = bh
        par = h % 2
        dep = t_lastqk.get((b, h - 2) if h >= 2 else (b - 1, h + 14), None)
        t_ldq[bh] = seq.phase_dma('sync', lambda e, h=h, b=b, par=par: [
            e.dma_start(out=sb['q'][:, par, :], in_=A['qf'][h*64:(h+1)*64, b*2048:(b+1)*2048]),
        ], 1, deps=[dep] if dep else None)

    def emit_qk(i):
        b, h, qt = iters[i]
        g = h // 4
        off = max(0, 512 - 128*qt)
        kc0 = off // 128
        kstart = b*2048 + 128*qt - 512
        def qk(e, qt=qt, kc0=kc0, kstart=kstart, hp=h % 2, gp=g % 2):
            last = None
            for kc in range(kc0, 5):
                ktile = (kstart + kc*128) // 128
                last = e.matmul(psS[:, kc*128:(kc+1)*128],
                                sb['q'][:, hp, qt*128:(qt+1)*128],
                                sb['kg'][:, gp, ktile*128:(ktile+1)*128],
                                start=True, stop=True)
            return last
        tok = seq.phase('tensor', qk, deps=[t_ldq[(b, h)], t_ldk[(b, g)], t['bias']])
        t_lastqk[(b, h)] = tok
        t_lastqk[('g', b, g)] = tok
        return tok

    def emit_head(i, t_qk):
        b, h, qt = iters[i]
        off = max(0, 512 - 128*qt)
        n = 640 - off
        t['bias'] = seq.phase('vector', lambda e, h=h, off=off, n=n: e.scalar_tensor_tensor(
            out=sb['sS'][:, 0:n], in0=psS[:, off:640], scalar=1.0,
            in1=sb['rel'][:, h, off:640], op0=ALU.mult, op1=ALU.add),
            deps=[t_qk, t['exp'], t['nm']])

    def emit_body(i):
        b, h, qt = iters[i]
        g = h // 4
        off = max(0, 512 - 128*qt)
        kc0 = off // 128
        kstart = b*2048 + 128*qt - 512
        n = 640 - off
        t['nm'] = seq.phase('vector', lambda e, n=n: e.tensor_reduce(out=sb['nm'][:], in_=sb['sS'][:, 0:n], axis=AX.X, op=ALU.max, negate=True),
                            deps=[t['bias'], t['exp']])
        t['exp'] = seq.phase('scalar', lambda e, n=n: e.activation(out=sb['sP'][:, 0:n], in_=sb['sS'][:, 0:n], func=AF.Exp,
                                                                   bias=sb['nm'][:, 0:1], accum_out=sb['sums'][:, 0:1]),
                             deps=[t['nm'], t['trp'], t['rs']])
        def trp(e, kc0=kc0, off=off):
            last = None
            for kc in range(kc0, 5):
                last = e.transpose(psPT[:, kc*128:(kc+1)*128], sb['sP'][:, kc*128-off:(kc+1)*128-off], sb['idb'][:])
            return last
        t['trp'] = seq.phase('tensor', trp, deps=[t['exp'], t['cpy']])
        t['cpy'] = seq.phase('vector', lambda e, kc0=kc0: e.tensor_copy(out=sb['sPT'][:, kc0*128:640], in_=psPT[:, kc0*128:640]),
                             deps=[t['trp'], t['pv']])
        def pv(e, g=g, kc0=kc0, kstart=kstart):
            last = None
            for kc in range(kc0, 5):
                ktile = (kstart + kc*128) // 128
                last = e.matmul(psO[:], sb['sPT'][:, kc*128:(kc+1)*128],
                                sb['vtm'][:, ktile, g*64:(g+1)*64],
                                start=(kc == kc0), stop=(kc == 4))
            return last
        t['pv'] = seq.phase('tensor', pv, deps=[t['cpy'], t['aoS']])
        t['rs'] = seq.phase('vector', lambda e: e.reciprocal(sb['rs'][:], sb['sums'][:]),
                            deps=[t['exp'], t['aoS']])
        t['aoS'] = seq.phase('vector', lambda e: e.tensor_scalar(out=sb['aoS'][:], in0=psO[:], scalar1=sb['rs'][:, 0:1], scalar2=None, op0=ALU.mult),
                             deps=[t['pv'], t['rs'], t['tr2']])
        t['tr2'] = seq.phase('tensor', lambda e: e.transpose(psT2[:], sb['aoS'][:], sb['idb'][:]),
                             deps=[t['aoS'], t['cp2']])
        t['cp2'] = seq.phase('vector', lambda e, b=b, h=h, qt=qt: e.tensor_copy(
            out=sb['aofm'][(h % 2)*64:(h % 2)*64+64, h//2, b*2048+qt*128: b*2048+(qt+1)*128], in_=psT2[:]),
            deps=[t['tr2']])

    emit_ldk((0, 0))
    emit_ldk((0, 1))
    emit_ldq((0, 0))
    emit_ldq((0, 1))
    pend = emit_qk(0)
    for i in range(len(iters)):
        emit_head(i, pend)
        if i + 1 < len(iters):
            bn, hn, qn = iters[i+1]
            if qn == 0:
                nb, nh = (bn, hn + 1) if hn < 15 else (bn + 1, 0)
                if nb < 2:
                    emit_ldq((nb, nh))
                if hn % 4 == 0:
                    gcur = (bn, hn // 4)
                    gnxt = (gcur[0], gcur[1] + 1) if gcur[1] < 3 else (gcur[0] + 1, 0)
                    if gnxt[0] < 2 and gnxt not in t_ldk:
                        emit_ldk(gnxt)
            pend = emit_qk(i + 1)
        emit_body(i)
    for tt in range(32):
        seq.phase_dma('sync', lambda e, tt=tt: [e.dma_start(out=sb['x0t'][:], in_=A['x0'][tt*128:(tt+1)*128, :])], 1)
        def prj(e, tt=tt):
            last = None
            for half in range(2):
                for fc in range(8):
                    last = e.matmul(psPRJ[:, half*512:(half+1)*512],
                                    sb['aofm'][:, fc, tt*128:(tt+1)*128],
                                    sb['wo'][:, fc, half*512:(half+1)*512],
                                    start=(fc == 0), stop=(fc == 7))
            return last
        seq.phase('tensor', prj)
        seq.phase('vector', lambda e: e.tensor_tensor(out=sb['x1t'][:], in0=sb['x0t'][:], in1=psPRJ[:], op=ALU.add))
        seq.phase_dma('sync', lambda e, tt=tt: [e.dma_start(out=A['x1loc'][tt*128:(tt+1)*128, :], in_=sb['x1t'][:])], 1)
    sec.end()


def emit_reshard(sec, A, src_loc, src_g, dst, base_name, coef):
    seq = sec.begin(); sb = sec.sb; S, P = sec.S, sec.P
    S('resh', [128, 32]); S('cid', [128, 1]); S('idxf', [128, 32]); S('idxi', [128, 32], I32)
    S('cids', [128, 1]); S('gt', [128, 1024])
    seq.phase_cc(lambda e: e.collective_compute(
        "AllGather", ALU.bypass, replica_groups=[list(range(8))],
        ins=[A[src_loc].opt()], outs=[A[src_g].opt()]))
    seq.phase_dma('sync', lambda e: [
        e.dma_start(out=sb['resh'][:], in_=A[base_name]),
        e.dma_start(out=sb['cid'][:], in_=A['CID']),
    ], 2)
    seq.phase('vector', lambda e: e.tensor_scalar_mul(sb['cids'][:], sb['cid'][:], float(coef)))
    seq.phase('vector', lambda e: e.tensor_scalar(out=sb['idxf'][:], in0=sb['resh'][:], scalar1=1.0,
                                                  scalar2=sb['cids'][:, 0:1], op0=ALU.mult, op1=ALU.add))
    seq.phase('vector', lambda e: e.tensor_copy(out=sb['idxi'][:], in_=sb['idxf'][:]))
    for j in range(32):
        seq.phase_dma('gpsimd', lambda e, j=j: [
            e.indirect_dma_start(out=sb['gt'][:], out_offset=None, in_=A[src_g],
                                 in_offset=bass.IndirectOffsetOnAxis(ap=sb['idxi'][:, j:j+1], axis=0)),
        ], 1)
        seq.phase_dma('sync', lambda e, j=j: [
            e.dma_start(out=A[dst][j*128:(j+1)*128, :], in_=sb['gt'][:]),
        ], 1)
    sec.end()


def emit_s4_ffn0a(sec, A):
    seq = sec.begin(); sb = sec.sb; S, P = sec.S, sec.P
    S('wk', [128, 8, 1536], BF16)
    S('idf', [128, 128]); S('scal', [128, 4])
    S('w1rep', [128, 1024]); S('b1rep', [128, 1024])
    S('xt', [128, 4, 1024]); S('ut', [128, 4, 1024]); S('lt', [128, 4, 1024])
    S('lnstat', [128, 32]); S('lnscr', [128, 4, 1024])
    S('ufm', [128, 8, 512], BF16)
    S('outb', [128, 12, 512], BF16)
    psA = P("psA", [128, 512]); psT = P("psT", [128, 512])
    seq.phase_dma('gpsimd', lambda e: [
        e.dma_start(out=sb['wk'][:], in_=A['qkvT'].rearrange("(c p) o -> p c o", p=128)),
    ], 1)
    seq.phase_dma('sync', lambda e: [
        e.dma_start(out=sb['idf'][:], in_=A['IDF']),
        e.dma_start(out=sb['scal'][:], in_=A['SCAL']),
        e.dma_start(out=sb['w1rep'][:], in_=A['W1REP']),
        e.dma_start(out=sb['b1rep'][:], in_=A['B1REP']),
    ], 4)
    for tc in range(8):
        seq.phase_dma('sync', lambda e, tc=tc: [
            e.dma_start(out=sb['xt'][:, i, :], in_=A['x1r'][tc*512+i*128: tc*512+(i+1)*128, :]) for i in range(4)
        ], 4)
        emit_ln(sec, [sb['xt'][:, i, :] for i in range(4)], [sb['ut'][:, i, :] for i in range(4)], 1024)
        seq.phase_dma('sync', lambda e, tc=tc: [
            e.dma_start(out=A['u1'][tc*512+i*128: tc*512+(i+1)*128, :], in_=sb['ut'][:, i, :]) for i in range(4)
        ], 4)
        def lw(e):
            last = None
            for i in range(4):
                last = e.tensor_tensor(out=sb['lt'][:, i, :], in0=sb['ut'][:, i, :], in1=sb['w1rep'][:], op=ALU.mult)
            return last
        seq.phase('vector', lw)
        def lb(e):
            last = None
            for i in range(4):
                last = e.tensor_tensor(out=sb['lt'][:, i, :], in0=sb['lt'][:, i, :], in1=sb['b1rep'][:], op=ALU.add)
            return last
        seq.phase('vector', lb)
        for fc in range(8):
            def tr(e, fc=fc):
                last = None
                for i in range(4):
                    last = e.transpose(psT[:, i*128:(i+1)*128], sb['lt'][:, i, fc*128:(fc+1)*128], sb['idf'][:])
                return last
            seq.phase('tensor', tr)
            seq.phase('vector', lambda e, fc=fc: e.tensor_copy(out=sb['ufm'][:, fc, :], in_=psT[:]))
        for oc in range(12):
            def mm(e, oc=oc):
                last = None
                for fc in range(8):
                    last = e.matmul(psA[:], sb['wk'][:, fc, oc*128:(oc+1)*128], sb['ufm'][:, fc, :],
                                    start=(fc == 0), stop=(fc == 7))
                return last
            seq.phase('tensor', mm)
            g, j = oc // 3, oc % 3
            if j < 2:
                seq.phase('vector', lambda e, oc=oc: e.tensor_scalar_mul(sb['outb'][:, oc, :], psA[:], 0.125))
            else:
                seq.phase('vector', lambda e, oc=oc: e.tensor_copy(out=sb['outb'][:, oc, :], in_=psA[:]))
        def st(e, tc=tc):
            ds = []
            for oc in range(12):
                g, j = oc // 3, oc % 3
                if j < 2:
                    ds.append(e.dma_start(out=A['qtf'][(g*256+j*128):(g*256+j*128+128), tc*512:(tc+1)*512],
                                          in_=sb['outb'][:, oc, :]))
                else:
                    ds.append(e.dma_start(out=A['ktf'][g*64:(g+1)*64, tc*512:(tc+1)*512],
                                          in_=sb['outb'][0:64, oc, :]))
                    ds.append(e.dma_start(out=A['vtf'][g*64:(g+1)*64, tc*512:(tc+1)*512],
                                          in_=sb['outb'][64:128, oc, :]))
            return ds
        seq.phase_dma('sync', st, 16)
    sec.end()


S5STAGE = [99]

def emit_s5_tiny(sec, A):
    seq = sec.begin(); sb = sec.sb; S, P = sec.S, sec.P
    S('idb', [128, 128], BF16)
    S('idb64', [128, 64], BF16)
    S('tmask', [128, 512], BF16)
    S('ind16', [128, 8], BF16)
    S('ind16t', [8, 128])
    S('vtfs', [128, 2, T], BF16)
    S('kbd', [128, 4, 1024], BF16)
    S('qmv', [128, 4, 2048], BF16)
    S('vmvT', [128, 4, 8, 64], BF16)
    S('sE', [128, 512], BF16)
    S('mE', [128, 512], BF16)
    S('rD', [8, 512])
    S('sP', [128, 512], BF16)
    S('pbd', [128, 4, 1024], BF16)
    S('aostage', [128, 2, 4096], BF16)
    psSC = P("psSC", [128, 512])
    psD = P("psD", [8, 512])
    psB = P("psB", [128, 512])
    psAO = [P(f"psAO{b4}", [128, 256]) for b4 in range(4)]
    psTv = P("psTv", [128, 512], BF16)
    seq.phase_dma('sync', lambda e: [
        e.dma_start(out=sb['idb'][:], in_=A['IDB']),
        e.dma_start(out=sb['idb64'][:], in_=A['IDB64']),
        e.dma_start(out=sb['tmask'][:], in_=A['TMASKM']),
        e.dma_start(out=sb['ind16'][:], in_=A['IND16']),
        e.dma_start(out=sb['ind16t'][:], in_=A['IND16T']),
    ], 5)
    seq.phase('vector', lambda e: e.memset(sb['pbd'][:], 0.0))
    seq.phase('vector', lambda e: e.memset(sb['kbd'][:], 0.0))
    seq.phase_dma('gpsimd', lambda e: [
        e.dma_start(out=sb['vtfs'][:], in_=A['vtf'].rearrange("(c p) t -> p c t", p=128)),
    ], 1)
    for ch in range(4):
        chp = ch * 32
        def ldk(e, chp=chp):
            ds = []
            for g in range(4):
                for pp in range(2):
                    ds.append(e.dma_start(
                        out=sb['kbd'][pp*64:(pp+1)*64, g, :].rearrange("p (a t) -> p a t", t=32)[:, :, pp*16:(pp+1)*16],
                        in_=A['ktf'][g*64:(g+1)*64, :].rearrange("p (a w) -> p a w", w=32)[:, chp:chp+32, pp*16:(pp+1)*16]))
            return ds
        seq.phase_dma('sync', ldk, 8)
        def ldq(e, chp=chp):
            ds = []
            for g in range(4):
                for qi in range(4):
                    for pp in range(2):
                        ds.append(e.dma_start(
                            out=sb['qmv'][pp*64:(pp+1)*64, g, :].rearrange("p (a x) -> p a x", x=64)[:, :, qi*16:(qi+1)*16],
                            in_=A['qtf'][g*256+qi*64:g*256+qi*64+64, :].rearrange("p (a w) -> p a w", w=32)[:, chp:chp+32, pp*16:(pp+1)*16]))
            return ds
        seq.phase_dma('gpsimd', ldq, 32)
        for g in range(4):
            def trv(e, g=g, ch=ch):
                last = None
                for tt in range(8):
                    last = e.transpose(psTv[:, tt*64:(tt+1)*64],
                                       sb['vtfs'][(g % 2)*64:(g % 2)*64+64, g//2, ch*1024+tt*128: ch*1024+(tt+1)*128],
                                       sb['idb64'][(g % 2)*64:(g % 2)*64+64, :])
                return last
            seq.phase('tensor', trv)
            seq.phase('vector', lambda e, g=g: e.tensor_copy(
                out=sb['vmvT'][:, g, :, :].rearrange("p a d -> p (a d)"), in_=psTv[:]))
        if S5STAGE[0] < 1:
            continue
        for sg in range(4):
            def qk(e, sg=sg):
                last = None
                for g in range(4):
                    for pi in range(8):
                        pr = sg*8 + pi
                        last = e.matmul(psSC[g*32:(g+1)*32, pi*64:(pi+1)*64],
                                        sb['kbd'][:, g, pr*32:(pr+1)*32],
                                        sb['qmv'][:, g, pr*64:(pr+1)*64],
                                        start=True, stop=True, tile_position=(0, g*32))
                return last
            seq.phase('tensor', qk)
            if S5STAGE[0] < 2:
                continue
            seq.phase('scalar', lambda e: e.activation(out=sb['sE'][:], in_=psSC[:], func=AF.Exp))
            seq.phase('vector', lambda e: e.tensor_tensor(out=sb['mE'][:], in0=sb['sE'][:], in1=sb['tmask'][:], op=ALU.mult))
            seq.phase('tensor', lambda e: e.matmul(psD[:], sb['ind16'][:], sb['mE'][:], start=True, stop=True))
            seq.phase('vector', lambda e: e.reciprocal(sb['rD'][:], psD[:]))
            seq.phase('tensor', lambda e: e.matmul(psB[:], sb['ind16t'][:], sb['rD'][:], start=True, stop=True))
            seq.phase('vector', lambda e: e.tensor_tensor(out=sb['sP'][:], in0=sb['mE'][:], in1=psB[:], op=ALU.mult))
            if S5STAGE[0] < 3:
                continue
            def pba(e):
                ds = []
                for g in range(4):
                    for pp in range(2):
                        for b4 in range(4):
                            ds.append(e.dma_start(
                                out=sb['pbd'][b4*32+pp*16: b4*32+pp*16+16, g, :]
                                    .rearrange("p (a x) -> p a x", x=128)[:, b4::4, pp*64:(pp+1)*64],
                                in_=sb['sP'][g*32+pp*16: g*32+pp*16+16, :]
                                    .rearrange("p (a x) -> p a x", x=64)[:, b4::4, :]))
                return ds
            seq.phase_dma('gpsimd', pba, 32)
            if S5STAGE[0] < 4:
                continue
            for rnd in range(2):
                def ao(e, sg=sg, rnd=rnd):
                    last = None
                    for b4 in range(4):
                        pi = rnd*4 + b4
                        tt = (sg*8 + pi) // 4
                        for g in range(4):
                            last = e.matmul(psAO[b4][(g % 2)*64:(g % 2)*64+64, (g//2)*128:(g//2)*128+128],
                                            sb['vmvT'][b4*32:(b4+1)*32, g, tt, :],
                                            sb['pbd'][b4*32:(b4+1)*32, g, pi*128:(pi+1)*128],
                                            start=True, stop=True, tile_position=(b4*32, (g % 2)*64))
                    return last
                seq.phase('tensor', ao)
                def ev(e, sg=sg, rnd=rnd):
                    last = None
                    for b4 in range(4):
                        pi = rnd*4 + b4
                        last = e.tensor_copy(
                            out=sb['aostage'][:, :, sg*1024+pi*128: sg*1024+(pi+1)*128],
                            in_=psAO[b4].rearrange("p (g x) -> p g x", x=128))
                    return last
                seq.phase('vector', ev)
        if S5STAGE[0] < 5:
            continue
        def afm(e, ch=ch):
            ds = []
            for g in range(4):
                for qi in range(4):
                    for pp in range(2):
                        src = sb['aostage'][(g % 2)*64:(g % 2)*64+64, g//2, :] \
                            .rearrange("p (a b x) -> p a b x", b=8, x=128)[:, :, :, pp*64+qi*16: pp*64+qi*16+16]
                        dst = A['aofm_t'][g*256+qi*64: g*256+qi*64+64, :] \
                            .rearrange("p (a b pz s) -> p a b pz s", b=8, pz=2, s=16)[:, ch*4:(ch+1)*4, :, pp, :]
                        ds.append(e.dma_start(out=dst, in_=src))
            return ds
        seq.phase_dma('gpsimd', afm, 32)
    sec.end()


def emit_s6_ffn0b(sec, A):
    seq = sec.begin(); sb = sec.sb; S, P = sec.S, sec.P
    S('aofm', [128, 8, T], BF16)
    S('pw', [128, 8, 1024], BF16)
    S('rw', [128, 8, 8], BF16)
    S('iota8', [128, 8]); S('srow', [128, 16])
    S('idf', [128, 128]); S('scal', [128, 4])
    S('w2rep', [128, 1024]); S('b2rep', [128, 1024])
    S('xt', [128, 1, 1024]); S('ut', [128, 1, 1024])
    S('lt', [128, 1024]); S('l2b', [128, 1024], BF16)
    S('l2fm', [128, 8, 128], BF16)
    S('lnstat', [128, 32]); S('lnscr', [128, 1, 1024])
    S('u1t', [128, 1024])
    S('lg', [128, 8]); S('mx', [128, 8]); S('oh1', [128, 8]); S('oh2', [128, 8])
    S('msk', [128, 8]); S('indt', [128, 8])
    S('e1c', [128, 8]); S('e2c', [128, 8])
    S('rt', [128, 8]); S('Rt', [16, 8])
    psPRJ = P("psPRJ", [128, 1024]); psT = P("psT", [128, 512])
    psLG = P("psLG", [128, 8]); psRT = P("psRT", [16, 8])
    seq.phase_dma('gpsimd', lambda e: [
        e.dma_start(out=sb['aofm'][:], in_=A['aofm_t'].rearrange("(c p) t -> p c t", p=128)),
        e.dma_start(out=sb['pw'][:], in_=A['prjT'].rearrange("(c p) o -> p c o", p=128)),
        e.dma_start(out=sb['rw'][:], in_=A['rw'].rearrange("(c p) o -> p c o", p=128)),
    ], 3)
    seq.phase_dma('sync', lambda e: [
        e.dma_start(out=sb['iota8'][:], in_=A['IOTA8']),
        e.dma_start(out=sb['srow'][:], in_=A['SROWSEL']),
        e.dma_start(out=sb['idf'][:], in_=A['IDF']),
        e.dma_start(out=sb['scal'][:], in_=A['SCAL']),
        e.dma_start(out=sb['w2rep'][:], in_=A['W2REP']),
        e.dma_start(out=sb['b2rep'][:], in_=A['B2REP']),
    ], 6)
    for tt in range(32):
        seq.phase_dma('sync', lambda e, tt=tt: [
            e.dma_start(out=sb['u1t'][:], in_=A['u1'][tt*128:(tt+1)*128, :]),
        ], 1)
        def prj(e, tt=tt):
            last = None
            for half in range(2):
                for fc in range(8):
                    last = e.matmul(psPRJ[:, half*512:(half+1)*512],
                                    sb['aofm'][:, fc, tt*128:(tt+1)*128],
                                    sb['pw'][:, fc, half*512:(half+1)*512],
                                    start=(fc == 0), stop=(fc == 7))
            return last
        seq.phase('tensor', prj)
        seq.phase('vector', lambda e: e.tensor_tensor(out=sb['xt'][:, 0, :], in0=sb['u1t'][:], in1=psPRJ[:], op=ALU.add))
        seq.phase_dma('sync', lambda e, tt=tt: [
            e.dma_start(out=A['h1'][tt*128:(tt+1)*128, :], in_=sb['xt'][:, 0, :]),
        ], 1)
        emit_ln(sec, [sb['xt'][:, 0, :]], [sb['ut'][:, 0, :]], 1024)
        seq.phase('vector', lambda e: e.tensor_tensor(out=sb['lt'][:], in0=sb['ut'][:, 0, :], in1=sb['w2rep'][:], op=ALU.mult))
        seq.phase('vector', lambda e: e.tensor_tensor(out=sb['lt'][:], in0=sb['lt'][:], in1=sb['b2rep'][:], op=ALU.add))
        seq.phase('vector', lambda e: e.tensor_copy(out=sb['l2b'][:], in_=sb['lt'][:]))
        seq.phase_dma('sync', lambda e, tt=tt: [
            e.dma_start(out=A['ln2tm'][tt*128:(tt+1)*128, :], in_=sb['l2b'][:]),
        ], 1)
        for fc in range(4):
            def tr(e, fc=fc):
                e.transpose(psT[:, 0:128], sb['lt'][:, (2*fc)*128:(2*fc+1)*128], sb['idf'][:])
                return e.transpose(psT[:, 128:256], sb['lt'][:, (2*fc+1)*128:(2*fc+2)*128], sb['idf'][:])
            seq.phase('tensor', tr)
            def ev(e, fc=fc):
                e.tensor_copy(out=sb['l2fm'][:, 2*fc, :], in_=psT[:, 0:128])
                return e.tensor_copy(out=sb['l2fm'][:, 2*fc+1, :], in_=psT[:, 128:256])
            seq.phase('vector', ev)
        def rmm(e):
            last = None
            for fc in range(8):
                last = e.matmul(psLG[:], sb['l2fm'][:, fc, :], sb['rw'][:, fc, :],
                                start=(fc == 0), stop=(fc == 7))
            return last
        seq.phase('tensor', rmm)
        seq.phase('vector', lambda e: e.tensor_copy(out=sb['lg'][:], in_=psLG[:]))
        seq.phase('vector', lambda e: e.max(out=sb['mx'][:], in_=sb['lg'][:]))
        seq.phase('vector', lambda e: e.tensor_scalar(out=sb['oh1'][:], in0=sb['lg'][:], scalar1=sb['mx'][:, 0:1], scalar2=None, op0=ALU.is_equal))
        seq.phase('vector', lambda e: e.scalar_tensor_tensor(out=sb['msk'][:], in0=sb['oh1'][:], scalar=NEG, in1=sb['lg'][:], op0=ALU.mult, op1=ALU.add))
        seq.phase('vector', lambda e: e.tensor_scalar(out=sb['oh2'][:], in0=sb['msk'][:], scalar1=sb['mx'][:, 1:2], scalar2=None, op0=ALU.is_equal))
        seq.phase('vector', lambda e: e.tensor_tensor(out=sb['indt'][:], in0=sb['oh1'][:], in1=sb['oh2'][:], op=ALU.add))
        def ec(e):
            e.tensor_tensor(out=sb['e1c'][:], in0=sb['oh1'][:], in1=sb['iota8'][:], op=ALU.mult)
            return e.tensor_tensor(out=sb['e2c'][:], in0=sb['oh2'][:], in1=sb['iota8'][:], op=ALU.mult)
        seq.phase('vector', ec)
        def er(e):
            e.tensor_reduce(out=sb['rt'][:, 0:1], in_=sb['e1c'][:], axis=AX.X, op=ALU.add)
            return e.tensor_reduce(out=sb['rt'][:, 1:2], in_=sb['e2c'][:], axis=AX.X, op=ALU.add)
        seq.phase('vector', er)
        seq.phase('vector', lambda e: e.tensor_tensor(out=sb['rt'][:, 4:5], in0=sb['mx'][:, 0:1], in1=sb['mx'][:, 1:2], op=ALU.subtract))
        seq.phase('scalar', lambda e: e.activation(out=sb['rt'][:, 2:3], in_=sb['rt'][:, 4:5], func=AF.Sigmoid))
        seq.phase('vector', lambda e: e.tensor_scalar(out=sb['rt'][:, 3:4], in0=sb['rt'][:, 2:3], scalar1=-1.0, scalar2=1.0, op0=ALU.mult, op1=ALU.add))
        seq.phase('tensor', lambda e, tt=tt: e.matmul(psRT[:], sb['srow'][:], sb['indt'][:],
                                                      start=(tt == 0), stop=(tt == 31)))
        seq.phase_dma('sync', lambda e, tt=tt: [
            e.dma_start(out=A['ind'][tt*128:(tt+1)*128, :], in_=sb['indt'][:]),
            e.dma_start(out=A['route'][tt*128:(tt+1)*128, :], in_=sb['rt'][:, 0:4]),
        ], 2)
    seq.phase('vector', lambda e: e.tensor_copy(out=sb['Rt'][:], in_=psRT[:]))
    seq.phase_dma('sync', lambda e: [e.dma_start(out=A['Rloc'], in_=sb['Rt'][:])], 1)
    sec.end()


def emit_s7_rgather(sec, A):
    seq = sec.begin()
    seq.phase_cc(lambda e: e.collective_compute(
        "AllGather", ALU.bypass, replica_groups=[list(range(8))],
        ins=[A['Rloc'].opt()], outs=[A['Rg'].opt()]))
    sec.end()


def emit_s8_disp(sec, A):
    seq = sec.begin(); sb = sec.sb; S, P = sec.S, sec.P
    for nm, shp in [('tri', [128, 128]), ('allones', [128, 128]), ('csel', [128, 16]),
                    ('coretri', [128, 16]), ('iota8', [128, 8]), ('onesf', [128, 1])]:
        S(nm, shp)
    S('tri16', [16, 16]); S('selbank', [16, 16*128]); S('onesr', [1, 128])
    S('Rall', [128, 8]); S('G', [16, 8]); S('SOFF', [16, 8])
    S('ind0', [128, 8]); S('ind1', [128, 8])
    S('oh1', [128, 8]); S('oh2', [128, 8])
    S('rt', [128, 4])
    S('lnt', [128, 1024], BF16)
    S('vm', [128, 8]); S('tmpA', [128, 8]); S('tmpB', [128, 8])
    S('vindt', [128, 8]); S('slotsS', [128, 8])
    S('tmpC', [128, 8]); S('tmpD', [128, 8])
    S('wt', [128, 24])
    S('ixt1', [128, 1], I32); S('ixt2', [128, 1], I32)
    S('pvt', [128, 2])
    S('rtA', [1, 8]); S('rtB', [1, 8]); S('rtS', [1, 8])
    psPF = P("psPF", [128, 8]); psSL = P("psSL", [128, 8]); psR2 = P("psR2", [1, 8])
    psG = P("psG", [16, 8]); psSO = P("psSO", [16, 8])
    seq.phase_dma('sync', lambda e: [
        e.dma_start(out=sb['tri'][:], in_=A['TRI128']),
        e.dma_start(out=sb['tri16'][:], in_=A['TRI16']),
        e.dma_start(out=sb['allones'][:], in_=A['ALLONES']),
        e.dma_start(out=sb['csel'][:], in_=A['CSEL']),
        e.dma_start(out=sb['coretri'][:], in_=A['CORETRI']),
        e.dma_start(out=sb['selbank'][:], in_=A['SELBANK']),
        e.dma_start(out=sb['iota8'][:], in_=A['IOTA8']),
        e.dma_start(out=sb['onesf'][:], in_=A['ONESF']),
        e.dma_start(out=sb['onesr'][:], in_=A['ONESR']),
        e.dma_start(out=sb['Rall'][:], in_=A['Rg']),
    ], 10)
    seq.phase('tensor', lambda e: e.matmul(psG[:], sb['csel'][:], sb['Rall'][:], start=True, stop=True))
    seq.phase('vector', lambda e: e.tensor_copy(out=sb['G'][:], in_=psG[:]))
    def soff(e):
        e.matmul(psSO[:], sb['tri16'][:], sb['G'][:], start=True, stop=False)
        return e.matmul(psSO[:], sb['coretri'][:], sb['Rall'][:], start=False, stop=True)
    seq.phase('tensor', soff)
    seq.phase('vector', lambda e: e.tensor_copy(out=sb['SOFF'][:], in_=psSO[:]))
    seq.phase('vector', lambda e: e.memset(sb['rtA'][:], 0.0))
    ind_r = A['ind'].rearrange("(p s) e -> s p e", s=16)
    rt_r = A['route'].rearrange("(p s) e -> s p e", s=16)
    ln_r = A['ln2tm'].rearrange("(p s) f -> s p f", s=16)
    for j in range(32):
        s_, pb_ = j // 2, j % 2
        cur = 'ind0' if j % 2 == 0 else 'ind1'
        prev = 'ind1' if j % 2 == 0 else 'ind0'
        rt_cur = 'rtA' if j % 2 == 0 else 'rtB'
        rt_nxt = 'rtB' if j % 2 == 0 else 'rtA'
        seq.phase_dma('sync', lambda e, s_=s_, pb_=pb_, cur=cur: [
            e.dma_start(out=sb[cur][:], in_=ind_r[s_, pb_*128:(pb_+1)*128, :]),
            e.dma_start(out=sb['rt'][:], in_=rt_r[s_, pb_*128:(pb_+1)*128, :]),
            e.dma_start(out=sb['lnt'][:], in_=ln_r[s_, pb_*128:(pb_+1)*128, :]),
        ], 3)
        def g1(e, s_=s_, pb_=pb_, cur=cur, prev=prev):
            e.matmul(psPF[:], sb['tri'][:], sb[cur][:], start=True, stop=False)
            if pb_ == 1:
                e.matmul(psPF[:], sb['allones'][:], sb[prev][:], start=False, stop=False)
            return e.matmul(psPF[:], sb['selbank'][:, s_*128:(s_+1)*128], sb['SOFF'][:],
                            start=False, stop=True)
        seq.phase('tensor', g1)
        seq.phase('vector', lambda e: e.tensor_scalar(out=sb['vm'][:], in0=psPF[:], scalar1=float(CAP0), scalar2=None, op0=ALU.is_lt))
        def v2(e):
            e.tensor_scalar(out=sb['oh1'][:], in0=sb['iota8'][:], scalar1=sb['rt'][:, 0:1], scalar2=None, op0=ALU.is_equal)
            return e.tensor_scalar(out=sb['oh2'][:], in0=sb['iota8'][:], scalar1=sb['rt'][:, 1:2], scalar2=None, op0=ALU.is_equal)
        seq.phase('vector', v2)
        def v3(e):
            e.tensor_tensor(out=sb['tmpA'][:], in0=sb['vm'][:], in1=sb['oh1'][:], op=ALU.mult)
            return e.tensor_tensor(out=sb['tmpB'][:], in0=sb['vm'][:], in1=sb['oh2'][:], op=ALU.mult)
        seq.phase('vector', v3)
        def v4(e):
            e.tensor_reduce(out=sb['wt'][:, 4:5], in_=sb['tmpA'][:], axis=AX.X, op=ALU.add)
            return e.tensor_reduce(out=sb['wt'][:, 5:6], in_=sb['tmpB'][:], axis=AX.X, op=ALU.add)
        seq.phase('vector', v4)
        def v5(e):
            e.tensor_scalar(out=sb['tmpA'][:], in0=sb['oh1'][:], scalar1=sb['wt'][:, 4:5], scalar2=None, op0=ALU.mult)
            return e.tensor_scalar(out=sb['tmpB'][:], in0=sb['oh2'][:], scalar1=sb['wt'][:, 5:6], scalar2=None, op0=ALU.mult)
        seq.phase('vector', v5)
        seq.phase('vector', lambda e: e.tensor_tensor(out=sb['vindt'][:], in0=sb['tmpA'][:], in1=sb['tmpB'][:], op=ALU.add))
        def g2(e, rt_cur=rt_cur):
            e.matmul(psSL[:], sb['onesr'][:], sb[rt_cur][:], start=True, stop=False)
            return e.matmul(psSL[:], sb['tri'][:], sb['vindt'][:], start=False, stop=True)
        seq.phase('tensor', g2)
        seq.phase('tensor', lambda e: e.matmul(psR2[:], sb['onesf'][:], sb['vindt'][:], start=True, stop=True))
        def v7(e):
            e.tensor_copy(out=sb['slotsS'][:], in_=psSL[:])
            return e.tensor_copy(out=sb['rtS'][:], in_=psR2[:])
        seq.phase('vector', v7)
        seq.phase('vector', lambda e, rt_cur=rt_cur, rt_nxt=rt_nxt: e.tensor_tensor(out=sb[rt_nxt][:], in0=sb[rt_cur][:], in1=sb['rtS'][:], op=ALU.add))
        def v9a(e):
            e.tensor_tensor(out=sb['tmpC'][:], in0=sb['slotsS'][:], in1=sb['tmpA'][:], op=ALU.mult)
            return e.tensor_tensor(out=sb['tmpD'][:], in0=sb['slotsS'][:], in1=sb['tmpB'][:], op=ALU.mult)
        seq.phase('vector', v9a)
        def v9b(e):
            e.tensor_reduce(out=sb['wt'][:, 6:7], in_=sb['tmpC'][:], axis=AX.X, op=ALU.add)
            return e.tensor_reduce(out=sb['wt'][:, 7:8], in_=sb['tmpD'][:], axis=AX.X, op=ALU.add)
        seq.phase('vector', v9b)
        def v10(e):
            e.tensor_scalar(out=sb['wt'][:, 8:9], in0=sb['wt'][:, 6:7], scalar1=float(LCAP0), scalar2=None, op0=ALU.is_lt)
            return e.tensor_scalar(out=sb['wt'][:, 9:10], in0=sb['wt'][:, 7:8], scalar1=float(LCAP0), scalar2=None, op0=ALU.is_lt)
        seq.phase('vector', v10)
        def v11(e):
            e.tensor_tensor(out=sb['wt'][:, 10:11], in0=sb['wt'][:, 4:5], in1=sb['wt'][:, 8:9], op=ALU.mult)
            return e.tensor_tensor(out=sb['wt'][:, 11:12], in0=sb['wt'][:, 5:6], in1=sb['wt'][:, 9:10], op=ALU.mult)
        seq.phase('vector', v11)
        def v12(e):
            e.tensor_scalar(out=sb['wt'][:, 12:13], in0=sb['rt'][:, 0:1], scalar1=float(LCAP0), scalar2=sb['wt'][:, 6:7], op0=ALU.mult, op1=ALU.add)
            return e.tensor_scalar(out=sb['wt'][:, 13:14], in0=sb['rt'][:, 1:2], scalar1=float(LCAP0), scalar2=sb['wt'][:, 7:8], op0=ALU.mult, op1=ALU.add)
        seq.phase('vector', v12)
        def v13(e):
            e.tensor_tensor(out=sb['wt'][:, 14:15], in0=sb['wt'][:, 12:13], in1=sb['wt'][:, 10:11], op=ALU.mult)
            return e.tensor_tensor(out=sb['wt'][:, 15:16], in0=sb['wt'][:, 13:14], in1=sb['wt'][:, 11:12], op=ALU.mult)
        seq.phase('vector', v13)
        def v14(e):
            e.tensor_scalar(out=sb['wt'][:, 16:17], in0=sb['wt'][:, 10:11], scalar1=-float(ZSLOT0), scalar2=float(ZSLOT0), op0=ALU.mult, op1=ALU.add)
            return e.tensor_scalar(out=sb['wt'][:, 17:18], in0=sb['wt'][:, 11:12], scalar1=-float(ZSLOT0), scalar2=float(ZSLOT0), op0=ALU.mult, op1=ALU.add)
        seq.phase('vector', v14)
        def v15(e):
            e.tensor_tensor(out=sb['wt'][:, 18:19], in0=sb['wt'][:, 14:15], in1=sb['wt'][:, 16:17], op=ALU.add)
            return e.tensor_tensor(out=sb['wt'][:, 19:20], in0=sb['wt'][:, 15:16], in1=sb['wt'][:, 17:18], op=ALU.add)
        seq.phase('vector', v15)
        def v16(e):
            e.tensor_copy(out=sb['ixt1'][:], in_=sb['wt'][:, 18:19])
            return e.tensor_copy(out=sb['ixt2'][:], in_=sb['wt'][:, 19:20])
        seq.phase('vector', v16)
        def v17(e):
            e.tensor_tensor(out=sb['pvt'][:, 0:1], in0=sb['rt'][:, 2:3], in1=sb['wt'][:, 10:11], op=ALU.mult)
            return e.tensor_tensor(out=sb['pvt'][:, 1:2], in0=sb['rt'][:, 3:4], in1=sb['wt'][:, 11:12], op=ALU.mult)
        seq.phase('vector', v17)
        seq.phase_dma('sync', lambda e, j=j: [
            e.dma_start(out=A['dst1'][j*128:(j+1)*128, :], in_=sb['ixt1'][:]),
            e.dma_start(out=A['dst2'][j*128:(j+1)*128, :], in_=sb['ixt2'][:]),
            e.dma_start(out=A['pv'][j*128:(j+1)*128, :], in_=sb['pvt'][:]),
        ], 3)
        seq.phase_dma('gpsimd', lambda e: [
            e.indirect_dma_start(out=A['bucket0'], out_offset=bass.IndirectOffsetOnAxis(ap=sb['ixt1'][:], axis=0),
                                 in_=sb['lnt'][:], in_offset=None)], 1)
        seq.phase_dma('gpsimd', lambda e: [
            e.indirect_dma_start(out=A['bucket0'], out_offset=bass.IndirectOffsetOnAxis(ap=sb['ixt2'][:], axis=0),
                                 in_=sb['lnt'][:], in_offset=None)], 1)
    sec.end()


def emit_s9_moeffn(sec, A):
    seq = sec.begin(); sb = sec.sb; S, P = sec.S, sec.P
    S('w1', [128, 8, 2048], BF16)
    S('w2', [128, 16, 1024], BF16)
    S('xb', [128, 8, LCAP0], BF16)
    S('fc1', [128, 16, LCAP0], BF16)
    S('o2', [128, 2, 1024])
    S('zz', [1, 1024])
    psF = [P("psF0", [128, 512]), P("psF1", [128, 512])]
    psO2 = [P("psO20", [128, 1024]), P("psO21", [128, 1024])]
    seq.phase('vector', lambda e: e.memset(sb['zz'][:], 0.0))
    seq.phase_dma('sync', lambda e: [e.dma_start(out=A['fc2buf'][ZSLOT0:ZSLOT0+1, :], in_=sb['zz'][:])], 1)
    t_mm1_last = None
    t_mm2_last = None
    t_cp = {0: None, 1: None}
    t_st = {0: None, 1: None}
    t_gelu = {0: None, 1: None}
    for ex in range(8):
        t_w = seq.phase_dma('gpsimd', lambda e, ex=ex: [
            e.dma_start(out=sb['w1'][:], in_=A['w1'][ex].rearrange("(c p) f -> p c f", p=128)),
            e.dma_start(out=sb['w2'][:], in_=A['w2'][ex].rearrange("(c p) f -> p c f", p=128)),
        ], 2, deps=[t_mm1_last, t_mm2_last] if ex else None)
        t_xb = seq.phase_dma('sync', lambda e, ex=ex: [
            e.dma_start(out=sb['xb'][:, fc, :], in_=A['bucket0'][ex*LCAP0:(ex+1)*LCAP0, fc*128:(fc+1)*128],
                        transpose=True) for fc in range(8)
        ], 8, deps=[t_mm1_last] if ex else None)
        j = 0
        t_g_last = None
        for Fc in range(16):
            for sc in range(LCAP0 // 512):
                par = j % 2
                def mm(e, Fc=Fc, sc=sc, par=par):
                    last = None
                    for fc in range(8):
                        last = e.matmul(psF[par][:], sb['w1'][:, fc, Fc*128:(Fc+1)*128],
                                        sb['xb'][:, fc, sc*512:(sc+1)*512],
                                        start=(fc == 0), stop=(fc == 7))
                    return last
                t_mm1_last = seq.phase('tensor', mm, deps=[t_w, t_xb, t_gelu[par], t_mm2_last])
                t_gelu[par] = seq.phase('scalar', lambda e, Fc=Fc, sc=sc, par=par: e.activation(
                    out=sb['fc1'][:, Fc, sc*512:(sc+1)*512], in_=psF[par][:], func=AF.Gelu_apprx_tanh),
                    deps=[t_mm1_last, t_mm2_last])
                t_g_last = t_gelu[par]
                j += 1
        for sc2 in range(LCAP0 // 128):
            par = sc2 % 2
            def mm2(e, sc2=sc2, par=par):
                last = None
                for half in range(2):
                    for Fc in range(16):
                        last = e.matmul(psO2[par][:, half*512:(half+1)*512],
                                        sb['fc1'][:, Fc, sc2*128:(sc2+1)*128],
                                        sb['w2'][:, Fc, half*512:(half+1)*512],
                                        start=(Fc == 0), stop=(Fc == 15))
                return last
            t_mm2_last = seq.phase('tensor', mm2, deps=[t_g_last, t_cp[par]])
            t_cp[par] = seq.phase('vector', lambda e, par=par: e.tensor_copy(out=sb['o2'][:, par, :], in_=psO2[par][:]),
                                  deps=[t_mm2_last, t_st[par]])
            t_st[par] = seq.phase_dma('sync', lambda e, ex=ex, sc2=sc2, par=par: [
                e.dma_start(out=A['fc2buf'][ex*LCAP0+sc2*128: ex*LCAP0+(sc2+1)*128, :], in_=sb['o2'][:, par, :]),
            ], 1, deps=[t_cp[par]])
    sec.end()


def emit_s10_ret0(sec, A):
    seq = sec.begin(); sb = sec.sb; S, P = sec.S, sec.P
    S('ix1', [128, 1], I32); S('ix2', [128, 1], I32); S('pv', [128, 2])
    S('g1', [128, 1024]); S('g2', [128, 1024]); S('g2s', [128, 1024])
    S('x1t', [128, 1024]); S('h1t', [128, 1024])
    S('acc', [128, 1024]); S('acc2', [128, 1024]); S('moe', [128, 1024])
    x1_r = A['x1r'].rearrange("(p s) f -> s p f", s=16)
    h1_rr = A['h1'].rearrange("(p s) f -> s p f", s=16)
    for j in range(32):
        s_, pb_ = j // 2, j % 2
        seq.phase_dma('sync', lambda e, j=j, s_=s_, pb_=pb_: [
            e.dma_start(out=sb['ix1'][:], in_=A['dst1'][j*128:(j+1)*128, :]),
            e.dma_start(out=sb['ix2'][:], in_=A['dst2'][j*128:(j+1)*128, :]),
            e.dma_start(out=sb['pv'][:], in_=A['pv'][j*128:(j+1)*128, :]),
            e.dma_start(out=sb['x1t'][:], in_=x1_r[s_, pb_*128:(pb_+1)*128, :]),
            e.dma_start(out=sb['h1t'][:], in_=h1_rr[s_, pb_*128:(pb_+1)*128, :]),
        ], 5)
        seq.phase_dma('gpsimd', lambda e: [
            e.indirect_dma_start(out=sb['g1'][:], out_offset=None, in_=A['fc2buf'],
                                 in_offset=bass.IndirectOffsetOnAxis(ap=sb['ix1'][:], axis=0))], 1)
        seq.phase_dma('gpsimd', lambda e: [
            e.indirect_dma_start(out=sb['g2'][:], out_offset=None, in_=A['fc2buf'],
                                 in_offset=bass.IndirectOffsetOnAxis(ap=sb['ix2'][:], axis=0))], 1)
        seq.phase('vector', lambda e: e.tensor_scalar(out=sb['g2s'][:], in0=sb['g2'][:], scalar1=sb['pv'][:, 1:2], scalar2=None, op0=ALU.mult))
        seq.phase('vector', lambda e: e.scalar_tensor_tensor(out=sb['moe'][:], in0=sb['g1'][:], scalar=sb['pv'][:, 0:1], in1=sb['g2s'][:], op0=ALU.mult, op1=ALU.add))
        seq.phase('vector', lambda e: e.tensor_tensor(out=sb['acc'][:], in0=sb['x1t'][:], in1=sb['h1t'][:], op=ALU.add))
        seq.phase('vector', lambda e: e.tensor_tensor(out=sb['acc2'][:], in0=sb['acc'][:], in1=sb['moe'][:], op=ALU.add))
        seq.phase_dma('sync', lambda e, j=j: [
            e.dma_start(out=A['x2loc'][j*128:(j+1)*128, :], in_=sb['acc2'][:]),
        ], 1)
    sec.end()


def emit_s12_qkv2(sec, A):
    seq = sec.begin(); sb = sec.sb; S, P = sec.S, sec.P
    S('wq', [128, 8, 1024], BF16)
    S('wkv', [128, 8, 2048], BF16)
    S('idf', [128, 128]); S('idb', [128, 128], BF16); S('scal', [128, 4])
    S('xt', [128, 4, 1024]); S('ut', [128, 4, 1024])
    S('lnstat', [128, 32]); S('lnscr', [128, 4, 1024])
    S('ufm', [128, 8, 512], BF16)
    S('outb', [128, 8, 512], BF16)
    S('vfm', [128, 8, 512], BF16)
    S('vtmS', [128, 4, 1024], BF16)
    psA = P("psA", [128, 512]); psT = P("psT", [128, 512]); psC = P("psC", [128, 1024], BF16)
    seq.phase_dma('gpsimd', lambda e: [
        e.dma_start(out=sb['wq'][:], in_=A['Wq'].rearrange("(c p) o -> p c o", p=128)),
        e.dma_start(out=sb['wkv'][:], in_=A['Wkv'].rearrange("(c p) o -> p c o", p=128)),
    ], 2)
    seq.phase_dma('sync', lambda e: [
        e.dma_start(out=sb['idf'][:], in_=A['IDF']),
        e.dma_start(out=sb['idb'][:], in_=A['IDB']),
        e.dma_start(out=sb['scal'][:], in_=A['SCAL']),
    ], 3)
    for tc in range(8):
        seq.phase_dma('sync', lambda e, tc=tc: [
            e.dma_start(out=sb['xt'][:, i, :], in_=A['x2b'][tc*512+i*128: tc*512+(i+1)*128, :]) for i in range(4)
        ], 4)
        emit_ln(sec, [sb['xt'][:, i, :] for i in range(4)], [sb['ut'][:, i, :] for i in range(4)], 1024)
        for fc in range(8):
            def tr(e, fc=fc):
                last = None
                for i in range(4):
                    last = e.transpose(psT[:, i*128:(i+1)*128], sb['ut'][:, i, fc*128:(fc+1)*128], sb['idf'][:])
                return last
            seq.phase('tensor', tr)
            seq.phase('vector', lambda e, fc=fc: e.tensor_copy(out=sb['ufm'][:, fc, :], in_=psT[:]))
        for oc in range(8):
            def mm(e, oc=oc):
                last = None
                for fc in range(8):
                    last = e.matmul(psA[:], sb['wq'][:, fc, oc*128:(oc+1)*128], sb['ufm'][:, fc, :],
                                    start=(fc == 0), stop=(fc == 7))
                return last
            seq.phase('tensor', mm)
            seq.phase('vector', lambda e, oc=oc: e.tensor_scalar_mul(sb['outb'][:, oc, :], psA[:], 0.125))
        seq.phase_dma('sync', lambda e, tc=tc: [
            e.dma_start(out=A['q1f'][oc*128:(oc+1)*128, tc*512:(tc+1)*512], in_=sb['outb'][:, oc, :]) for oc in range(8)
        ], 8)
        for oc in range(8):
            def mm(e, oc=oc):
                last = None
                for fc in range(8):
                    last = e.matmul(psA[:], sb['wkv'][:, fc, oc*128:(oc+1)*128], sb['ufm'][:, fc, :],
                                    start=(fc == 0), stop=(fc == 7))
                return last
            seq.phase('tensor', mm)
            seq.phase('vector', lambda e, oc=oc: e.tensor_copy(out=sb['outb'][:, oc, :], in_=psA[:]))
        seq.phase_dma('sync', lambda e, tc=tc: [
            e.dma_start(out=A['k1f'][oc*128:(oc+1)*128, tc*512:(tc+1)*512], in_=sb['outb'][:, oc, :]) for oc in range(8)
        ], 8)
        for oc in range(8):
            def mm(e, oc=oc):
                last = None
                for fc in range(8):
                    last = e.matmul(psA[:], sb['wkv'][:, fc, 1024+oc*128:1024+(oc+1)*128], sb['ufm'][:, fc, :],
                                    start=(fc == 0), stop=(fc == 7))
                return last
            seq.phase('tensor', mm)
            seq.phase('vector', lambda e, oc=oc: e.tensor_copy(out=sb['vfm'][:, oc, :], in_=psA[:]))
        for i in range(4):
            def trv(e, i=i):
                last = None
                for oc in range(8):
                    last = e.transpose(psC[:, oc*128:(oc+1)*128], sb['vfm'][:, oc, i*128:(i+1)*128], sb['idb'][:])
                return last
            seq.phase('tensor', trv)
            seq.phase('vector', lambda e, i=i: e.tensor_copy(out=sb['vtmS'][:, i, :], in_=psC[:]))
        seq.phase_dma('sync', lambda e, tc=tc: [
            e.dma_start(out=A['v1tm'][tc*512+i*128: tc*512+(i+1)*128, :], in_=sb['vtmS'][:, i, :]) for i in range(4)
        ], 4)
    sec.end()


def emit_s13_att1(sec, A):
    seq = sec.begin(); sb = sec.sb; S, P = sec.S, sec.P
    S('vtm', [128, 32, 1024], BF16)
    S('wo', [128, 8, 1024], BF16)
    S('dmask', [128, 128]); S('idb', [128, 128], BF16)
    S('q', [64, 2, 2048], BF16); S('kg', [64, 2, 2048], BF16)
    S('sS', [128, 2048]); S('sP', [128, 2048], BF16); S('sPT', [128, 2048], BF16)
    S('nm', [128, 1]); S('sums', [128, 1]); S('rs', [128, 1])
    S('aoS', [128, 64], BF16)
    S('aofm', [128, 8, T], BF16)
    S('x2t', [128, 1024]); S('x3t', [128, 1024])
    psS = P("psS", [128, 2048]); psPT = P("psPT", [128, 1024], BF16)
    psO = P("psO", [128, 64]); psPRJ = P("psPRJ", [128, 1024])
    seq.phase_dma('sync', lambda e: [
        e.dma_start(out=sb['vtm'][:], in_=A['v1tm'].rearrange("(n p) f -> p n f", p=128)),
        e.dma_start(out=sb['dmask'][:], in_=A['DMASK']),
        e.dma_start(out=sb['idb'][:], in_=A['IDB']),
    ], 3)
    seq.phase_dma('gpsimd', lambda e: [
        e.dma_start(out=sb['wo'][:], in_=A['Wo'].rearrange("(c p) o -> p c o", p=128)),
    ], 1)

    iters = [(b, h, qt) for b in range(2) for h in range(16) for qt in range(16)]
    t_ld = {}
    t_lastqk = {}
    t = {'evsc': None, 'exp': None, 'nm': None, 'trp': None, 'cpy': None,
         'pv': None, 'rs': None, 'aoS': None, 'tr2': None, 'cp2': None}

    def emit_load(bh):
        b, h = bh
        par = h % 2
        dep = t_lastqk.get((b, h - 2) if h >= 2 else (b - 1, h + 14), None)
        t_ld[bh] = seq.phase_dma('sync', lambda e, h=h, b=b, par=par: [
            e.dma_start(out=sb['q'][:, par, :], in_=A['q1f'][h*64:(h+1)*64, b*2048:(b+1)*2048]),
            e.dma_start(out=sb['kg'][:, par, :], in_=A['k1f'][h*64:(h+1)*64, b*2048:(b+1)*2048]),
        ], 2, deps=[dep] if dep else None)

    def emit_qk(i):
        b, h, qt = iters[i]
        par = h % 2
        def qk(e, qt=qt, par=par):
            last = None
            for kc in range(qt + 1):
                last = e.matmul(psS[:, kc*128:(kc+1)*128],
                                sb['q'][:, par, qt*128:(qt+1)*128],
                                sb['kg'][:, par, kc*128:(kc+1)*128],
                                start=True, stop=True)
            return last
        tok = seq.phase('tensor', qk, deps=[t_ld[(b, h)], t['evsc']])
        t_lastqk[(b, h)] = tok
        return tok

    def emit_head(i, t_qk):
        b, h, qt = iters[i]
        nk = (qt + 1) * 128
        def evsc(e, qt=qt, nk=nk):
            if qt > 0:
                e.tensor_copy(out=sb['sS'][:, 0:qt*128], in_=psS[:, 0:qt*128])
            return e.scalar_tensor_tensor(out=sb['sS'][:, qt*128:nk], in0=psS[:, qt*128:nk],
                                          scalar=1.0, in1=sb['dmask'][:], op0=ALU.mult, op1=ALU.add)
        t['evsc'] = seq.phase('vector', evsc, deps=[t_qk, t['exp'], t['nm']])

    def emit_body(i):
        b, h, qt = iters[i]
        nk = (qt + 1) * 128
        t['nm'] = seq.phase('vector', lambda e, nk=nk: e.tensor_reduce(out=sb['nm'][:], in_=sb['sS'][:, 0:nk], axis=AX.X, op=ALU.max, negate=True),
                            deps=[t['evsc'], t['exp']])
        t['exp'] = seq.phase('scalar', lambda e, nk=nk: e.activation(out=sb['sP'][:, 0:nk], in_=sb['sS'][:, 0:nk], func=AF.Exp,
                                                                     bias=sb['nm'][:, 0:1], accum_out=sb['sums'][:, 0:1]),
                             deps=[t['nm'], t['trp'], t['rs']])
        tprev_cpy = t['cpy']
        for half in range(2):
            kcs = list(range(half*8, min(qt+1, half*8+8)))
            if len(kcs) == 0:
                continue
            def trp(e, kcs=kcs, half=half):
                last = None
                for kc in kcs:
                    last = e.transpose(psPT[:, (kc-half*8)*128:(kc-half*8+1)*128],
                                       sb['sP'][:, kc*128:(kc+1)*128], sb['idb'][:])
                return last
            t['trp'] = seq.phase('tensor', trp, deps=[t['exp'], t['cpy'], t['cp2']])
            t['cpy'] = seq.phase('vector', lambda e, kcs=kcs, half=half: e.tensor_copy(
                out=sb['sPT'][:, kcs[0]*128: (kcs[-1]+1)*128],
                in_=psPT[:, (kcs[0]-half*8)*128:(kcs[-1]-half*8+1)*128]),
                deps=[t['trp'], t['pv']])
        def pv(e, b=b, h=h, qt=qt):
            last = None
            for kc in range(qt + 1):
                ktile = b*16 + kc
                last = e.matmul(psO[:], sb['sPT'][:, kc*128:(kc+1)*128],
                                sb['vtm'][:, ktile, h*64:(h+1)*64],
                                start=(kc == 0), stop=(kc == qt))
            return last
        t['pv'] = seq.phase('tensor', pv, deps=[t['cpy'], t['aoS']])
        t['rs'] = seq.phase('vector', lambda e: e.reciprocal(sb['rs'][:], sb['sums'][:]),
                            deps=[t['exp'], t['aoS']])
        t['aoS'] = seq.phase('vector', lambda e: e.tensor_scalar(out=sb['aoS'][:], in0=psO[:], scalar1=sb['rs'][:, 0:1], scalar2=None, op0=ALU.mult),
                             deps=[t['pv'], t['rs'], t['tr2']])
        t['tr2'] = seq.phase('tensor', lambda e: e.transpose(psPT[0:64, 0:128], sb['aoS'][:], sb['idb'][:]),
                             deps=[t['aoS'], t['cpy']])
        t['cp2'] = seq.phase('vector', lambda e, b=b, h=h, qt=qt: e.tensor_copy(
            out=sb['aofm'][(h % 2)*64:(h % 2)*64+64, h//2, b*2048+qt*128: b*2048+(qt+1)*128],
            in_=psPT[0:64, 0:128]), deps=[t['tr2']])

    emit_load((0, 0))
    emit_load((0, 1))
    pend = emit_qk(0)
    for i in range(len(iters)):
        emit_head(i, pend)
        if i + 1 < len(iters):
            bn, hn, qn = iters[i+1]
            if qn == 0:
                nb, nh = (bn, hn + 1) if hn < 15 else (bn + 1, 0)
                if nb < 2:
                    emit_load((nb, nh))
            pend = emit_qk(i + 1)
        emit_body(i)
    for tt in range(32):
        tx = seq.phase_dma('sync', lambda e, tt=tt: [e.dma_start(out=sb['x2t'][:], in_=A['x2b'][tt*128:(tt+1)*128, :])], 1)
        def prj(e, tt=tt):
            last = None
            for half in range(2):
                for fc in range(8):
                    last = e.matmul(psPRJ[:, half*512:(half+1)*512],
                                    sb['aofm'][:, fc, tt*128:(tt+1)*128],
                                    sb['wo'][:, fc, half*512:(half+1)*512],
                                    start=(fc == 0), stop=(fc == 7))
            return last
        seq.phase('tensor', prj)
        seq.phase('vector', lambda e: e.tensor_tensor(out=sb['x3t'][:], in0=sb['x2t'][:], in1=psPRJ[:], op=ALU.add))
        seq.phase_dma('sync', lambda e, tt=tt: [e.dma_start(out=A['x3'][tt*128:(tt+1)*128, :], in_=sb['x3t'][:])], 1)
    sec.end()


def emit_s14_ffn1a(sec, A):
    seq = sec.begin(); sb = sec.sb; S, P = sec.S, sec.P
    S('centT', [128, 8, 8], BF16)
    S('tri', [128, 128]); S('iota8', [128, 8]); S('onesf', [128, 1]); S('onesr', [1, 128])
    S('idf', [128, 128]); S('scal', [128, 4])
    S('xt', [128, 1, 1024]); S('ut', [128, 1, 1024]); S('nt', [128, 1, 1024])
    S('ntb', [128, 1024], BF16)
    S('lnstat', [128, 32]); S('lnscr', [128, 1, 1024])
    S('ufm', [128, 8, 128], BF16)
    S('af', [128, 8]); S('mx', [128, 8]); S('oh', [128, 8]); S('ec', [128, 8])
    S('slotsS', [128, 8]); S('tmpC', [128, 8])
    S('wt', [128, 16])
    S('ix', [128, 1], I32)
    S('rtA', [1, 8]); S('rtB', [1, 8]); S('rtS', [1, 8])
    psT = P("psT", [128, 512]); psAF = P("psAF", [128, 8])
    psSL = P("psSL", [128, 8]); psR2 = P("psR2", [1, 8])
    seq.phase_dma('gpsimd', lambda e: [
        e.dma_start(out=sb['centT'][:], in_=A['centT'].rearrange("(c p) o -> p c o", p=128)),
    ], 1)
    seq.phase_dma('sync', lambda e: [
        e.dma_start(out=sb['tri'][:], in_=A['TRI128']),
        e.dma_start(out=sb['iota8'][:], in_=A['IOTA8']),
        e.dma_start(out=sb['onesf'][:], in_=A['ONESF']),
        e.dma_start(out=sb['onesr'][:], in_=A['ONESR']),
        e.dma_start(out=sb['idf'][:], in_=A['IDF']),
        e.dma_start(out=sb['scal'][:], in_=A['SCAL']),
    ], 6)
    seq.phase('vector', lambda e: e.memset(sb['rtA'][:], 0.0))
    for j in range(32):
        cur_rt = 'rtA' if j % 2 == 0 else 'rtB'
        nxt_rt = 'rtB' if j % 2 == 0 else 'rtA'
        seq.phase_dma('sync', lambda e, j=j: [
            e.dma_start(out=sb['xt'][:, 0, :], in_=A['x3'][j*128:(j+1)*128, :]),
        ], 1)
        emit_ln(sec, [sb['xt'][:, 0, :]], [sb['ut'][:, 0, :]], 1024)
        seq.phase_dma('sync', lambda e, j=j: [
            e.dma_start(out=A['u3'][j*128:(j+1)*128, :], in_=sb['ut'][:, 0, :]),
        ], 1)
        emit_ln(sec, [sb['ut'][:, 0, :]], [sb['nt'][:, 0, :]], 1024)
        seq.phase('vector', lambda e: e.tensor_copy(out=sb['ntb'][:], in_=sb['nt'][:, 0, :]))
        for fc in range(8):
            seq.phase('tensor', lambda e, fc=fc: e.transpose(psT[:, 0:128], sb['ut'][:, 0, fc*128:(fc+1)*128], sb['idf'][:]))
            seq.phase('vector', lambda e, fc=fc: e.tensor_copy(out=sb['ufm'][:, fc, :], in_=psT[:, 0:128]))
        def afm(e):
            last = None
            for fc in range(8):
                last = e.matmul(psAF[:], sb['ufm'][:, fc, :], sb['centT'][:, fc, :],
                                start=(fc == 0), stop=(fc == 7))
            return last
        seq.phase('tensor', afm)
        seq.phase('vector', lambda e: e.tensor_copy(out=sb['af'][:], in_=psAF[:]))
        seq.phase('vector', lambda e: e.max(out=sb['mx'][:], in_=sb['af'][:]))
        seq.phase('scalar', lambda e: e.activation(out=sb['wt'][:, 0:1], in_=sb['mx'][:, 0:1], func=AF.Sigmoid))
        seq.phase('vector', lambda e: e.tensor_scalar(out=sb['oh'][:], in0=sb['af'][:], scalar1=sb['mx'][:, 0:1], scalar2=None, op0=ALU.is_equal))
        seq.phase('vector', lambda e: e.tensor_tensor(out=sb['ec'][:], in0=sb['oh'][:], in1=sb['iota8'][:], op=ALU.mult))
        seq.phase('vector', lambda e: e.tensor_reduce(out=sb['wt'][:, 1:2], in_=sb['ec'][:], axis=AX.X, op=ALU.add))
        def g2(e, cur_rt=cur_rt):
            e.matmul(psSL[:], sb['onesr'][:], sb[cur_rt][:], start=True, stop=False)
            return e.matmul(psSL[:], sb['tri'][:], sb['oh'][:], start=False, stop=True)
        seq.phase('tensor', g2)
        seq.phase('tensor', lambda e: e.matmul(psR2[:], sb['onesf'][:], sb['oh'][:], start=True, stop=True))
        def v7(e):
            e.tensor_copy(out=sb['slotsS'][:], in_=psSL[:])
            return e.tensor_copy(out=sb['rtS'][:], in_=psR2[:])
        seq.phase('vector', v7)
        seq.phase('vector', lambda e, cur_rt=cur_rt, nxt_rt=nxt_rt: e.tensor_tensor(out=sb[nxt_rt][:], in0=sb[cur_rt][:], in1=sb['rtS'][:], op=ALU.add))
        seq.phase('vector', lambda e: e.tensor_tensor(out=sb['tmpC'][:], in0=sb['slotsS'][:], in1=sb['oh'][:], op=ALU.mult))
        seq.phase('vector', lambda e: e.tensor_reduce(out=sb['wt'][:, 2:3], in_=sb['tmpC'][:], axis=AX.X, op=ALU.add))
        seq.phase('vector', lambda e: e.tensor_scalar(out=sb['wt'][:, 3:4], in0=sb['wt'][:, 2:3], scalar1=float(LCAP1), scalar2=None, op0=ALU.is_lt))
        seq.phase('vector', lambda e: e.tensor_scalar(out=sb['wt'][:, 4:5], in0=sb['wt'][:, 1:2], scalar1=float(LCAP1), scalar2=sb['wt'][:, 2:3], op0=ALU.mult, op1=ALU.add))
        seq.phase('vector', lambda e: e.tensor_tensor(out=sb['wt'][:, 5:6], in0=sb['wt'][:, 4:5], in1=sb['wt'][:, 3:4], op=ALU.mult))
        seq.phase('vector', lambda e: e.tensor_scalar(out=sb['wt'][:, 6:7], in0=sb['wt'][:, 3:4], scalar1=-float(ZSLOT1), scalar2=float(ZSLOT1), op0=ALU.mult, op1=ALU.add))
        seq.phase('vector', lambda e: e.tensor_tensor(out=sb['wt'][:, 7:8], in0=sb['wt'][:, 5:6], in1=sb['wt'][:, 6:7], op=ALU.add))
        seq.phase('vector', lambda e: e.tensor_copy(out=sb['ix'][:], in_=sb['wt'][:, 7:8]))
        seq.phase_dma('sync', lambda e, j=j: [
            e.dma_start(out=A['alpha'][j*128:(j+1)*128, :], in_=sb['wt'][:, 0:1]),
            e.dma_start(out=A['dsti1'][j*128:(j+1)*128, :], in_=sb['ix'][:]),
        ], 2)
        seq.phase_dma('gpsimd', lambda e: [
            e.indirect_dma_start(out=A['bucket1'], out_offset=bass.IndirectOffsetOnAxis(ap=sb['ix'][:, 0:1], axis=0),
                                 in_=sb['ntb'][:], in_offset=None)], 1)
    sec.end()


def emit_s15_ffn1b(sec, A):
    seq = sec.begin(); sb = sec.sb; S, P = sec.S, sec.P
    S('w1', [128, 8, 4096], BF16)
    S('w2', [128, 32, 1024], BF16)
    S('lng', [128, 8]); S('lnb', [128, 8]); S('b1c', [128, 32]); S('b2r', [128, 1024])
    S('xbe', [128, 8, LCAP1], BF16)
    S('xg', [128, 2, 8, 256], BF16)
    S('fc1', [128, 32, 256], BF16)
    S('o2', [128, 2, 1024])
    S('zz', [1, 1024])
    psF = [P("psF0", [128, 256]), P("psF1", [128, 256])]
    psO2 = [P("psO20", [128, 1024]), P("psO21", [128, 1024])]
    seq.phase('vector', lambda e: e.memset(sb['zz'][:], 0.0))
    seq.phase_dma('sync', lambda e: [e.dma_start(out=A['fc2buf1'][ZSLOT1:ZSLOT1+1, :], in_=sb['zz'][:])], 1)
    t_mm1_last = None
    t_mm2_last = None
    t_cp = {0: None, 1: None}
    t_st = {0: None, 1: None}
    t_relu = {0: None, 1: None}
    t_gb = {0: None, 1: None}
    for ex in range(8):
        t_w = seq.phase_dma('gpsimd', lambda e, ex=ex: [
            e.dma_start(out=sb['w1'][:], in_=A['w1T'][ex].rearrange("(c p) f -> p c f", p=128)),
            e.dma_start(out=sb['w2'][:], in_=A['w2T'][ex].rearrange("(c p) f -> p c f", p=128)),
        ], 2, deps=[t_mm1_last, t_mm2_last] if ex else None)
        t_c = seq.phase_dma('sync', lambda e, ex=ex: [
            e.dma_start(out=sb['lng'][:], in_=A['LNG'][ex]),
            e.dma_start(out=sb['lnb'][:], in_=A['LNB'][ex]),
            e.dma_start(out=sb['b1c'][:], in_=A['B1C'][ex]),
            e.dma_start(out=sb['b2r'][:], in_=A['B2REP1'][ex]),
        ], 4, deps=[t_gb[0], t_gb[1], t_relu[0], t_relu[1], t_cp[0], t_cp[1], t_mm2_last] if ex else None)
        t_xb = seq.phase_dma('sync', lambda e, ex=ex: [
            e.dma_start(out=sb['xbe'][:, fc, :], in_=A['bucket1'][ex*LCAP1:(ex+1)*LCAP1, fc*128:(fc+1)*128],
                        transpose=True) for fc in range(8)
        ], 8, deps=[t_gb[0], t_gb[1]] if ex else None)
        for sc in range(LCAP1 // 256):
            scp = sc % 2
            def gb(e, sc=sc, scp=scp):
                last = None
                for fc in range(8):
                    last = e.tensor_scalar(out=sb['xg'][:, scp, fc, :], in0=sb['xbe'][:, fc, sc*256:(sc+1)*256],
                                           scalar1=sb['lng'][:, fc:fc+1], scalar2=sb['lnb'][:, fc:fc+1],
                                           op0=ALU.mult, op1=ALU.add)
                return last
            t_gb[scp] = seq.phase('vector', gb, deps=[t_c, t_xb, t_mm1_last])
            t_r_last = None
            for Fc in range(32):
                par = Fc % 2
                def mm(e, Fc=Fc, par=par, scp=scp):
                    last = None
                    for fc in range(8):
                        last = e.matmul(psF[par][:], sb['w1'][:, fc, Fc*128:(Fc+1)*128],
                                        sb['xg'][:, scp, fc, :],
                                        start=(fc == 0), stop=(fc == 7))
                    return last
                t_mm1_last = seq.phase('tensor', mm, deps=[t_w, t_gb[scp], t_relu[par], t_mm2_last])
                t_relu[par] = seq.phase('scalar', lambda e, Fc=Fc, par=par: e.activation(
                    out=sb['fc1'][:, Fc, :], in_=psF[par][:], func=AF.Relu, bias=sb['b1c'][:, Fc:Fc+1]),
                    deps=[t_mm1_last, t_mm2_last])
                t_r_last = t_relu[par]
            for sc2 in range(2):
                par = sc2 % 2
                def mm2(e, sc2=sc2, par=par):
                    last = None
                    for half in range(2):
                        for Fc in range(32):
                            last = e.matmul(psO2[par][:, half*512:(half+1)*512],
                                            sb['fc1'][:, Fc, sc2*128:(sc2+1)*128],
                                            sb['w2'][:, Fc, half*512:(half+1)*512],
                                            start=(Fc == 0), stop=(Fc == 31))
                    return last
                t_mm2_last = seq.phase('tensor', mm2, deps=[t_r_last, t_cp[par]])
                t_cp[par] = seq.phase('vector', lambda e, par=par: e.scalar_tensor_tensor(
                    out=sb['o2'][:, par, :], in0=psO2[par][:], scalar=1.0, in1=sb['b2r'][:],
                    op0=ALU.mult, op1=ALU.add), deps=[t_mm2_last, t_c, t_st[par]])
                t_st[par] = seq.phase_dma('sync', lambda e, ex=ex, sc=sc, sc2=sc2, par=par: [
                    e.dma_start(out=A['fc2buf1'][ex*LCAP1+sc*256+sc2*128: ex*LCAP1+sc*256+(sc2+1)*128, :], in_=sb['o2'][:, par, :]),
                ], 1, deps=[t_cp[par]])
    sec.end()


def emit_s16_ret1(sec, A):
    seq = sec.begin(); sb = sec.sb; S, P = sec.S, sec.P
    S('ix', [128, 1], I32); S('al', [128, 1])
    S('g1', [128, 1024]); S('gs', [128, 1024])
    S('x3t', [128, 1024]); S('u3t', [128, 1024])
    S('a1', [128, 1024]); S('a2', [128, 1024])
    for j in range(32):
        seq.phase_dma('sync', lambda e, j=j: [
            e.dma_start(out=sb['ix'][:], in_=A['dsti1'][j*128:(j+1)*128, :]),
            e.dma_start(out=sb['al'][:], in_=A['alpha'][j*128:(j+1)*128, :]),
            e.dma_start(out=sb['x3t'][:], in_=A['x3'][j*128:(j+1)*128, :]),
            e.dma_start(out=sb['u3t'][:], in_=A['u3'][j*128:(j+1)*128, :]),
        ], 4)
        seq.phase_dma('gpsimd', lambda e: [
            e.indirect_dma_start(out=sb['g1'][:], out_offset=None, in_=A['fc2buf1'],
                                 in_offset=bass.IndirectOffsetOnAxis(ap=sb['ix'][:, 0:1], axis=0))], 1)
        seq.phase('vector', lambda e: e.tensor_scalar(out=sb['gs'][:], in0=sb['g1'][:], scalar1=sb['al'][:, 0:1], scalar2=None, op0=ALU.mult))
        seq.phase('vector', lambda e: e.tensor_tensor(out=sb['a1'][:], in0=sb['x3t'][:], in1=sb['u3t'][:], op=ALU.add))
        seq.phase('vector', lambda e: e.tensor_tensor(out=sb['a2'][:], in0=sb['a1'][:], in1=sb['gs'][:], op=ALU.add))
        seq.phase_dma('sync', lambda e, j=j: [
            e.dma_start(out=A['out'][j*128:(j+1)*128, :], in_=sb['a2'][:]),
        ], 1)
    sec.end()



def build_fused(upto=99, dbg=(), lo=0, ext_in=()):
    nc = bass.Bass("TRN2", target_bir_lowering=False, debug=False)
    Section.counter[0] = 0
    A = {}
    ins_spec = [
        ("x0", [T, D], F32), ("wqT", [D, D], F32), ("wkT", [D, 256], F32), ("wvT", [D, 256], F32),
        ("qnw", [128, 8], F32), ("knw", [128, 2], F32), ("woT", [D, D], F32),
        ("qkvT", [D, 1536], F32), ("W1REP", [128, D], F32), ("B1REP", [128, D], F32),
        ("prjT", [D, D], F32), ("W2REP", [128, D], F32), ("B2REP", [128, D], F32),
        ("rw", [D, 8], F32), ("w1", [8, D, 2048], F32), ("w2", [8, 2048, D], F32),
        ("Wq", [D, D], F32), ("Wkv", [D, 2*D], F32), ("Wo", [D, D], F32),
        ("centT", [D, 8], F32), ("w1T", [8, D, 4096], F32), ("w2T", [8, 4096, D], F32),
        ("LNG", [8, 128, 8], F32), ("LNB", [8, 128, 8], F32), ("B1C", [8, 128, 32], F32),
        ("B2REP1", [8, 128, D], F32),
        ("IDB", [128, 128], BF16), ("IDF", [128, 128], F32), ("IDB64", [128, 64], BF16),
        ("ONESF", [128, 1], F32), ("ONESR", [1, 128], F32),
        ("RELB", [16, 128, 640], BF16), ("DMASK", [128, 128], F32),
        ("IOTA8", [128, 8], F32), ("TRI128", [128, 128], F32), ("TRI16", [16, 16], F32),
        ("ALLONES", [128, 128], F32), ("SROWSEL", [128, 16], F32), ("CSEL", [128, 16], F32),
        ("SELBANK", [16, 16*128], F32), ("SCAL", [128, 4], F32),
        ("CORETRI", [128, 16], F32), ("CID", [128, 1], F32),
        ("TMASKM", [128, 512], BF16), ("IND16", [128, 8], BF16), ("IND16T", [8, 128], F32),
        ("RESHC", [128, 32], F32), ("RESHC2", [128, 32], F32),
    ]
    for nm, shp, dt in ins_spec:
        A[nm] = nc.dram_tensor(nm, shp, dt, kind="ExternalInput").ap()
    mid_spec = [
        ("qf", [D, T], BF16), ("kf", [256, T], BF16), ("vtm", [T, 256], BF16),
        ("x1loc", [T, D], F32), ("x1g", [8*T, D], F32), ("x1r", [T, D], F32),
        ("u1", [T, D], F32), ("qtf", [D, T], BF16), ("ktf", [256, T], BF16), ("vtf", [256, T], BF16),
        ("aofm_t", [D, T], BF16),
        ("h1", [T, D], F32), ("ln2tm", [T, D], BF16), ("ind", [T, 8], F32), ("route", [T, 4], F32),
        ("Rloc", [16, 8], F32), ("Rg", [128, 8], F32),
        ("dst1", [T, 1], I32), ("dst2", [T, 1], I32), ("pv", [T, 2], F32),
        ("bucket0", [8*LCAP0+1, D], BF16), ("fc2buf", [8*LCAP0+1, D], F32),
        ("x2loc", [T, D], F32), ("x2g", [8*T, D], F32), ("x2b", [T, D], F32),
        ("q1f", [D, T], BF16), ("k1f", [D, T], BF16), ("v1tm", [T, D], BF16),
        ("x3", [T, D], F32), ("u3", [T, D], F32), ("alpha", [T, 1], F32), ("dsti1", [T, 1], I32),
        ("bucket1", [8*LCAP1+1, D], BF16), ("fc2buf1", [8*LCAP1+1, D], F32),
    ]
    for nm, shp, dt in mid_spec:
        kind = "ExternalOutput" if nm in dbg else ("ExternalInput" if nm in ext_in else "Internal")
        A[nm] = nc.dram_tensor(nm, shp, dt, kind=kind).ap()
    A['out'] = nc.dram_tensor("out", [T, D], F32, kind="ExternalOutput").ap()

    gctx = ExitStack()
    sems = {k: gctx.enter_context(nc.semaphore(f"G{k}")) for k in ('g', 'h', 'cc', 't', 'v', 's', 'p', 'y')}
    Section.gseq[0] = Seq(nc, sems)
    nc._keepalive = gctx
    sections = [
        lambda sec: emit_s1_qkv0(sec, A),
        lambda sec: emit_s2_att0(sec, A),
        lambda sec: emit_reshard(sec, A, 'x1loc', 'x1g', 'x1r', 'RESHC', 256),
        lambda sec: emit_s4_ffn0a(sec, A),
        lambda sec: emit_s5_tiny(sec, A),
        lambda sec: emit_s6_ffn0b(sec, A),
        lambda sec: emit_s7_rgather(sec, A),
        lambda sec: emit_s8_disp(sec, A),
        lambda sec: emit_s9_moeffn(sec, A),
        lambda sec: emit_s10_ret0(sec, A),
        lambda sec: emit_reshard(sec, A, 'x2loc', 'x2g', 'x2b', 'RESHC2', 512),
        lambda sec: emit_s12_qkv2(sec, A),
        lambda sec: emit_s13_att1(sec, A),
        lambda sec: emit_s14_ffn1a(sec, A),
        lambda sec: emit_s15_ffn1b(sec, A),
        lambda sec: emit_s16_ret1(sec, A),
    ]
    nsec = min(upto, len(sections))
    for i, emit in enumerate(sections):
        if i >= upto or i < lo:
            continue
        sec = Section(nc)
        sec.final = (i == nsec - 1)
        emit(sec)
    return nc


def kernel(**inputs):
    nc = build_fused()
    in_maps = [host_inputs(inputs, c) for c in range(8)]
    res = run_bass_kernel_spmd(nc, in_maps, core_ids=list(range(8)))
    out = np.empty((16, 2048, D), np.float32)
    for c in range(8):
        out[2*c:2*c+2] = np.asarray(res.results[c]["out"]).reshape(2, 2048, D)
    return out


# version 04
# speedup vs baseline: 1.2230x; 1.0638x over previous
import sys
if '/opt/trn_rl_repo' not in sys.path:
    sys.path.insert(0, '/opt/trn_rl_repo')
import numpy as np
import ml_dtypes
import concourse.bass as bass
import concourse.mybir as mybir
from concourse.bass_utils import run_bass_kernel_spmd
from contextlib import ExitStack

F32 = mybir.dt.float32
BF16 = mybir.dt.bfloat16
I32 = mybir.dt.int32
AF = mybir.ActivationFunctionType
ALU = mybir.AluOpType
AX = mybir.AxisListType
BF = ml_dtypes.bfloat16

T = 4096
D = 1024
NEG = -1.0e9
LCAP0 = 1536
CAP0 = 10240
ZSLOT0 = 8 * LCAP0
LCAP1 = 1024
ZSLOT1 = 8 * LCAP1


class Seq:
    ENG2K = {'tensor': 't', 'vector': 'v', 'scalar': 's', 'gpsimd': 'p', 'sync': 'y'}

    def __init__(self, nc, sems):
        self.nc = nc
        self.sems = sems
        self.vals = {k: 0 for k in sems}
        self.streams = {}

    def _now(self):
        return dict(self.vals)

    def _fold(self, deps):
        if deps is None:
            return self._now()
        w = {}
        for d in deps:
            if d is None:
                continue
            k, v = d
            w[k] = max(w.get(k, 0), v)
        return w

    def _waits(self, e, w):
        for k, v in w.items():
            if v:
                e.wait_ge(self.sems[k], v)

    def phase(self, eng, fn, deps=None):
        w = self._fold(deps)
        k = Seq.ENG2K[eng]
        sem = self.sems[k]
        def emit(e, w=w, fn=fn, sem=sem):
            self._waits(e, w)
            last = fn(e)
            last.then_inc(sem, 1)
        self.streams.setdefault(eng, []).append(emit)
        self.vals[k] += 1
        return (k, self.vals[k])

    def phase_dma(self, eng, fn, n, deps=None):
        w = self._fold(deps)
        k = 'g' if eng == 'gpsimd' else 'h'
        w[k] = max(w.get(k, 0), self.vals[k])
        sem = self.sems[k]
        def emit(e, w=w, fn=fn, n=n, sem=sem):
            self._waits(e, w)
            insts = fn(e)
            assert len(insts) == n, f"dma phase expected {n} got {len(insts)}"
            for i in insts:
                i.then_inc(sem, 16)
        self.streams.setdefault(eng, []).append(emit)
        self.vals[k] += 16 * n
        return (k, self.vals[k])

    def phase_cc(self, fn, deps=None):
        w = self._fold(deps)
        sem = self.sems['cc']
        def emit(e, w=w, fn=fn, sem=sem):
            self._waits(e, w)
            fn(e).then_inc(sem, 1)
        self.streams.setdefault('gpsimd', []).append(emit)
        self.vals['cc'] += 1
        return ('cc', self.vals['cc'])

    def flush(self, block, final=False):
        fin = self._now()
        streams = self.streams
        for eng in ('sync', 'tensor', 'vector', 'scalar', 'gpsimd'):
            def body(e, eng=eng):
                for emit in streams.get(eng, []):
                    emit(e)
                if final and eng == 'sync':
                    self._waits(e, fin)
            getattr(block, eng)(body)
        self.streams = {}


class Section:
    counter = [0]
    gseq = [None]

    def __init__(self, nc):
        self.nc = nc
        self.ctx = ExitStack()
        self.sb = {}
        Section.counter[0] += 1
        self.uid = Section.counter[0]
        self.final = False

    def S(self, name, shape, dt=F32):
        t = self.ctx.enter_context(self.nc.sbuf_tensor(f"s{self.uid}_{name}", shape, dt))
        self.sb[name] = t
        return t

    def P(self, name, shape, dt=F32):
        t = self.ctx.enter_context(self.nc.psum_tensor(f"s{self.uid}_{name}", shape, dt))
        return t

    def begin(self):
        self.blk = self.ctx.enter_context(self.nc.Block())
        self.seq = Section.gseq[0]
        return self.seq

    def end(self):
        self.seq.flush(self.blk, final=self.final)
        self.ctx.close()


def emit_ln(sec, x_tiles, out_tiles, width, eps=1e-5):
    seq, sb = sec.seq, sec.sb
    nt = len(x_tiles)
    st = sb['lnstat']
    assert st.shape[1] >= 8*nt
    def vsum(e):
        last = None
        for i in range(nt):
            last = e.tensor_reduce(out=st[:, i:i+1], in_=x_tiles[i], axis=AX.X, op=ALU.add)
        return last
    seq.phase('vector', vsum)
    def ssq(e):
        last = None
        for i in range(nt):
            last = e.activation(out=sb['lnscr'][:, i, 0:width], in_=x_tiles[i], func=AF.Square,
                                accum_out=st[:, nt+i:nt+i+1])
        return last
    seq.phase('scalar', ssq)
    def v1(e):
        e.tensor_scalar_mul(st[:, 2*nt:3*nt], st[:, 0:nt], -1.0/width)
        return e.tensor_scalar_mul(st[:, 3*nt:4*nt], st[:, nt:2*nt], 1.0/width)
    seq.phase('vector', v1)
    seq.phase('vector', lambda e: e.tensor_tensor(out=st[:, 4*nt:5*nt], in0=st[:, 2*nt:3*nt], in1=st[:, 2*nt:3*nt], op=ALU.mult))
    seq.phase('vector', lambda e: e.tensor_tensor(out=st[:, 5*nt:6*nt], in0=st[:, 3*nt:4*nt], in1=st[:, 4*nt:5*nt], op=ALU.subtract))
    ec = 0 if eps == 1e-5 else 1
    seq.phase('scalar', lambda e: e.activation(out=st[:, 6*nt:7*nt], in_=st[:, 5*nt:6*nt], func=AF.Sqrt, bias=sb['scal'][:, ec:ec+1]))
    seq.phase('vector', lambda e: e.reciprocal(st[:, 7*nt:8*nt], st[:, 6*nt:7*nt]))
    def vapply(e):
        last = None
        for i in range(nt):
            last = e.tensor_scalar(out=out_tiles[i], in0=x_tiles[i],
                                   scalar1=st[:, 2*nt+i:2*nt+i+1], scalar2=st[:, 7*nt+i:7*nt+i+1],
                                   op0=ALU.add, op1=ALU.mult)
        return last
    seq.phase('vector', vapply)



def host_consts(core_id):
    C = {}
    C['IDB'] = np.eye(128, dtype=BF)
    C['IDF'] = np.eye(128, dtype=np.float32)
    C['ONESF'] = np.ones((128, 1), np.float32)
    C['ONESR'] = np.ones((1, 128), np.float32)
    NH, WINDOW = 16, 512
    slopes = np.array([(2.0**(-8.0/NH))**(h+1) for h in range(NH)], np.float32)
    i = np.arange(128)[:, None]; j = np.arange(640)[None, :]
    rel = (j - i - 512).astype(np.float32)
    valid = (rel <= 0) & (rel >= -WINDOW)
    C['RELB'] = np.where(valid[None], slopes[:, None, None]*rel[None], np.float32(NEG)).astype(BF)
    ii = np.arange(128)[:, None]; jj = np.arange(128)[None, :]
    C['DMASK'] = np.where(jj <= ii, 0.0, NEG).astype(np.float32)
    C['IOTA8'] = np.tile(np.arange(8, dtype=np.float32)[None, :], (128, 1))
    C['TRI128'] = (np.arange(128)[:, None] < np.arange(128)[None, :]).astype(np.float32)
    C['TRI16'] = (np.arange(16)[:, None] < np.arange(16)[None, :]).astype(np.float32)
    C['ALLONES'] = np.ones((128, 128), np.float32)
    C['SROWSEL'] = (np.arange(128)[:, None] % 16 == np.arange(16)[None, :]).astype(np.float32)
    C['CSEL'] = C['SROWSEL'].copy()
    sel = np.zeros((16, 16*128), np.float32)
    for s_ in range(16):
        sel[s_, s_*128:(s_+1)*128] = 1.0
    C['SELBANK'] = sel
    C['SCAL'] = np.tile(np.array([[1e-5, 1e-6, 0.0, 1.0]], np.float32), (128, 1))
    r = np.arange(128)
    C['CORETRI'] = (((r // 16)[:, None] < core_id) &
                    (r[:, None] % 16 == np.arange(16)[None, :])).astype(np.float32)
    C['CID'] = np.full((128, 1), float(core_id), np.float32)
    C['IDB64'] = np.vstack([np.eye(64, dtype=BF), np.eye(64, dtype=BF)])
    p = np.arange(128); t = p % 16
    f = np.arange(512); s = f % 16
    C['TMASKM'] = (t[:, None] <= s[None, :]).astype(BF)
    C['IND16'] = np.zeros((128, 8), BF)
    for g in range(4):
        for pp in range(2):
            C['IND16'][g*32+pp*16:(g*32+pp*16+16), g*2+pp] = 1
    C['IND16T'] = np.ascontiguousarray(C['IND16'].astype(np.float32).T)
    RESH = np.zeros((128, 32), np.float32)
    for jj_ in range(32):
        for pp in range(128):
            r_ = jj_*128 + pp
            pos, s_ = r_ // 16, r_ % 16
            RESH[pp, jj_] = (s_//2)*4096 + (s_ % 2)*2048 + pos
    C['RESHC'] = RESH
    RESH2 = np.zeros((128, 32), np.float32)
    for jj_ in range(32):
        for pp in range(128):
            r_ = jj_*128 + pp
            b_l, pos = r_ // 2048, r_ % 2048
            RESH2[pp, jj_] = (pos//256)*4096 + 256*b_l + (pos % 256)
    C['RESHC2'] = RESH2
    return C


def _asc(x):
    return np.ascontiguousarray(x)


def host_inputs(I, core_id):
    C = host_consts(core_id)
    x0 = np.asarray(I['L0_attn_x'])
    m = {
        "x0": _asc(x0[2*core_id:2*core_id+2].reshape(T, D)),
        "wqT": _asc(I['L0_attn_wq'].T), "wkT": _asc(I['L0_attn_wk'].T), "wvT": _asc(I['L0_attn_wv'].T),
        "qnw": _asc(np.asarray(I['L0_attn_q_norm_w']).reshape(8, 128).T),
        "knw": _asc(np.asarray(I['L0_attn_k_norm_w']).reshape(2, 128).T),
        "woT": _asc(I['L0_attn_wo'].T),
        "qkvT": _asc(I['L0_ffn_qkv_weight'].T),
        "W1REP": _asc(np.broadcast_to(I['L0_ffn_ln1_weight'], (128, D))),
        "B1REP": _asc(np.broadcast_to(I['L0_ffn_ln1_bias'], (128, D))),
        "prjT": _asc(I['L0_ffn_proj_weight'].T),
        "W2REP": _asc(np.broadcast_to(I['L0_ffn_ln2_weight'], (128, D))),
        "B2REP": _asc(np.broadcast_to(I['L0_ffn_ln2_bias'], (128, D))),
        "rw": _asc(I['L0_ffn_router_weight']),
        "w1": np.asarray(I['L0_ffn_moe_w1']), "w2": np.asarray(I['L0_ffn_moe_w2']),
        "Wq": _asc(I['L1_attn_Wq']), "Wkv": _asc(I['L1_attn_Wkv']), "Wo": _asc(I['L1_attn_Wo']),
        "centT": _asc(np.asarray(I['L1_ffn_centroids']).T),
        "w1T": _asc(np.asarray(I['L1_ffn_w1']).transpose(0, 2, 1)),
        "w2T": _asc(np.asarray(I['L1_ffn_w2']).transpose(0, 2, 1)),
        "LNG": _asc(np.asarray(I['L1_ffn_ln_g']).reshape(8, 8, 128).transpose(0, 2, 1)),
        "LNB": _asc(np.asarray(I['L1_ffn_ln_b']).reshape(8, 8, 128).transpose(0, 2, 1)),
        "B1C": _asc(np.asarray(I['L1_ffn_b1']).reshape(8, 32, 128).transpose(0, 2, 1)),
        "B2REP1": _asc(np.broadcast_to(np.asarray(I['L1_ffn_b2'])[:, None, :], (8, 128, D))),
    }
    for k in ('IDB', 'IDF', 'ONESF', 'ONESR', 'RELB', 'DMASK', 'IOTA8', 'TRI128', 'TRI16',
              'ALLONES', 'SROWSEL', 'CSEL', 'SELBANK', 'SCAL', 'CORETRI', 'CID',
              'TMASKM', 'IND16', 'IND16T', 'RESHC', 'RESHC2', 'IDB64'):
        m[k] = C[k]
    return m



def emit_s1_qkv0(sec, A):
    seq = sec.begin(); sb = sec.sb; S, P = sec.S, sec.P
    S('wq', [128, 8, 1024], BF16); S('wk', [128, 8, 256], BF16); S('wv', [128, 8, 256], BF16)
    S('qnw', [128, 8]); S('knw', [128, 2])
    S('idf', [128, 128]); S('onesf', [128, 1]); S('onesr', [1, 128]); S('idb', [128, 128], BF16)
    S('scal', [128, 4])
    S('xt', [128, 4, 1024]); S('ut', [128, 4, 1024])
    S('lnstat', [128, 32]); S('lnscr', [128, 4, 1024])
    S('ufm', [128, 8, 512], BF16)
    S('qraw', [128, 8, 512]); S('sq', [128, 512]); S('rstd1', [1, 512]); S('rstd2', [1, 512])
    S('outb', [128, 8, 512], BF16)
    S('vfm', [128, 2, 512], BF16)
    S('vtmS', [128, 4, 256], BF16)
    psA = P("psA", [128, 512]); psT = P("psT", [128, 512]); psR = P("psR", [1, 512])
    psB = P("psB", [128, 512]); psC = P("psC", [128, 256], BF16)

    seq.phase_dma('gpsimd', lambda e: [
        e.dma_start(out=sb['wq'][:], in_=A['wqT'].rearrange("(c p) o -> p c o", p=128)),
        e.dma_start(out=sb['wk'][:], in_=A['wkT'].rearrange("(c p) o -> p c o", p=128)),
        e.dma_start(out=sb['wv'][:], in_=A['wvT'].rearrange("(c p) o -> p c o", p=128)),
    ], 3)
    seq.phase_dma('sync', lambda e: [
        e.dma_start(out=sb['qnw'][:], in_=A['qnw']),
        e.dma_start(out=sb['knw'][:], in_=A['knw']),
        e.dma_start(out=sb['idf'][:], in_=A['IDF']),
        e.dma_start(out=sb['onesf'][:], in_=A['ONESF']),
        e.dma_start(out=sb['onesr'][:], in_=A['ONESR']),
        e.dma_start(out=sb['idb'][:], in_=A['IDB']),
        e.dma_start(out=sb['scal'][:], in_=A['SCAL']),
    ], 7)
    seq.phase('vector', lambda e: e.tensor_scalar_mul(sb['qnw'][:], sb['qnw'][:], 0.125))

    def rms_block(wname, nouts, nrm_name):
        for oc in range(nouts):
            def mm(e, oc=oc):
                last = None
                for fc in range(8):
                    last = e.matmul(psA[:], sb[wname][:, fc, oc*128:(oc+1)*128], sb['ufm'][:, fc, :],
                                    start=(fc == 0), stop=(fc == 7))
                return last
            seq.phase('tensor', mm)
            seq.phase('vector', lambda e, oc=oc: e.tensor_copy(out=sb['qraw'][:, oc, :], in_=psA[:]))
            seq.phase('scalar', lambda e, oc=oc: e.activation(out=sb['sq'][:], in_=sb['qraw'][:, oc, :], func=AF.Square))
            seq.phase('tensor', lambda e, oc=oc, n=nouts: e.matmul(psR[:], sb['onesf'][:], sb['sq'][:],
                                                                   start=(oc == 0), stop=(oc == n-1)))
        dim = nouts * 128
        seq.phase('vector', lambda e: e.tensor_scalar_mul(sb['rstd1'][:], psR[:], 1.0/dim))
        seq.phase('scalar', lambda e: e.activation(out=sb['rstd2'][:], in_=sb['rstd1'][:], func=AF.Sqrt, bias=sb['scal'][0:1, 1:2]))
        seq.phase('vector', lambda e: e.reciprocal(sb['rstd1'][:], sb['rstd2'][:]))
        seq.phase('tensor', lambda e: e.matmul(psB[:], sb['onesr'][:], sb['rstd1'][:], start=True, stop=True))
        def app(e):
            last = None
            for oc in range(nouts):
                last = e.scalar_tensor_tensor(out=sb['outb'][:, oc, :], in0=sb['qraw'][:, oc, :],
                                              scalar=sb[nrm_name][:, oc:oc+1], in1=psB[:],
                                              op0=ALU.mult, op1=ALU.mult)
            return last
        seq.phase('vector', app)

    for tc in range(8):
        seq.phase_dma('sync', lambda e, tc=tc: [
            e.dma_start(out=sb['xt'][:, i, :], in_=A['x0'][tc*512+i*128: tc*512+(i+1)*128, :]) for i in range(4)
        ], 4)
        emit_ln(sec, [sb['xt'][:, i, :] for i in range(4)], [sb['ut'][:, i, :] for i in range(4)], 1024)
        for fc in range(8):
            def tr(e, fc=fc):
                last = None
                for i in range(4):
                    last = e.transpose(psT[:, i*128:(i+1)*128], sb['ut'][:, i, fc*128:(fc+1)*128], sb['idf'][:])
                return last
            seq.phase('tensor', tr)
            seq.phase('vector', lambda e, fc=fc: e.tensor_copy(out=sb['ufm'][:, fc, :], in_=psT[:]))
        rms_block('wq', 8, 'qnw')
        seq.phase_dma('sync', lambda e, tc=tc: [
            e.dma_start(out=A['qf'][oc*128:(oc+1)*128, tc*512:(tc+1)*512], in_=sb['outb'][:, oc, :]) for oc in range(8)
        ], 8)
        rms_block('wk', 2, 'knw')
        seq.phase_dma('sync', lambda e, tc=tc: [
            e.dma_start(out=A['kf'][oc*128:(oc+1)*128, tc*512:(tc+1)*512], in_=sb['outb'][:, oc, :]) for oc in range(2)
        ], 2)
        for oc in range(2):
            def mmv(e, oc=oc):
                last = None
                for fc in range(8):
                    last = e.matmul(psA[:], sb['wv'][:, fc, oc*128:(oc+1)*128], sb['ufm'][:, fc, :],
                                    start=(fc == 0), stop=(fc == 7))
                return last
            seq.phase('tensor', mmv)
            seq.phase('vector', lambda e, oc=oc: e.tensor_copy(out=sb['vfm'][:, oc, :], in_=psA[:]))
        for i in range(4):
            def trv(e, i=i):
                e.transpose(psC[:, 0:128], sb['vfm'][:, 0, i*128:(i+1)*128], sb['idb'][:])
                return e.transpose(psC[:, 128:256], sb['vfm'][:, 1, i*128:(i+1)*128], sb['idb'][:])
            seq.phase('tensor', trv)
            seq.phase('vector', lambda e, i=i: e.tensor_copy(out=sb['vtmS'][:, i, :], in_=psC[:]))
        seq.phase_dma('sync', lambda e, tc=tc: [
            e.dma_start(out=A['vtm'][tc*512+i*128: tc*512+(i+1)*128, :], in_=sb['vtmS'][:, i, :]) for i in range(4)
        ], 4)
    sec.end()


def emit_s2_att0(sec, A):
    seq = sec.begin(); sb = sec.sb; S, P = sec.S, sec.P
    S('kg', [64, 2, T], BF16)
    S('vtm', [128, 32, 256], BF16)
    S('rel', [128, 16, 640], BF16)
    S('wo', [128, 8, 1024], BF16)
    S('q', [64, 2, 2048], BF16)
    S('sS', [128, 640]); S('sP', [128, 640], BF16); S('sPT', [128, 640], BF16)
    S('nm', [128, 1]); S('sums', [128, 1]); S('rs', [128, 1])
    S('aoS', [128, 64], BF16)
    S('aofm', [128, 8, T], BF16)
    S('x0t', [128, 1024]); S('x1t', [128, 1024])
    S('idb', [128, 128], BF16)
    psS = P("psS", [128, 640]); psPT = P("psPT", [128, 640], BF16)
    psO = P("psO", [128, 64]); psT2 = P("psT2", [64, 128], BF16)
    psPRJ = P("psPRJ", [128, 1024])
    seq.phase_dma('sync', lambda e: [
        e.dma_start(out=sb['vtm'][:], in_=A['vtm'].rearrange("(n p) f -> p n f", p=128)),
        e.dma_start(out=sb['rel'][:], in_=A['RELB'].rearrange("h p k -> p h k")),
        e.dma_start(out=sb['idb'][:], in_=A['IDB']),
    ], 3)
    seq.phase_dma('gpsimd', lambda e: [
        e.dma_start(out=sb['wo'][:], in_=A['woT'].rearrange("(c p) o -> p c o", p=128)),
    ], 1)

    iters = [(b, h, qt) for b in range(2) for h in range(16) for qt in range(16)]
    t_ldq = {}
    t_ldk = {}
    t_lastqk = {}
    t = {'bias': None, 'exp': None, 'nm': None, 'trp': None, 'cpy': None,
         'pv': None, 'rs': None, 'aoS': None, 'tr2': None, 'cp2': None}

    def emit_ldk(bg):
        b, g = bg
        par = g % 2
        prev = (b, g - 2) if g >= 2 else (b - 1, g + 2)
        dep = t_lastqk.get(('g',) + prev, None)
        t_ldk[bg] = seq.phase_dma('sync', lambda e, g=g, par=par: [
            e.dma_start(out=sb['kg'][:, par, :], in_=A['kf'][g*64:(g+1)*64, :]),
        ], 1, deps=[dep] if dep else None)

    def emit_ldq(bh):
        b, h = bh
        par = h % 2
        dep = t_lastqk.get((b, h - 2) if h >= 2 else (b - 1, h + 14), None)
        t_ldq[bh] = seq.phase_dma('sync', lambda e, h=h, b=b, par=par: [
            e.dma_start(out=sb['q'][:, par, :], in_=A['qf'][h*64:(h+1)*64, b*2048:(b+1)*2048]),
        ], 1, deps=[dep] if dep else None)

    def emit_qk(i):
        b, h, qt = iters[i]
        g = h // 4
        off = max(0, 512 - 128*qt)
        kc0 = off // 128
        kstart = b*2048 + 128*qt - 512
        def qk(e, qt=qt, kc0=kc0, kstart=kstart, hp=h % 2, gp=g % 2):
            last = None
            for kc in range(kc0, 5):
                ktile = (kstart + kc*128) // 128
                last = e.matmul(psS[:, kc*128:(kc+1)*128],
                                sb['q'][:, hp, qt*128:(qt+1)*128],
                                sb['kg'][:, gp, ktile*128:(ktile+1)*128],
                                start=True, stop=True)
            return last
        tok = seq.phase('tensor', qk, deps=[t_ldq[(b, h)], t_ldk[(b, g)], t['bias']])
        t_lastqk[(b, h)] = tok
        t_lastqk[('g', b, g)] = tok
        return tok

    def emit_head(i, t_qk):
        b, h, qt = iters[i]
        off = max(0, 512 - 128*qt)
        n = 640 - off
        t['bias'] = seq.phase('vector', lambda e, h=h, off=off, n=n: e.scalar_tensor_tensor(
            out=sb['sS'][:, 0:n], in0=psS[:, off:640], scalar=1.0,
            in1=sb['rel'][:, h, off:640], op0=ALU.mult, op1=ALU.add),
            deps=[t_qk, t['exp'], t['nm']])

    def emit_body(i):
        b, h, qt = iters[i]
        g = h // 4
        off = max(0, 512 - 128*qt)
        kc0 = off // 128
        kstart = b*2048 + 128*qt - 512
        n = 640 - off
        t['nm'] = seq.phase('vector', lambda e, n=n: e.tensor_reduce(out=sb['nm'][:], in_=sb['sS'][:, 0:n], axis=AX.X, op=ALU.max, negate=True),
                            deps=[t['bias'], t['exp']])
        t['exp'] = seq.phase('scalar', lambda e, n=n: e.activation(out=sb['sP'][:, 0:n], in_=sb['sS'][:, 0:n], func=AF.Exp,
                                                                   bias=sb['nm'][:, 0:1], accum_out=sb['sums'][:, 0:1]),
                             deps=[t['nm'], t['trp'], t['rs']])
        def trp(e, kc0=kc0, off=off):
            last = None
            for kc in range(kc0, 5):
                last = e.transpose(psPT[:, kc*128:(kc+1)*128], sb['sP'][:, kc*128-off:(kc+1)*128-off], sb['idb'][:])
            return last
        t['trp'] = seq.phase('tensor', trp, deps=[t['exp'], t['cpy']])
        t['cpy'] = seq.phase('vector', lambda e, kc0=kc0: e.tensor_copy(out=sb['sPT'][:, kc0*128:640], in_=psPT[:, kc0*128:640]),
                             deps=[t['trp'], t['pv']])
        def pv(e, g=g, kc0=kc0, kstart=kstart):
            last = None
            for kc in range(kc0, 5):
                ktile = (kstart + kc*128) // 128
                last = e.matmul(psO[:], sb['sPT'][:, kc*128:(kc+1)*128],
                                sb['vtm'][:, ktile, g*64:(g+1)*64],
                                start=(kc == kc0), stop=(kc == 4))
            return last
        t['pv'] = seq.phase('tensor', pv, deps=[t['cpy'], t['aoS']])
        t['rs'] = seq.phase('vector', lambda e: e.reciprocal(sb['rs'][:], sb['sums'][:]),
                            deps=[t['exp'], t['aoS']])
        t['aoS'] = seq.phase('vector', lambda e: e.tensor_scalar(out=sb['aoS'][:], in0=psO[:], scalar1=sb['rs'][:, 0:1], scalar2=None, op0=ALU.mult),
                             deps=[t['pv'], t['rs'], t['tr2']])
        t['tr2'] = seq.phase('tensor', lambda e: e.transpose(psT2[:], sb['aoS'][:], sb['idb'][:]),
                             deps=[t['aoS'], t['cp2']])
        t['cp2'] = seq.phase('vector', lambda e, b=b, h=h, qt=qt: e.tensor_copy(
            out=sb['aofm'][(h % 2)*64:(h % 2)*64+64, h//2, b*2048+qt*128: b*2048+(qt+1)*128], in_=psT2[:]),
            deps=[t['tr2']])

    emit_ldk((0, 0))
    emit_ldk((0, 1))
    emit_ldq((0, 0))
    emit_ldq((0, 1))
    pend = emit_qk(0)
    for i in range(len(iters)):
        emit_head(i, pend)
        if i + 1 < len(iters):
            bn, hn, qn = iters[i+1]
            if qn == 0:
                nb, nh = (bn, hn + 1) if hn < 15 else (bn + 1, 0)
                if nb < 2:
                    emit_ldq((nb, nh))
                if hn % 4 == 0:
                    gcur = (bn, hn // 4)
                    gnxt = (gcur[0], gcur[1] + 1) if gcur[1] < 3 else (gcur[0] + 1, 0)
                    if gnxt[0] < 2 and gnxt not in t_ldk:
                        emit_ldk(gnxt)
            pend = emit_qk(i + 1)
        emit_body(i)
    for tt in range(32):
        seq.phase_dma('sync', lambda e, tt=tt: [e.dma_start(out=sb['x0t'][:], in_=A['x0'][tt*128:(tt+1)*128, :])], 1)
        def prj(e, tt=tt):
            last = None
            for half in range(2):
                for fc in range(8):
                    last = e.matmul(psPRJ[:, half*512:(half+1)*512],
                                    sb['aofm'][:, fc, tt*128:(tt+1)*128],
                                    sb['wo'][:, fc, half*512:(half+1)*512],
                                    start=(fc == 0), stop=(fc == 7))
            return last
        seq.phase('tensor', prj)
        seq.phase('vector', lambda e: e.tensor_tensor(out=sb['x1t'][:], in0=sb['x0t'][:], in1=psPRJ[:], op=ALU.add))
        seq.phase_dma('sync', lambda e, tt=tt: [e.dma_start(out=A['x1loc'][tt*128:(tt+1)*128, :], in_=sb['x1t'][:])], 1)
    sec.end()


def emit_reshard(sec, A, src_loc, src_g, dst, base_name, coef):
    seq = sec.begin(); sb = sec.sb; S, P = sec.S, sec.P
    S('resh', [128, 32]); S('cid', [128, 1]); S('idxf', [128, 32]); S('idxi', [128, 32], I32)
    S('cids', [128, 1]); S('gt', [128, 4, 1024])
    seq.phase_cc(lambda e: e.collective_compute(
        "AllGather", ALU.bypass, replica_groups=[list(range(8))],
        ins=[A[src_loc].opt()], outs=[A[src_g].opt()]))
    seq.phase_dma('sync', lambda e: [
        e.dma_start(out=sb['resh'][:], in_=A[base_name]),
        e.dma_start(out=sb['cid'][:], in_=A['CID']),
    ], 2)
    seq.phase('vector', lambda e: e.tensor_scalar_mul(sb['cids'][:], sb['cid'][:], float(coef)))
    seq.phase('vector', lambda e: e.tensor_scalar(out=sb['idxf'][:], in0=sb['resh'][:], scalar1=1.0,
                                                  scalar2=sb['cids'][:, 0:1], op0=ALU.mult, op1=ALU.add))
    seq.phase('vector', lambda e: e.tensor_copy(out=sb['idxi'][:], in_=sb['idxf'][:]))
    for g in range(8):
        seq.phase_dma('gpsimd', lambda e, g=g: [
            e.indirect_dma_start(out=sb['gt'][:, t, :], out_offset=None, in_=A[src_g],
                                 in_offset=bass.IndirectOffsetOnAxis(ap=sb['idxi'][:, g*4+t:g*4+t+1], axis=0))
            for t in range(4)
        ], 4)
        seq.phase_dma('sync', lambda e, g=g: [
            e.dma_start(out=A[dst][(g*4+t)*128:(g*4+t+1)*128, :], in_=sb['gt'][:, t, :])
            for t in range(4)
        ], 4)
    sec.end()


def emit_s4_ffn0a(sec, A):
    seq = sec.begin(); sb = sec.sb; S, P = sec.S, sec.P
    S('wk', [128, 8, 1536], BF16)
    S('idf', [128, 128]); S('scal', [128, 4])
    S('w1rep', [128, 1024]); S('b1rep', [128, 1024])
    S('xt', [128, 4, 1024]); S('ut', [128, 4, 1024]); S('lt', [128, 4, 1024])
    S('lnstat', [128, 32]); S('lnscr', [128, 4, 1024])
    S('ufm', [128, 8, 512], BF16)
    S('outb', [128, 12, 512], BF16)
    psA = P("psA", [128, 512]); psT = P("psT", [128, 512])
    seq.phase_dma('gpsimd', lambda e: [
        e.dma_start(out=sb['wk'][:], in_=A['qkvT'].rearrange("(c p) o -> p c o", p=128)),
    ], 1)
    seq.phase_dma('sync', lambda e: [
        e.dma_start(out=sb['idf'][:], in_=A['IDF']),
        e.dma_start(out=sb['scal'][:], in_=A['SCAL']),
        e.dma_start(out=sb['w1rep'][:], in_=A['W1REP']),
        e.dma_start(out=sb['b1rep'][:], in_=A['B1REP']),
    ], 4)
    for tc in range(8):
        seq.phase_dma('sync', lambda e, tc=tc: [
            e.dma_start(out=sb['xt'][:, i, :], in_=A['x1r'][tc*512+i*128: tc*512+(i+1)*128, :]) for i in range(4)
        ], 4)
        emit_ln(sec, [sb['xt'][:, i, :] for i in range(4)], [sb['ut'][:, i, :] for i in range(4)], 1024)
        seq.phase_dma('sync', lambda e, tc=tc: [
            e.dma_start(out=A['u1'][tc*512+i*128: tc*512+(i+1)*128, :], in_=sb['ut'][:, i, :]) for i in range(4)
        ], 4)
        def lw(e):
            last = None
            for i in range(4):
                last = e.tensor_tensor(out=sb['lt'][:, i, :], in0=sb['ut'][:, i, :], in1=sb['w1rep'][:], op=ALU.mult)
            return last
        seq.phase('vector', lw)
        def lb(e):
            last = None
            for i in range(4):
                last = e.tensor_tensor(out=sb['lt'][:, i, :], in0=sb['lt'][:, i, :], in1=sb['b1rep'][:], op=ALU.add)
            return last
        seq.phase('vector', lb)
        for fc in range(8):
            def tr(e, fc=fc):
                last = None
                for i in range(4):
                    last = e.transpose(psT[:, i*128:(i+1)*128], sb['lt'][:, i, fc*128:(fc+1)*128], sb['idf'][:])
                return last
            seq.phase('tensor', tr)
            seq.phase('vector', lambda e, fc=fc: e.tensor_copy(out=sb['ufm'][:, fc, :], in_=psT[:]))
        for oc in range(12):
            def mm(e, oc=oc):
                last = None
                for fc in range(8):
                    last = e.matmul(psA[:], sb['wk'][:, fc, oc*128:(oc+1)*128], sb['ufm'][:, fc, :],
                                    start=(fc == 0), stop=(fc == 7))
                return last
            seq.phase('tensor', mm)
            g, j = oc // 3, oc % 3
            if j < 2:
                seq.phase('vector', lambda e, oc=oc: e.tensor_scalar_mul(sb['outb'][:, oc, :], psA[:], 0.125))
            else:
                seq.phase('vector', lambda e, oc=oc: e.tensor_copy(out=sb['outb'][:, oc, :], in_=psA[:]))
        def st(e, tc=tc):
            ds = []
            for oc in range(12):
                g, j = oc // 3, oc % 3
                if j < 2:
                    ds.append(e.dma_start(out=A['qtf'][(g*256+j*128):(g*256+j*128+128), tc*512:(tc+1)*512],
                                          in_=sb['outb'][:, oc, :]))
                else:
                    ds.append(e.dma_start(out=A['ktf'][g*64:(g+1)*64, tc*512:(tc+1)*512],
                                          in_=sb['outb'][0:64, oc, :]))
                    ds.append(e.dma_start(out=A['vtf'][g*64:(g+1)*64, tc*512:(tc+1)*512],
                                          in_=sb['outb'][64:128, oc, :]))
            return ds
        seq.phase_dma('sync', st, 16)
    sec.end()


S5STAGE = [99]

def emit_s5_tiny(sec, A):
    seq = sec.begin(); sb = sec.sb; S, P = sec.S, sec.P
    S('idb', [128, 128], BF16)
    S('idb64', [128, 64], BF16)
    S('tmask', [128, 512], BF16)
    S('ind16', [128, 8], BF16)
    S('ind16t', [8, 128])
    S('vtfs', [128, 2, T], BF16)
    S('kbd', [128, 4, 1024], BF16)
    S('qmv', [128, 4, 2048], BF16)
    S('vmvT', [128, 4, 8, 64], BF16)
    S('sE', [128, 512], BF16)
    S('mE', [128, 512], BF16)
    S('rD', [8, 512])
    S('sP', [128, 512], BF16)
    S('pbd', [128, 4, 1024], BF16)
    S('aostage', [128, 2, 4096], BF16)
    psSC = P("psSC", [128, 512])
    psD = P("psD", [8, 512])
    psB = P("psB", [128, 512])
    psAO = [P(f"psAO{b4}", [128, 256]) for b4 in range(4)]
    psTv = P("psTv", [128, 512], BF16)
    seq.phase_dma('sync', lambda e: [
        e.dma_start(out=sb['idb'][:], in_=A['IDB']),
        e.dma_start(out=sb['idb64'][:], in_=A['IDB64']),
        e.dma_start(out=sb['tmask'][:], in_=A['TMASKM']),
        e.dma_start(out=sb['ind16'][:], in_=A['IND16']),
        e.dma_start(out=sb['ind16t'][:], in_=A['IND16T']),
    ], 5)
    seq.phase('vector', lambda e: e.memset(sb['pbd'][:], 0.0))
    seq.phase('vector', lambda e: e.memset(sb['kbd'][:], 0.0))
    seq.phase_dma('gpsimd', lambda e: [
        e.dma_start(out=sb['vtfs'][:], in_=A['vtf'].rearrange("(c p) t -> p c t", p=128)),
    ], 1)
    for ch in range(4):
        chp = ch * 32
        def ldk(e, chp=chp):
            ds = []
            for g in range(4):
                for pp in range(2):
                    ds.append(e.dma_start(
                        out=sb['kbd'][pp*64:(pp+1)*64, g, :].rearrange("p (a t) -> p a t", t=32)[:, :, pp*16:(pp+1)*16],
                        in_=A['ktf'][g*64:(g+1)*64, :].rearrange("p (a w) -> p a w", w=32)[:, chp:chp+32, pp*16:(pp+1)*16]))
            return ds
        seq.phase_dma('sync', ldk, 8)
        def ldq(e, chp=chp):
            ds = []
            for g in range(4):
                for qi in range(4):
                    for pp in range(2):
                        ds.append(e.dma_start(
                            out=sb['qmv'][pp*64:(pp+1)*64, g, :].rearrange("p (a x) -> p a x", x=64)[:, :, qi*16:(qi+1)*16],
                            in_=A['qtf'][g*256+qi*64:g*256+qi*64+64, :].rearrange("p (a w) -> p a w", w=32)[:, chp:chp+32, pp*16:(pp+1)*16]))
            return ds
        seq.phase_dma('gpsimd', ldq, 32)
        for g in range(4):
            def trv(e, g=g, ch=ch):
                last = None
                for tt in range(8):
                    last = e.transpose(psTv[:, tt*64:(tt+1)*64],
                                       sb['vtfs'][(g % 2)*64:(g % 2)*64+64, g//2, ch*1024+tt*128: ch*1024+(tt+1)*128],
                                       sb['idb64'][(g % 2)*64:(g % 2)*64+64, :])
                return last
            seq.phase('tensor', trv)
            seq.phase('vector', lambda e, g=g: e.tensor_copy(
                out=sb['vmvT'][:, g, :, :].rearrange("p a d -> p (a d)"), in_=psTv[:]))
        if S5STAGE[0] < 1:
            continue
        for sg in range(4):
            def qk(e, sg=sg):
                last = None
                for g in range(4):
                    for pi in range(8):
                        pr = sg*8 + pi
                        last = e.matmul(psSC[g*32:(g+1)*32, pi*64:(pi+1)*64],
                                        sb['kbd'][:, g, pr*32:(pr+1)*32],
                                        sb['qmv'][:, g, pr*64:(pr+1)*64],
                                        start=True, stop=True, tile_position=(0, g*32))
                return last
            seq.phase('tensor', qk)
            if S5STAGE[0] < 2:
                continue
            seq.phase('scalar', lambda e: e.activation(out=sb['sE'][:], in_=psSC[:], func=AF.Exp))
            seq.phase('vector', lambda e: e.tensor_tensor(out=sb['mE'][:], in0=sb['sE'][:], in1=sb['tmask'][:], op=ALU.mult))
            seq.phase('tensor', lambda e: e.matmul(psD[:], sb['ind16'][:], sb['mE'][:], start=True, stop=True))
            seq.phase('vector', lambda e: e.reciprocal(sb['rD'][:], psD[:]))
            seq.phase('tensor', lambda e: e.matmul(psB[:], sb['ind16t'][:], sb['rD'][:], start=True, stop=True))
            seq.phase('vector', lambda e: e.tensor_tensor(out=sb['sP'][:], in0=sb['mE'][:], in1=psB[:], op=ALU.mult))
            if S5STAGE[0] < 3:
                continue
            def pba(e):
                ds = []
                for g in range(4):
                    for pp in range(2):
                        for b4 in range(4):
                            ds.append(e.dma_start(
                                out=sb['pbd'][b4*32+pp*16: b4*32+pp*16+16, g, :]
                                    .rearrange("p (a x) -> p a x", x=128)[:, b4::4, pp*64:(pp+1)*64],
                                in_=sb['sP'][g*32+pp*16: g*32+pp*16+16, :]
                                    .rearrange("p (a x) -> p a x", x=64)[:, b4::4, :]))
                return ds
            seq.phase_dma('gpsimd', pba, 32)
            if S5STAGE[0] < 4:
                continue
            for rnd in range(2):
                def ao(e, sg=sg, rnd=rnd):
                    last = None
                    for b4 in range(4):
                        pi = rnd*4 + b4
                        tt = (sg*8 + pi) // 4
                        for g in range(4):
                            last = e.matmul(psAO[b4][(g % 2)*64:(g % 2)*64+64, (g//2)*128:(g//2)*128+128],
                                            sb['vmvT'][b4*32:(b4+1)*32, g, tt, :],
                                            sb['pbd'][b4*32:(b4+1)*32, g, pi*128:(pi+1)*128],
                                            start=True, stop=True, tile_position=(b4*32, (g % 2)*64))
                    return last
                seq.phase('tensor', ao)
                def ev(e, sg=sg, rnd=rnd):
                    last = None
                    for b4 in range(4):
                        pi = rnd*4 + b4
                        last = e.tensor_copy(
                            out=sb['aostage'][:, :, sg*1024+pi*128: sg*1024+(pi+1)*128],
                            in_=psAO[b4].rearrange("p (g x) -> p g x", x=128))
                    return last
                seq.phase('vector', ev)
        if S5STAGE[0] < 5:
            continue
        def afm(e, ch=ch):
            ds = []
            for g in range(4):
                for qi in range(4):
                    for pp in range(2):
                        src = sb['aostage'][(g % 2)*64:(g % 2)*64+64, g//2, :] \
                            .rearrange("p (a b x) -> p a b x", b=8, x=128)[:, :, :, pp*64+qi*16: pp*64+qi*16+16]
                        dst = A['aofm_t'][g*256+qi*64: g*256+qi*64+64, :] \
                            .rearrange("p (a b pz s) -> p a b pz s", b=8, pz=2, s=16)[:, ch*4:(ch+1)*4, :, pp, :]
                        ds.append(e.dma_start(out=dst, in_=src))
            return ds
        seq.phase_dma('gpsimd', afm, 32)
    sec.end()


def emit_s6_ffn0b(sec, A):
    seq = sec.begin(); sb = sec.sb; S, P = sec.S, sec.P
    S('aofm', [128, 8, T], BF16)
    S('pw', [128, 8, 1024], BF16)
    S('rw', [128, 8, 8], BF16)
    S('iota8', [128, 8]); S('srow', [128, 16])
    S('idf', [128, 128]); S('scal', [128, 4])
    S('w2rep', [128, 1024]); S('b2rep', [128, 1024])
    S('xt', [128, 1, 1024]); S('ut', [128, 1, 1024])
    S('lt', [128, 1024]); S('l2b', [128, 1024], BF16)
    S('l2fm', [128, 8, 128], BF16)
    S('lnstat', [128, 32]); S('lnscr', [128, 1, 1024])
    S('u1t', [128, 1024])
    S('lg', [128, 8]); S('mx', [128, 8]); S('oh1', [128, 8]); S('oh2', [128, 8])
    S('msk', [128, 8]); S('indt', [128, 8])
    S('e1c', [128, 8]); S('e2c', [128, 8])
    S('rt', [128, 8]); S('Rt', [16, 8])
    psPRJ = P("psPRJ", [128, 1024]); psT = P("psT", [128, 512])
    psLG = P("psLG", [128, 8]); psRT = P("psRT", [16, 8])
    seq.phase_dma('gpsimd', lambda e: [
        e.dma_start(out=sb['aofm'][:], in_=A['aofm_t'].rearrange("(c p) t -> p c t", p=128)),
        e.dma_start(out=sb['pw'][:], in_=A['prjT'].rearrange("(c p) o -> p c o", p=128)),
        e.dma_start(out=sb['rw'][:], in_=A['rw'].rearrange("(c p) o -> p c o", p=128)),
    ], 3)
    seq.phase_dma('sync', lambda e: [
        e.dma_start(out=sb['iota8'][:], in_=A['IOTA8']),
        e.dma_start(out=sb['srow'][:], in_=A['SROWSEL']),
        e.dma_start(out=sb['idf'][:], in_=A['IDF']),
        e.dma_start(out=sb['scal'][:], in_=A['SCAL']),
        e.dma_start(out=sb['w2rep'][:], in_=A['W2REP']),
        e.dma_start(out=sb['b2rep'][:], in_=A['B2REP']),
    ], 6)
    for tt in range(32):
        seq.phase_dma('sync', lambda e, tt=tt: [
            e.dma_start(out=sb['u1t'][:], in_=A['u1'][tt*128:(tt+1)*128, :]),
        ], 1)
        def prj(e, tt=tt):
            last = None
            for half in range(2):
                for fc in range(8):
                    last = e.matmul(psPRJ[:, half*512:(half+1)*512],
                                    sb['aofm'][:, fc, tt*128:(tt+1)*128],
                                    sb['pw'][:, fc, half*512:(half+1)*512],
                                    start=(fc == 0), stop=(fc == 7))
            return last
        seq.phase('tensor', prj)
        seq.phase('vector', lambda e: e.tensor_tensor(out=sb['xt'][:, 0, :], in0=sb['u1t'][:], in1=psPRJ[:], op=ALU.add))
        seq.phase_dma('sync', lambda e, tt=tt: [
            e.dma_start(out=A['h1'][tt*128:(tt+1)*128, :], in_=sb['xt'][:, 0, :]),
        ], 1)
        emit_ln(sec, [sb['xt'][:, 0, :]], [sb['ut'][:, 0, :]], 1024)
        seq.phase('vector', lambda e: e.tensor_tensor(out=sb['lt'][:], in0=sb['ut'][:, 0, :], in1=sb['w2rep'][:], op=ALU.mult))
        seq.phase('vector', lambda e: e.tensor_tensor(out=sb['lt'][:], in0=sb['lt'][:], in1=sb['b2rep'][:], op=ALU.add))
        seq.phase('vector', lambda e: e.tensor_copy(out=sb['l2b'][:], in_=sb['lt'][:]))
        seq.phase_dma('sync', lambda e, tt=tt: [
            e.dma_start(out=A['ln2tm'][tt*128:(tt+1)*128, :], in_=sb['l2b'][:]),
        ], 1)
        for fc in range(4):
            def tr(e, fc=fc):
                e.transpose(psT[:, 0:128], sb['lt'][:, (2*fc)*128:(2*fc+1)*128], sb['idf'][:])
                return e.transpose(psT[:, 128:256], sb['lt'][:, (2*fc+1)*128:(2*fc+2)*128], sb['idf'][:])
            seq.phase('tensor', tr)
            def ev(e, fc=fc):
                e.tensor_copy(out=sb['l2fm'][:, 2*fc, :], in_=psT[:, 0:128])
                return e.tensor_copy(out=sb['l2fm'][:, 2*fc+1, :], in_=psT[:, 128:256])
            seq.phase('vector', ev)
        def rmm(e):
            last = None
            for fc in range(8):
                last = e.matmul(psLG[:], sb['l2fm'][:, fc, :], sb['rw'][:, fc, :],
                                start=(fc == 0), stop=(fc == 7))
            return last
        seq.phase('tensor', rmm)
        seq.phase('vector', lambda e: e.tensor_copy(out=sb['lg'][:], in_=psLG[:]))
        seq.phase('vector', lambda e: e.max(out=sb['mx'][:], in_=sb['lg'][:]))
        seq.phase('vector', lambda e: e.tensor_scalar(out=sb['oh1'][:], in0=sb['lg'][:], scalar1=sb['mx'][:, 0:1], scalar2=None, op0=ALU.is_equal))
        seq.phase('vector', lambda e: e.scalar_tensor_tensor(out=sb['msk'][:], in0=sb['oh1'][:], scalar=NEG, in1=sb['lg'][:], op0=ALU.mult, op1=ALU.add))
        seq.phase('vector', lambda e: e.tensor_scalar(out=sb['oh2'][:], in0=sb['msk'][:], scalar1=sb['mx'][:, 1:2], scalar2=None, op0=ALU.is_equal))
        seq.phase('vector', lambda e: e.tensor_tensor(out=sb['indt'][:], in0=sb['oh1'][:], in1=sb['oh2'][:], op=ALU.add))
        def ec(e):
            e.tensor_tensor(out=sb['e1c'][:], in0=sb['oh1'][:], in1=sb['iota8'][:], op=ALU.mult)
            return e.tensor_tensor(out=sb['e2c'][:], in0=sb['oh2'][:], in1=sb['iota8'][:], op=ALU.mult)
        seq.phase('vector', ec)
        def er(e):
            e.tensor_reduce(out=sb['rt'][:, 0:1], in_=sb['e1c'][:], axis=AX.X, op=ALU.add)
            return e.tensor_reduce(out=sb['rt'][:, 1:2], in_=sb['e2c'][:], axis=AX.X, op=ALU.add)
        seq.phase('vector', er)
        seq.phase('vector', lambda e: e.tensor_tensor(out=sb['rt'][:, 4:5], in0=sb['mx'][:, 0:1], in1=sb['mx'][:, 1:2], op=ALU.subtract))
        seq.phase('scalar', lambda e: e.activation(out=sb['rt'][:, 2:3], in_=sb['rt'][:, 4:5], func=AF.Sigmoid))
        seq.phase('vector', lambda e: e.tensor_scalar(out=sb['rt'][:, 3:4], in0=sb['rt'][:, 2:3], scalar1=-1.0, scalar2=1.0, op0=ALU.mult, op1=ALU.add))
        seq.phase('tensor', lambda e, tt=tt: e.matmul(psRT[:], sb['srow'][:], sb['indt'][:],
                                                      start=(tt == 0), stop=(tt == 31)))
        seq.phase_dma('sync', lambda e, tt=tt: [
            e.dma_start(out=A['ind'][tt*128:(tt+1)*128, :], in_=sb['indt'][:]),
            e.dma_start(out=A['route'][tt*128:(tt+1)*128, :], in_=sb['rt'][:, 0:4]),
        ], 2)
    seq.phase('vector', lambda e: e.tensor_copy(out=sb['Rt'][:], in_=psRT[:]))
    seq.phase_dma('sync', lambda e: [e.dma_start(out=A['Rloc'], in_=sb['Rt'][:])], 1)
    sec.end()


def emit_s7_rgather(sec, A):
    seq = sec.begin()
    seq.phase_cc(lambda e: e.collective_compute(
        "AllGather", ALU.bypass, replica_groups=[list(range(8))],
        ins=[A['Rloc'].opt()], outs=[A['Rg'].opt()]))
    sec.end()


def emit_s8_disp(sec, A):
    seq = sec.begin(); sb = sec.sb; S, P = sec.S, sec.P
    for nm, shp in [('tri', [128, 128]), ('allones', [128, 128]), ('csel', [128, 16]),
                    ('coretri', [128, 16]), ('iota8', [128, 8]), ('onesf', [128, 1])]:
        S(nm, shp)
    S('tri16', [16, 16]); S('selbank', [16, 16*128]); S('onesr', [1, 128])
    S('Rall', [128, 8]); S('G', [16, 8]); S('SOFF', [16, 8])
    S('inda', [128, 256]); S('rta', [128, 128])
    S('oh1', [128, 8]); S('oh2', [128, 8])
    S('lnta', [128, 32, 1024], BF16)
    S('vm', [128, 8]); S('tmpA', [128, 8]); S('tmpB', [128, 8])
    S('vindt', [128, 8]); S('slotsS', [128, 8])
    S('tmpC', [128, 8]); S('tmpD', [128, 8])
    S('wt', [128, 24])
    S('ixt1', [128, 1], I32); S('ixt2', [128, 1], I32)
    S('pvt', [128, 2])
    S('rtA', [1, 8]); S('rtB', [1, 8]); S('rtS', [1, 8])
    psPF = P("psPF", [128, 8]); psSL = P("psSL", [128, 8]); psR2 = P("psR2", [1, 8])
    psG = P("psG", [16, 8]); psSO = P("psSO", [16, 8])
    seq.phase_dma('sync', lambda e: [
        e.dma_start(out=sb['tri'][:], in_=A['TRI128']),
        e.dma_start(out=sb['tri16'][:], in_=A['TRI16']),
        e.dma_start(out=sb['allones'][:], in_=A['ALLONES']),
        e.dma_start(out=sb['csel'][:], in_=A['CSEL']),
        e.dma_start(out=sb['coretri'][:], in_=A['CORETRI']),
        e.dma_start(out=sb['selbank'][:], in_=A['SELBANK']),
        e.dma_start(out=sb['iota8'][:], in_=A['IOTA8']),
        e.dma_start(out=sb['onesf'][:], in_=A['ONESF']),
        e.dma_start(out=sb['onesr'][:], in_=A['ONESR']),
        e.dma_start(out=sb['Rall'][:], in_=A['Rg']),
    ], 10)
    def preload(e):
        ds = []
        for pb in range(2):
            ds.append(e.dma_start(
                out=sb['inda'][:].rearrange("p (s pb e) -> p s pb e", pb=2, e=8)[:, :, pb, :],
                in_=A['ind'].rearrange("(pb pl s) e -> pb pl s e", pb=2, pl=128)[pb]))
            ds.append(e.dma_start(
                out=sb['rta'][:].rearrange("p (s pb c) -> p s pb c", pb=2, c=4)[:, :, pb, :],
                in_=A['route'].rearrange("(pb pl s) c -> pb pl s c", pb=2, pl=128)[pb]))
            ds.append(e.dma_start(
                out=sb['lnta'][:].rearrange("p (s pb) f -> p s pb f", pb=2)[:, :, pb, :],
                in_=A['ln2tm'].rearrange("(pb pl s) f -> pb pl s f", pb=2, pl=128)[pb]))
        return ds
    seq.phase_dma('gpsimd', preload, 6)
    seq.phase('tensor', lambda e: e.matmul(psG[:], sb['csel'][:], sb['Rall'][:], start=True, stop=True))
    seq.phase('vector', lambda e: e.tensor_copy(out=sb['G'][:], in_=psG[:]))
    def soff(e):
        e.matmul(psSO[:], sb['tri16'][:], sb['G'][:], start=True, stop=False)
        return e.matmul(psSO[:], sb['coretri'][:], sb['Rall'][:], start=False, stop=True)
    seq.phase('tensor', soff)
    seq.phase('vector', lambda e: e.tensor_copy(out=sb['SOFF'][:], in_=psSO[:]))
    seq.phase('vector', lambda e: e.memset(sb['rtA'][:], 0.0))
    for j in range(32):
        s_, pb_ = j // 2, j % 2
        cur = sb['inda'][:, j*8:(j+1)*8]
        prev = sb['inda'][:, (j-1)*8:j*8] if j else None
        rt = sb['rta'][:, j*4:(j+1)*4]
        rt_cur = 'rtA' if j % 2 == 0 else 'rtB'
        rt_nxt = 'rtB' if j % 2 == 0 else 'rtA'
        def g1(e, s_=s_, pb_=pb_, cur=cur, prev=prev):
            e.matmul(psPF[:], sb['tri'][:], cur, start=True, stop=False)
            if pb_ == 1:
                e.matmul(psPF[:], sb['allones'][:], prev, start=False, stop=False)
            return e.matmul(psPF[:], sb['selbank'][:, s_*128:(s_+1)*128], sb['SOFF'][:],
                            start=False, stop=True)
        seq.phase('tensor', g1)
        seq.phase('vector', lambda e: e.tensor_scalar(out=sb['vm'][:], in0=psPF[:], scalar1=float(CAP0), scalar2=None, op0=ALU.is_lt))
        def v2(e, rt=rt):
            e.tensor_scalar(out=sb['oh1'][:], in0=sb['iota8'][:], scalar1=rt[:, 0:1], scalar2=None, op0=ALU.is_equal)
            return e.tensor_scalar(out=sb['oh2'][:], in0=sb['iota8'][:], scalar1=rt[:, 1:2], scalar2=None, op0=ALU.is_equal)
        seq.phase('vector', v2)
        def v3(e):
            e.tensor_tensor(out=sb['tmpA'][:], in0=sb['vm'][:], in1=sb['oh1'][:], op=ALU.mult)
            return e.tensor_tensor(out=sb['tmpB'][:], in0=sb['vm'][:], in1=sb['oh2'][:], op=ALU.mult)
        seq.phase('vector', v3)
        def v4(e):
            e.tensor_reduce(out=sb['wt'][:, 4:5], in_=sb['tmpA'][:], axis=AX.X, op=ALU.add)
            return e.tensor_reduce(out=sb['wt'][:, 5:6], in_=sb['tmpB'][:], axis=AX.X, op=ALU.add)
        seq.phase('vector', v4)
        def v5(e):
            e.tensor_scalar(out=sb['tmpA'][:], in0=sb['oh1'][:], scalar1=sb['wt'][:, 4:5], scalar2=None, op0=ALU.mult)
            return e.tensor_scalar(out=sb['tmpB'][:], in0=sb['oh2'][:], scalar1=sb['wt'][:, 5:6], scalar2=None, op0=ALU.mult)
        seq.phase('vector', v5)
        seq.phase('vector', lambda e: e.tensor_tensor(out=sb['vindt'][:], in0=sb['tmpA'][:], in1=sb['tmpB'][:], op=ALU.add))
        def g2(e, rt_cur=rt_cur):
            e.matmul(psSL[:], sb['onesr'][:], sb[rt_cur][:], start=True, stop=False)
            return e.matmul(psSL[:], sb['tri'][:], sb['vindt'][:], start=False, stop=True)
        seq.phase('tensor', g2)
        seq.phase('tensor', lambda e: e.matmul(psR2[:], sb['onesf'][:], sb['vindt'][:], start=True, stop=True))
        def v7(e):
            e.tensor_copy(out=sb['slotsS'][:], in_=psSL[:])
            return e.tensor_copy(out=sb['rtS'][:], in_=psR2[:])
        seq.phase('vector', v7)
        seq.phase('vector', lambda e, rt_cur=rt_cur, rt_nxt=rt_nxt: e.tensor_tensor(out=sb[rt_nxt][:], in0=sb[rt_cur][:], in1=sb['rtS'][:], op=ALU.add))
        def v9a(e):
            e.tensor_tensor(out=sb['tmpC'][:], in0=sb['slotsS'][:], in1=sb['tmpA'][:], op=ALU.mult)
            return e.tensor_tensor(out=sb['tmpD'][:], in0=sb['slotsS'][:], in1=sb['tmpB'][:], op=ALU.mult)
        seq.phase('vector', v9a)
        def v9b(e):
            e.tensor_reduce(out=sb['wt'][:, 6:7], in_=sb['tmpC'][:], axis=AX.X, op=ALU.add)
            return e.tensor_reduce(out=sb['wt'][:, 7:8], in_=sb['tmpD'][:], axis=AX.X, op=ALU.add)
        seq.phase('vector', v9b)
        def v10(e):
            e.tensor_scalar(out=sb['wt'][:, 8:9], in0=sb['wt'][:, 6:7], scalar1=float(LCAP0), scalar2=None, op0=ALU.is_lt)
            return e.tensor_scalar(out=sb['wt'][:, 9:10], in0=sb['wt'][:, 7:8], scalar1=float(LCAP0), scalar2=None, op0=ALU.is_lt)
        seq.phase('vector', v10)
        def v11(e):
            e.tensor_tensor(out=sb['wt'][:, 10:11], in0=sb['wt'][:, 4:5], in1=sb['wt'][:, 8:9], op=ALU.mult)
            return e.tensor_tensor(out=sb['wt'][:, 11:12], in0=sb['wt'][:, 5:6], in1=sb['wt'][:, 9:10], op=ALU.mult)
        seq.phase('vector', v11)
        def v12(e, rt=rt):
            e.tensor_scalar(out=sb['wt'][:, 12:13], in0=rt[:, 0:1], scalar1=float(LCAP0), scalar2=sb['wt'][:, 6:7], op0=ALU.mult, op1=ALU.add)
            return e.tensor_scalar(out=sb['wt'][:, 13:14], in0=rt[:, 1:2], scalar1=float(LCAP0), scalar2=sb['wt'][:, 7:8], op0=ALU.mult, op1=ALU.add)
        seq.phase('vector', v12)
        def v13(e):
            e.tensor_tensor(out=sb['wt'][:, 14:15], in0=sb['wt'][:, 12:13], in1=sb['wt'][:, 10:11], op=ALU.mult)
            return e.tensor_tensor(out=sb['wt'][:, 15:16], in0=sb['wt'][:, 13:14], in1=sb['wt'][:, 11:12], op=ALU.mult)
        seq.phase('vector', v13)
        def v14(e):
            e.tensor_scalar(out=sb['wt'][:, 16:17], in0=sb['wt'][:, 10:11], scalar1=-float(ZSLOT0), scalar2=float(ZSLOT0), op0=ALU.mult, op1=ALU.add)
            return e.tensor_scalar(out=sb['wt'][:, 17:18], in0=sb['wt'][:, 11:12], scalar1=-float(ZSLOT0), scalar2=float(ZSLOT0), op0=ALU.mult, op1=ALU.add)
        seq.phase('vector', v14)
        def v15(e):
            e.tensor_tensor(out=sb['wt'][:, 18:19], in0=sb['wt'][:, 14:15], in1=sb['wt'][:, 16:17], op=ALU.add)
            return e.tensor_tensor(out=sb['wt'][:, 19:20], in0=sb['wt'][:, 15:16], in1=sb['wt'][:, 17:18], op=ALU.add)
        seq.phase('vector', v15)
        def v16(e):
            e.tensor_copy(out=sb['ixt1'][:], in_=sb['wt'][:, 18:19])
            return e.tensor_copy(out=sb['ixt2'][:], in_=sb['wt'][:, 19:20])
        seq.phase('vector', v16)
        def v17(e, rt=rt):
            e.tensor_tensor(out=sb['pvt'][:, 0:1], in0=rt[:, 2:3], in1=sb['wt'][:, 10:11], op=ALU.mult)
            return e.tensor_tensor(out=sb['pvt'][:, 1:2], in0=rt[:, 3:4], in1=sb['wt'][:, 11:12], op=ALU.mult)
        seq.phase('vector', v17)
        seq.phase_dma('sync', lambda e, j=j: [
            e.dma_start(out=A['dst1'][j*128:(j+1)*128, :], in_=sb['ixt1'][:]),
            e.dma_start(out=A['dst2'][j*128:(j+1)*128, :], in_=sb['ixt2'][:]),
            e.dma_start(out=A['pv'][j*128:(j+1)*128, :], in_=sb['pvt'][:]),
        ], 3)
        seq.phase_dma('gpsimd', lambda e, j=j: [
            e.indirect_dma_start(out=A['bucket0'], out_offset=bass.IndirectOffsetOnAxis(ap=sb['ixt1'][:], axis=0),
                                 in_=sb['lnta'][:, j, :], in_offset=None),
            e.indirect_dma_start(out=A['bucket0'], out_offset=bass.IndirectOffsetOnAxis(ap=sb['ixt2'][:], axis=0),
                                 in_=sb['lnta'][:, j, :], in_offset=None)], 2)
    sec.end()


def emit_s9_moeffn(sec, A):
    seq = sec.begin(); sb = sec.sb; S, P = sec.S, sec.P
    S('w1', [128, 8, 2048], BF16)
    S('w2', [128, 16, 1024], BF16)
    S('xb', [128, 8, LCAP0], BF16)
    S('fc1', [128, 16, LCAP0], BF16)
    S('o2', [128, 2, 1024])
    S('zz', [1, 1024])
    psF = [P("psF0", [128, 512]), P("psF1", [128, 512])]
    psO2 = [P("psO20", [128, 1024]), P("psO21", [128, 1024])]
    seq.phase('vector', lambda e: e.memset(sb['zz'][:], 0.0))
    seq.phase_dma('sync', lambda e: [e.dma_start(out=A['fc2buf'][ZSLOT0:ZSLOT0+1, :], in_=sb['zz'][:])], 1)
    t_mm1_last = None
    t_mm2_last = None
    t_cp = {0: None, 1: None}
    t_st = {0: None, 1: None}
    t_gelu = {0: None, 1: None}
    for ex in range(8):
        t_w = seq.phase_dma('gpsimd', lambda e, ex=ex: [
            e.dma_start(out=sb['w1'][:], in_=A['w1'][ex].rearrange("(c p) f -> p c f", p=128)),
            e.dma_start(out=sb['w2'][:], in_=A['w2'][ex].rearrange("(c p) f -> p c f", p=128)),
        ], 2, deps=[t_mm1_last, t_mm2_last] if ex else None)
        t_xb = seq.phase_dma('sync', lambda e, ex=ex: [
            e.dma_start(out=sb['xb'][:, fc, :], in_=A['bucket0'][ex*LCAP0:(ex+1)*LCAP0, fc*128:(fc+1)*128],
                        transpose=True) for fc in range(8)
        ], 8, deps=[t_mm1_last] if ex else None)
        j = 0
        t_g_last = None
        for Fc in range(16):
            for sc in range(LCAP0 // 512):
                par = j % 2
                def mm(e, Fc=Fc, sc=sc, par=par):
                    last = None
                    for fc in range(8):
                        last = e.matmul(psF[par][:], sb['w1'][:, fc, Fc*128:(Fc+1)*128],
                                        sb['xb'][:, fc, sc*512:(sc+1)*512],
                                        start=(fc == 0), stop=(fc == 7))
                    return last
                t_mm1_last = seq.phase('tensor', mm, deps=[t_w, t_xb, t_gelu[par], t_mm2_last])
                t_gelu[par] = seq.phase('scalar', lambda e, Fc=Fc, sc=sc, par=par: e.activation(
                    out=sb['fc1'][:, Fc, sc*512:(sc+1)*512], in_=psF[par][:], func=AF.Gelu_apprx_tanh),
                    deps=[t_mm1_last, t_mm2_last])
                t_g_last = t_gelu[par]
                j += 1
        for sc2 in range(LCAP0 // 128):
            par = sc2 % 2
            def mm2(e, sc2=sc2, par=par):
                last = None
                for half in range(2):
                    for Fc in range(16):
                        last = e.matmul(psO2[par][:, half*512:(half+1)*512],
                                        sb['fc1'][:, Fc, sc2*128:(sc2+1)*128],
                                        sb['w2'][:, Fc, half*512:(half+1)*512],
                                        start=(Fc == 0), stop=(Fc == 15))
                return last
            t_mm2_last = seq.phase('tensor', mm2, deps=[t_g_last, t_cp[par]])
            t_cp[par] = seq.phase('vector', lambda e, par=par: e.tensor_copy(out=sb['o2'][:, par, :], in_=psO2[par][:]),
                                  deps=[t_mm2_last, t_st[par]])
            t_st[par] = seq.phase_dma('sync', lambda e, ex=ex, sc2=sc2, par=par: [
                e.dma_start(out=A['fc2buf'][ex*LCAP0+sc2*128: ex*LCAP0+(sc2+1)*128, :], in_=sb['o2'][:, par, :]),
            ], 1, deps=[t_cp[par]])
    sec.end()


def emit_s10_ret0(sec, A):
    seq = sec.begin(); sb = sec.sb; S, P = sec.S, sec.P
    S('ix1', [128, 4], I32); S('ix2', [128, 4], I32); S('pv', [128, 8])
    S('g1', [128, 4, 1024]); S('g2', [128, 4, 1024]); S('g2s', [128, 4, 1024])
    S('x1t', [128, 4, 1024]); S('h1t', [128, 4, 1024])
    S('acc', [128, 4, 1024]); S('moe', [128, 4, 1024])
    x1_r = A['x1r'].rearrange("(p s) f -> s p f", s=16)
    h1_rr = A['h1'].rearrange("(p s) f -> s p f", s=16)
    for g in range(8):
        def lds(e, g=g):
            ds = []
            for t in range(4):
                j = g*4 + t
                s_, pb_ = j // 2, j % 2
                ds.append(e.dma_start(out=sb['ix1'][:, t:t+1], in_=A['dst1'][j*128:(j+1)*128, :]))
                ds.append(e.dma_start(out=sb['ix2'][:, t:t+1], in_=A['dst2'][j*128:(j+1)*128, :]))
                ds.append(e.dma_start(out=sb['pv'][:, 2*t:2*t+2], in_=A['pv'][j*128:(j+1)*128, :]))
                ds.append(e.dma_start(out=sb['x1t'][:, t, :], in_=x1_r[s_, pb_*128:(pb_+1)*128, :]))
                ds.append(e.dma_start(out=sb['h1t'][:, t, :], in_=h1_rr[s_, pb_*128:(pb_+1)*128, :]))
            return ds
        seq.phase_dma('sync', lds, 20)
        def gth(e):
            ds = []
            for t in range(4):
                ds.append(e.indirect_dma_start(out=sb['g1'][:, t, :], out_offset=None, in_=A['fc2buf'],
                                               in_offset=bass.IndirectOffsetOnAxis(ap=sb['ix1'][:, t:t+1], axis=0)))
                ds.append(e.indirect_dma_start(out=sb['g2'][:, t, :], out_offset=None, in_=A['fc2buf'],
                                               in_offset=bass.IndirectOffsetOnAxis(ap=sb['ix2'][:, t:t+1], axis=0)))
            return ds
        seq.phase_dma('gpsimd', gth, 8)
        def v1(e):
            last = None
            for t in range(4):
                last = e.tensor_scalar(out=sb['g2s'][:, t, :], in0=sb['g2'][:, t, :], scalar1=sb['pv'][:, 2*t+1:2*t+2], scalar2=None, op0=ALU.mult)
            return last
        seq.phase('vector', v1)
        def v2(e):
            last = None
            for t in range(4):
                last = e.scalar_tensor_tensor(out=sb['moe'][:, t, :], in0=sb['g1'][:, t, :], scalar=sb['pv'][:, 2*t:2*t+1], in1=sb['g2s'][:, t, :], op0=ALU.mult, op1=ALU.add)
            return last
        seq.phase('vector', v2)
        seq.phase('vector', lambda e: e.tensor_tensor(out=sb['acc'][:], in0=sb['x1t'][:], in1=sb['h1t'][:], op=ALU.add))
        seq.phase('vector', lambda e: e.tensor_tensor(out=sb['x1t'][:], in0=sb['acc'][:], in1=sb['moe'][:], op=ALU.add))
        seq.phase_dma('sync', lambda e, g=g: [
            e.dma_start(out=A['x2loc'][(g*4+t)*128:(g*4+t+1)*128, :], in_=sb['x1t'][:, t, :]) for t in range(4)
        ], 4)
    sec.end()


def emit_s12_qkv2(sec, A):
    seq = sec.begin(); sb = sec.sb; S, P = sec.S, sec.P
    S('wq', [128, 8, 1024], BF16)
    S('wkv', [128, 8, 2048], BF16)
    S('idf', [128, 128]); S('idb', [128, 128], BF16); S('scal', [128, 4])
    S('xt', [128, 4, 1024]); S('ut', [128, 4, 1024])
    S('lnstat', [128, 32]); S('lnscr', [128, 4, 1024])
    S('ufm', [128, 8, 512], BF16)
    S('outb', [128, 8, 512], BF16)
    S('vfm', [128, 8, 512], BF16)
    S('vtmS', [128, 4, 1024], BF16)
    psA = P("psA", [128, 512]); psT = P("psT", [128, 512]); psC = P("psC", [128, 1024], BF16)
    seq.phase_dma('gpsimd', lambda e: [
        e.dma_start(out=sb['wq'][:], in_=A['Wq'].rearrange("(c p) o -> p c o", p=128)),
        e.dma_start(out=sb['wkv'][:], in_=A['Wkv'].rearrange("(c p) o -> p c o", p=128)),
    ], 2)
    seq.phase_dma('sync', lambda e: [
        e.dma_start(out=sb['idf'][:], in_=A['IDF']),
        e.dma_start(out=sb['idb'][:], in_=A['IDB']),
        e.dma_start(out=sb['scal'][:], in_=A['SCAL']),
    ], 3)
    for tc in range(8):
        seq.phase_dma('sync', lambda e, tc=tc: [
            e.dma_start(out=sb['xt'][:, i, :], in_=A['x2b'][tc*512+i*128: tc*512+(i+1)*128, :]) for i in range(4)
        ], 4)
        emit_ln(sec, [sb['xt'][:, i, :] for i in range(4)], [sb['ut'][:, i, :] for i in range(4)], 1024)
        for fc in range(8):
            def tr(e, fc=fc):
                last = None
                for i in range(4):
                    last = e.transpose(psT[:, i*128:(i+1)*128], sb['ut'][:, i, fc*128:(fc+1)*128], sb['idf'][:])
                return last
            seq.phase('tensor', tr)
            seq.phase('vector', lambda e, fc=fc: e.tensor_copy(out=sb['ufm'][:, fc, :], in_=psT[:]))
        for oc in range(8):
            def mm(e, oc=oc):
                last = None
                for fc in range(8):
                    last = e.matmul(psA[:], sb['wq'][:, fc, oc*128:(oc+1)*128], sb['ufm'][:, fc, :],
                                    start=(fc == 0), stop=(fc == 7))
                return last
            seq.phase('tensor', mm)
            seq.phase('vector', lambda e, oc=oc: e.tensor_scalar_mul(sb['outb'][:, oc, :], psA[:], 0.125))
        seq.phase_dma('sync', lambda e, tc=tc: [
            e.dma_start(out=A['q1f'][oc*128:(oc+1)*128, tc*512:(tc+1)*512], in_=sb['outb'][:, oc, :]) for oc in range(8)
        ], 8)
        for oc in range(8):
            def mm(e, oc=oc):
                last = None
                for fc in range(8):
                    last = e.matmul(psA[:], sb['wkv'][:, fc, oc*128:(oc+1)*128], sb['ufm'][:, fc, :],
                                    start=(fc == 0), stop=(fc == 7))
                return last
            seq.phase('tensor', mm)
            seq.phase('vector', lambda e, oc=oc: e.tensor_copy(out=sb['outb'][:, oc, :], in_=psA[:]))
        seq.phase_dma('sync', lambda e, tc=tc: [
            e.dma_start(out=A['k1f'][oc*128:(oc+1)*128, tc*512:(tc+1)*512], in_=sb['outb'][:, oc, :]) for oc in range(8)
        ], 8)
        for oc in range(8):
            def mm(e, oc=oc):
                last = None
                for fc in range(8):
                    last = e.matmul(psA[:], sb['wkv'][:, fc, 1024+oc*128:1024+(oc+1)*128], sb['ufm'][:, fc, :],
                                    start=(fc == 0), stop=(fc == 7))
                return last
            seq.phase('tensor', mm)
            seq.phase('vector', lambda e, oc=oc: e.tensor_copy(out=sb['vfm'][:, oc, :], in_=psA[:]))
        for i in range(4):
            def trv(e, i=i):
                last = None
                for oc in range(8):
                    last = e.transpose(psC[:, oc*128:(oc+1)*128], sb['vfm'][:, oc, i*128:(i+1)*128], sb['idb'][:])
                return last
            seq.phase('tensor', trv)
            seq.phase('vector', lambda e, i=i: e.tensor_copy(out=sb['vtmS'][:, i, :], in_=psC[:]))
        seq.phase_dma('sync', lambda e, tc=tc: [
            e.dma_start(out=A['v1tm'][tc*512+i*128: tc*512+(i+1)*128, :], in_=sb['vtmS'][:, i, :]) for i in range(4)
        ], 4)
    sec.end()


def emit_s13_att1(sec, A):
    seq = sec.begin(); sb = sec.sb; S, P = sec.S, sec.P
    S('vtm', [128, 32, 1024], BF16)
    S('wo', [128, 8, 1024], BF16)
    S('dmask', [128, 128]); S('idb', [128, 128], BF16)
    S('q', [64, 2, 2048], BF16); S('kg', [64, 2, 2048], BF16)
    S('sS', [128, 2048]); S('sP', [128, 2048], BF16); S('sPT', [128, 2048], BF16)
    S('nm', [128, 1]); S('sums', [128, 1]); S('rs', [128, 1])
    S('aoS', [128, 64], BF16)
    S('aofm', [128, 8, T], BF16)
    S('x2t', [128, 1024]); S('x3t', [128, 1024])
    psS = P("psS", [128, 2048]); psPT = P("psPT", [128, 1024], BF16)
    psO = P("psO", [128, 64]); psPRJ = P("psPRJ", [128, 1024])
    seq.phase_dma('sync', lambda e: [
        e.dma_start(out=sb['vtm'][:], in_=A['v1tm'].rearrange("(n p) f -> p n f", p=128)),
        e.dma_start(out=sb['dmask'][:], in_=A['DMASK']),
        e.dma_start(out=sb['idb'][:], in_=A['IDB']),
    ], 3)
    seq.phase_dma('gpsimd', lambda e: [
        e.dma_start(out=sb['wo'][:], in_=A['Wo'].rearrange("(c p) o -> p c o", p=128)),
    ], 1)

    iters = [(b, h, qt) for b in range(2) for h in range(16) for qt in range(16)]
    t_ld = {}
    t_lastqk = {}
    t = {'evsc': None, 'exp': None, 'nm': None, 'trp': None, 'cpy': None,
         'pv': None, 'rs': None, 'aoS': None, 'tr2': None, 'cp2': None}

    def emit_load(bh):
        b, h = bh
        par = h % 2
        dep = t_lastqk.get((b, h - 2) if h >= 2 else (b - 1, h + 14), None)
        t_ld[bh] = seq.phase_dma('sync', lambda e, h=h, b=b, par=par: [
            e.dma_start(out=sb['q'][:, par, :], in_=A['q1f'][h*64:(h+1)*64, b*2048:(b+1)*2048]),
            e.dma_start(out=sb['kg'][:, par, :], in_=A['k1f'][h*64:(h+1)*64, b*2048:(b+1)*2048]),
        ], 2, deps=[dep] if dep else None)

    def emit_qk(i):
        b, h, qt = iters[i]
        par = h % 2
        def qk(e, qt=qt, par=par):
            last = None
            for kc in range(qt + 1):
                last = e.matmul(psS[:, kc*128:(kc+1)*128],
                                sb['q'][:, par, qt*128:(qt+1)*128],
                                sb['kg'][:, par, kc*128:(kc+1)*128],
                                start=True, stop=True)
            return last
        tok = seq.phase('tensor', qk, deps=[t_ld[(b, h)], t['evsc']])
        t_lastqk[(b, h)] = tok
        return tok

    def emit_head(i, t_qk):
        b, h, qt = iters[i]
        nk = (qt + 1) * 128
        def evsc(e, qt=qt, nk=nk):
            if qt > 0:
                e.tensor_copy(out=sb['sS'][:, 0:qt*128], in_=psS[:, 0:qt*128])
            return e.scalar_tensor_tensor(out=sb['sS'][:, qt*128:nk], in0=psS[:, qt*128:nk],
                                          scalar=1.0, in1=sb['dmask'][:], op0=ALU.mult, op1=ALU.add)
        t['evsc'] = seq.phase('vector', evsc, deps=[t_qk, t['exp'], t['nm']])

    def emit_body(i):
        b, h, qt = iters[i]
        nk = (qt + 1) * 128
        t['nm'] = seq.phase('vector', lambda e, nk=nk: e.tensor_reduce(out=sb['nm'][:], in_=sb['sS'][:, 0:nk], axis=AX.X, op=ALU.max, negate=True),
                            deps=[t['evsc'], t['exp']])
        t['exp'] = seq.phase('scalar', lambda e, nk=nk: e.activation(out=sb['sP'][:, 0:nk], in_=sb['sS'][:, 0:nk], func=AF.Exp,
                                                                     bias=sb['nm'][:, 0:1], accum_out=sb['sums'][:, 0:1]),
                             deps=[t['nm'], t['trp'], t['rs']])
        tprev_cpy = t['cpy']
        for half in range(2):
            kcs = list(range(half*8, min(qt+1, half*8+8)))
            if len(kcs) == 0:
                continue
            def trp(e, kcs=kcs, half=half):
                last = None
                for kc in kcs:
                    last = e.transpose(psPT[:, (kc-half*8)*128:(kc-half*8+1)*128],
                                       sb['sP'][:, kc*128:(kc+1)*128], sb['idb'][:])
                return last
            t['trp'] = seq.phase('tensor', trp, deps=[t['exp'], t['cpy'], t['cp2']])
            t['cpy'] = seq.phase('vector', lambda e, kcs=kcs, half=half: e.tensor_copy(
                out=sb['sPT'][:, kcs[0]*128: (kcs[-1]+1)*128],
                in_=psPT[:, (kcs[0]-half*8)*128:(kcs[-1]-half*8+1)*128]),
                deps=[t['trp'], t['pv']])
        def pv(e, b=b, h=h, qt=qt):
            last = None
            for kc in range(qt + 1):
                ktile = b*16 + kc
                last = e.matmul(psO[:], sb['sPT'][:, kc*128:(kc+1)*128],
                                sb['vtm'][:, ktile, h*64:(h+1)*64],
                                start=(kc == 0), stop=(kc == qt))
            return last
        t['pv'] = seq.phase('tensor', pv, deps=[t['cpy'], t['aoS']])
        t['rs'] = seq.phase('vector', lambda e: e.reciprocal(sb['rs'][:], sb['sums'][:]),
                            deps=[t['exp'], t['aoS']])
        t['aoS'] = seq.phase('vector', lambda e: e.tensor_scalar(out=sb['aoS'][:], in0=psO[:], scalar1=sb['rs'][:, 0:1], scalar2=None, op0=ALU.mult),
                             deps=[t['pv'], t['rs'], t['tr2']])
        t['tr2'] = seq.phase('tensor', lambda e: e.transpose(psPT[0:64, 0:128], sb['aoS'][:], sb['idb'][:]),
                             deps=[t['aoS'], t['cpy']])
        t['cp2'] = seq.phase('vector', lambda e, b=b, h=h, qt=qt: e.tensor_copy(
            out=sb['aofm'][(h % 2)*64:(h % 2)*64+64, h//2, b*2048+qt*128: b*2048+(qt+1)*128],
            in_=psPT[0:64, 0:128]), deps=[t['tr2']])

    emit_load((0, 0))
    emit_load((0, 1))
    pend = emit_qk(0)
    for i in range(len(iters)):
        emit_head(i, pend)
        if i + 1 < len(iters):
            bn, hn, qn = iters[i+1]
            if qn == 0:
                nb, nh = (bn, hn + 1) if hn < 15 else (bn + 1, 0)
                if nb < 2:
                    emit_load((nb, nh))
            pend = emit_qk(i + 1)
        emit_body(i)
    for tt in range(32):
        tx = seq.phase_dma('sync', lambda e, tt=tt: [e.dma_start(out=sb['x2t'][:], in_=A['x2b'][tt*128:(tt+1)*128, :])], 1)
        def prj(e, tt=tt):
            last = None
            for half in range(2):
                for fc in range(8):
                    last = e.matmul(psPRJ[:, half*512:(half+1)*512],
                                    sb['aofm'][:, fc, tt*128:(tt+1)*128],
                                    sb['wo'][:, fc, half*512:(half+1)*512],
                                    start=(fc == 0), stop=(fc == 7))
            return last
        seq.phase('tensor', prj)
        seq.phase('vector', lambda e: e.tensor_tensor(out=sb['x3t'][:], in0=sb['x2t'][:], in1=psPRJ[:], op=ALU.add))
        seq.phase_dma('sync', lambda e, tt=tt: [e.dma_start(out=A['x3'][tt*128:(tt+1)*128, :], in_=sb['x3t'][:])], 1)
    sec.end()


def emit_s14_ffn1a(sec, A):
    seq = sec.begin(); sb = sec.sb; S, P = sec.S, sec.P
    S('centT', [128, 8, 8], BF16)
    S('tri', [128, 128]); S('iota8', [128, 8]); S('onesf', [128, 1]); S('onesr', [1, 128])
    S('idf', [128, 128]); S('scal', [128, 4])
    S('xt', [128, 2, 1024]); S('ut', [128, 2, 1024]); S('nt', [128, 2, 1024])
    S('ntb', [128, 2, 1024], BF16)
    S('lnstat', [128, 32]); S('lnscr', [128, 2, 1024])
    S('ufm', [128, 8, 256], BF16)
    S('af', [128, 16]); S('mx', [128, 8]); S('oh', [128, 8]); S('ec', [128, 8])
    S('slotsS', [128, 8]); S('tmpC', [128, 8])
    S('wt', [128, 16])
    S('ix', [128, 1], I32)
    S('rtA', [1, 8]); S('rtB', [1, 8]); S('rtS', [1, 8])
    psT = P("psT", [128, 512]); psAF = P("psAF", [128, 16])
    psSL = P("psSL", [128, 8]); psR2 = P("psR2", [1, 8])
    sec._psSL = psSL; sec._psR2 = psR2
    seq.phase_dma('gpsimd', lambda e: [
        e.dma_start(out=sb['centT'][:], in_=A['centT'].rearrange("(c p) o -> p c o", p=128)),
    ], 1)
    seq.phase_dma('sync', lambda e: [
        e.dma_start(out=sb['tri'][:], in_=A['TRI128']),
        e.dma_start(out=sb['iota8'][:], in_=A['IOTA8']),
        e.dma_start(out=sb['onesf'][:], in_=A['ONESF']),
        e.dma_start(out=sb['onesr'][:], in_=A['ONESR']),
        e.dma_start(out=sb['idf'][:], in_=A['IDF']),
        e.dma_start(out=sb['scal'][:], in_=A['SCAL']),
    ], 6)
    seq.phase('vector', lambda e: e.memset(sb['rtA'][:], 0.0))
    for pr in range(16):
        seq.phase_dma('sync', lambda e, pr=pr: [
            e.dma_start(out=sb['xt'][:], in_=A['x3'][pr*256:(pr+1)*256, :].rearrange("(t p) f -> p t f", p=128)),
        ], 1)
        emit_ln(sec, [sb['xt'][:, t, :] for t in range(2)], [sb['ut'][:, t, :] for t in range(2)], 1024)
        seq.phase_dma('sync', lambda e, pr=pr: [
            e.dma_start(out=A['u3'][pr*256:(pr+1)*256, :].rearrange("(t p) f -> p t f", p=128), in_=sb['ut'][:]),
        ], 1)
        emit_ln(sec, [sb['ut'][:, t, :] for t in range(2)], [sb['nt'][:, t, :] for t in range(2)], 1024)
        seq.phase('vector', lambda e: e.tensor_copy(out=sb['ntb'][:], in_=sb['nt'][:]))
        for q in range(4):
            def tr(e, q=q):
                last = None
                for h in range(2):
                    for t in range(2):
                        last = e.transpose(psT[:, (h*2+t)*128:(h*2+t+1)*128],
                                           sb['ut'][:, t, (2*q+h)*128:(2*q+h+1)*128], sb['idf'][:])
                return last
            seq.phase('tensor', tr)
            seq.phase('vector', lambda e, q=q: e.tensor_copy(
                out=sb['ufm'][:, 2*q:2*q+2, :].rearrange("p a b -> p (a b)"), in_=psT[:]))
        def afm(e):
            last = None
            for t in range(2):
                for fc in range(8):
                    last = e.matmul(psAF[:, t*8:(t+1)*8], sb['ufm'][:, fc, t*128:(t+1)*128],
                                    sb['centT'][:, fc, :], start=(fc == 0), stop=(fc == 7))
            return last
        seq.phase('tensor', afm)
        seq.phase('vector', lambda e: e.tensor_copy(out=sb['af'][:], in_=psAF[:]))
        for t in range(2):
            j = 2*pr + t
            _s14_scan(sec, A, j, t)
    sec.end()


def _s14_scan(sec, A, j, t):
    seq, sb = sec.seq, sec.sb
    psSL = sec._psSL; psR2 = sec._psR2
    if True:
        cur_rt = 'rtA' if j % 2 == 0 else 'rtB'
        nxt_rt = 'rtB' if j % 2 == 0 else 'rtA'
        afs = sb['af'][:, t*8:(t+1)*8]
        seq.phase('vector', lambda e, afs=afs: e.max(out=sb['mx'][:], in_=afs))
        seq.phase('scalar', lambda e: e.activation(out=sb['wt'][:, 0:1], in_=sb['mx'][:, 0:1], func=AF.Sigmoid))
        seq.phase('vector', lambda e, afs=afs: e.tensor_scalar(out=sb['oh'][:], in0=afs, scalar1=sb['mx'][:, 0:1], scalar2=None, op0=ALU.is_equal))
        seq.phase('vector', lambda e: e.tensor_tensor(out=sb['ec'][:], in0=sb['oh'][:], in1=sb['iota8'][:], op=ALU.mult))
        seq.phase('vector', lambda e: e.tensor_reduce(out=sb['wt'][:, 1:2], in_=sb['ec'][:], axis=AX.X, op=ALU.add))
        def g2(e, cur_rt=cur_rt):
            e.matmul(psSL[:], sb['onesr'][:], sb[cur_rt][:], start=True, stop=False)
            return e.matmul(psSL[:], sb['tri'][:], sb['oh'][:], start=False, stop=True)
        seq.phase('tensor', g2)
        seq.phase('tensor', lambda e: e.matmul(psR2[:], sb['onesf'][:], sb['oh'][:], start=True, stop=True))
        def v7(e):
            e.tensor_copy(out=sb['slotsS'][:], in_=psSL[:])
            return e.tensor_copy(out=sb['rtS'][:], in_=psR2[:])
        seq.phase('vector', v7)
        seq.phase('vector', lambda e, cur_rt=cur_rt, nxt_rt=nxt_rt: e.tensor_tensor(out=sb[nxt_rt][:], in0=sb[cur_rt][:], in1=sb['rtS'][:], op=ALU.add))
        seq.phase('vector', lambda e: e.tensor_tensor(out=sb['tmpC'][:], in0=sb['slotsS'][:], in1=sb['oh'][:], op=ALU.mult))
        seq.phase('vector', lambda e: e.tensor_reduce(out=sb['wt'][:, 2:3], in_=sb['tmpC'][:], axis=AX.X, op=ALU.add))
        seq.phase('vector', lambda e: e.tensor_scalar(out=sb['wt'][:, 3:4], in0=sb['wt'][:, 2:3], scalar1=float(LCAP1), scalar2=None, op0=ALU.is_lt))
        seq.phase('vector', lambda e: e.tensor_scalar(out=sb['wt'][:, 4:5], in0=sb['wt'][:, 1:2], scalar1=float(LCAP1), scalar2=sb['wt'][:, 2:3], op0=ALU.mult, op1=ALU.add))
        seq.phase('vector', lambda e: e.tensor_tensor(out=sb['wt'][:, 5:6], in0=sb['wt'][:, 4:5], in1=sb['wt'][:, 3:4], op=ALU.mult))
        seq.phase('vector', lambda e: e.tensor_scalar(out=sb['wt'][:, 6:7], in0=sb['wt'][:, 3:4], scalar1=-float(ZSLOT1), scalar2=float(ZSLOT1), op0=ALU.mult, op1=ALU.add))
        seq.phase('vector', lambda e: e.tensor_tensor(out=sb['wt'][:, 7:8], in0=sb['wt'][:, 5:6], in1=sb['wt'][:, 6:7], op=ALU.add))
        seq.phase('vector', lambda e: e.tensor_copy(out=sb['ix'][:], in_=sb['wt'][:, 7:8]))
        seq.phase_dma('sync', lambda e, j=j: [
            e.dma_start(out=A['alpha'][j*128:(j+1)*128, :], in_=sb['wt'][:, 0:1]),
            e.dma_start(out=A['dsti1'][j*128:(j+1)*128, :], in_=sb['ix'][:]),
        ], 2)
        seq.phase_dma('gpsimd', lambda e, t=t: [
            e.indirect_dma_start(out=A['bucket1'], out_offset=bass.IndirectOffsetOnAxis(ap=sb['ix'][:, 0:1], axis=0),
                                 in_=sb['ntb'][:, t, :], in_offset=None)], 1)


def emit_s15_ffn1b(sec, A):
    seq = sec.begin(); sb = sec.sb; S, P = sec.S, sec.P
    S('w1', [128, 8, 4096], BF16)
    S('w2', [128, 32, 1024], BF16)
    S('lng', [128, 8]); S('lnb', [128, 8]); S('b1c', [128, 32]); S('b2r', [128, 1024])
    S('xbe', [128, 8, LCAP1], BF16)
    S('xg', [128, 2, 8, 256], BF16)
    S('fc1', [128, 32, 256], BF16)
    S('o2', [128, 2, 1024])
    S('zz', [1, 1024])
    psF = [P("psF0", [128, 256]), P("psF1", [128, 256])]
    psO2 = [P("psO20", [128, 1024]), P("psO21", [128, 1024])]
    seq.phase('vector', lambda e: e.memset(sb['zz'][:], 0.0))
    seq.phase_dma('sync', lambda e: [e.dma_start(out=A['fc2buf1'][ZSLOT1:ZSLOT1+1, :], in_=sb['zz'][:])], 1)
    t_mm1_last = None
    t_mm2_last = None
    t_cp = {0: None, 1: None}
    t_st = {0: None, 1: None}
    t_relu = {0: None, 1: None}
    t_gb = {0: None, 1: None}
    for ex in range(8):
        t_w = seq.phase_dma('gpsimd', lambda e, ex=ex: [
            e.dma_start(out=sb['w1'][:], in_=A['w1T'][ex].rearrange("(c p) f -> p c f", p=128)),
            e.dma_start(out=sb['w2'][:], in_=A['w2T'][ex].rearrange("(c p) f -> p c f", p=128)),
        ], 2, deps=[t_mm1_last, t_mm2_last] if ex else None)
        t_c = seq.phase_dma('sync', lambda e, ex=ex: [
            e.dma_start(out=sb['lng'][:], in_=A['LNG'][ex]),
            e.dma_start(out=sb['lnb'][:], in_=A['LNB'][ex]),
            e.dma_start(out=sb['b1c'][:], in_=A['B1C'][ex]),
            e.dma_start(out=sb['b2r'][:], in_=A['B2REP1'][ex]),
        ], 4, deps=[t_gb[0], t_gb[1], t_relu[0], t_relu[1], t_cp[0], t_cp[1], t_mm2_last] if ex else None)
        t_xb = seq.phase_dma('sync', lambda e, ex=ex: [
            e.dma_start(out=sb['xbe'][:, fc, :], in_=A['bucket1'][ex*LCAP1:(ex+1)*LCAP1, fc*128:(fc+1)*128],
                        transpose=True) for fc in range(8)
        ], 8, deps=[t_gb[0], t_gb[1]] if ex else None)
        for sc in range(LCAP1 // 256):
            scp = sc % 2
            def gb(e, sc=sc, scp=scp):
                last = None
                for fc in range(8):
                    last = e.tensor_scalar(out=sb['xg'][:, scp, fc, :], in0=sb['xbe'][:, fc, sc*256:(sc+1)*256],
                                           scalar1=sb['lng'][:, fc:fc+1], scalar2=sb['lnb'][:, fc:fc+1],
                                           op0=ALU.mult, op1=ALU.add)
                return last
            t_gb[scp] = seq.phase('vector', gb, deps=[t_c, t_xb, t_mm1_last])
            t_r_last = None
            for Fc in range(32):
                par = Fc % 2
                def mm(e, Fc=Fc, par=par, scp=scp):
                    last = None
                    for fc in range(8):
                        last = e.matmul(psF[par][:], sb['w1'][:, fc, Fc*128:(Fc+1)*128],
                                        sb['xg'][:, scp, fc, :],
                                        start=(fc == 0), stop=(fc == 7))
                    return last
                t_mm1_last = seq.phase('tensor', mm, deps=[t_w, t_gb[scp], t_relu[par], t_mm2_last])
                t_relu[par] = seq.phase('scalar', lambda e, Fc=Fc, par=par: e.activation(
                    out=sb['fc1'][:, Fc, :], in_=psF[par][:], func=AF.Relu, bias=sb['b1c'][:, Fc:Fc+1]),
                    deps=[t_mm1_last, t_mm2_last])
                t_r_last = t_relu[par]
            for sc2 in range(2):
                par = sc2 % 2
                def mm2(e, sc2=sc2, par=par):
                    last = None
                    for half in range(2):
                        for Fc in range(32):
                            last = e.matmul(psO2[par][:, half*512:(half+1)*512],
                                            sb['fc1'][:, Fc, sc2*128:(sc2+1)*128],
                                            sb['w2'][:, Fc, half*512:(half+1)*512],
                                            start=(Fc == 0), stop=(Fc == 31))
                    return last
                t_mm2_last = seq.phase('tensor', mm2, deps=[t_r_last, t_cp[par]])
                t_cp[par] = seq.phase('vector', lambda e, par=par: e.scalar_tensor_tensor(
                    out=sb['o2'][:, par, :], in0=psO2[par][:], scalar=1.0, in1=sb['b2r'][:],
                    op0=ALU.mult, op1=ALU.add), deps=[t_mm2_last, t_c, t_st[par]])
                t_st[par] = seq.phase_dma('sync', lambda e, ex=ex, sc=sc, sc2=sc2, par=par: [
                    e.dma_start(out=A['fc2buf1'][ex*LCAP1+sc*256+sc2*128: ex*LCAP1+sc*256+(sc2+1)*128, :], in_=sb['o2'][:, par, :]),
                ], 1, deps=[t_cp[par]])
    sec.end()


def emit_s16_ret1(sec, A):
    seq = sec.begin(); sb = sec.sb; S, P = sec.S, sec.P
    S('ix', [128, 4], I32); S('al', [128, 4])
    S('g1', [128, 4, 1024]); S('gs', [128, 4, 1024])
    S('x3t', [128, 4, 1024]); S('u3t', [128, 4, 1024])
    S('a1', [128, 4, 1024])
    for g in range(8):
        def lds(e, g=g):
            ds = []
            for t in range(4):
                j = g*4 + t
                ds.append(e.dma_start(out=sb['ix'][:, t:t+1], in_=A['dsti1'][j*128:(j+1)*128, :]))
                ds.append(e.dma_start(out=sb['al'][:, t:t+1], in_=A['alpha'][j*128:(j+1)*128, :]))
            ds.append(e.dma_start(out=sb['x3t'][:], in_=A['x3'][g*512:(g+1)*512, :].rearrange("(t p) f -> p t f", p=128)))
            ds.append(e.dma_start(out=sb['u3t'][:], in_=A['u3'][g*512:(g+1)*512, :].rearrange("(t p) f -> p t f", p=128)))
            return ds
        seq.phase_dma('sync', lds, 10)
        seq.phase_dma('gpsimd', lambda e: [
            e.indirect_dma_start(out=sb['g1'][:, t, :], out_offset=None, in_=A['fc2buf1'],
                                 in_offset=bass.IndirectOffsetOnAxis(ap=sb['ix'][:, t:t+1], axis=0))
            for t in range(4)], 4)
        def v1(e):
            last = None
            for t in range(4):
                last = e.tensor_scalar(out=sb['gs'][:, t, :], in0=sb['g1'][:, t, :], scalar1=sb['al'][:, t:t+1], scalar2=None, op0=ALU.mult)
            return last
        seq.phase('vector', v1)
        seq.phase('vector', lambda e: e.tensor_tensor(out=sb['a1'][:], in0=sb['x3t'][:], in1=sb['u3t'][:], op=ALU.add))
        seq.phase('vector', lambda e: e.tensor_tensor(out=sb['x3t'][:], in0=sb['a1'][:], in1=sb['gs'][:], op=ALU.add))
        seq.phase_dma('sync', lambda e, g=g: [
            e.dma_start(out=A['out'][g*512:(g+1)*512, :].rearrange("(t p) f -> p t f", p=128), in_=sb['x3t'][:]),
        ], 1)
    sec.end()



def build_fused(upto=99, dbg=(), lo=0, ext_in=()):
    nc = bass.Bass("TRN2", target_bir_lowering=False, debug=False)
    Section.counter[0] = 0
    A = {}
    ins_spec = [
        ("x0", [T, D], F32), ("wqT", [D, D], F32), ("wkT", [D, 256], F32), ("wvT", [D, 256], F32),
        ("qnw", [128, 8], F32), ("knw", [128, 2], F32), ("woT", [D, D], F32),
        ("qkvT", [D, 1536], F32), ("W1REP", [128, D], F32), ("B1REP", [128, D], F32),
        ("prjT", [D, D], F32), ("W2REP", [128, D], F32), ("B2REP", [128, D], F32),
        ("rw", [D, 8], F32), ("w1", [8, D, 2048], F32), ("w2", [8, 2048, D], F32),
        ("Wq", [D, D], F32), ("Wkv", [D, 2*D], F32), ("Wo", [D, D], F32),
        ("centT", [D, 8], F32), ("w1T", [8, D, 4096], F32), ("w2T", [8, 4096, D], F32),
        ("LNG", [8, 128, 8], F32), ("LNB", [8, 128, 8], F32), ("B1C", [8, 128, 32], F32),
        ("B2REP1", [8, 128, D], F32),
        ("IDB", [128, 128], BF16), ("IDF", [128, 128], F32), ("IDB64", [128, 64], BF16),
        ("ONESF", [128, 1], F32), ("ONESR", [1, 128], F32),
        ("RELB", [16, 128, 640], BF16), ("DMASK", [128, 128], F32),
        ("IOTA8", [128, 8], F32), ("TRI128", [128, 128], F32), ("TRI16", [16, 16], F32),
        ("ALLONES", [128, 128], F32), ("SROWSEL", [128, 16], F32), ("CSEL", [128, 16], F32),
        ("SELBANK", [16, 16*128], F32), ("SCAL", [128, 4], F32),
        ("CORETRI", [128, 16], F32), ("CID", [128, 1], F32),
        ("TMASKM", [128, 512], BF16), ("IND16", [128, 8], BF16), ("IND16T", [8, 128], F32),
        ("RESHC", [128, 32], F32), ("RESHC2", [128, 32], F32),
    ]
    for nm, shp, dt in ins_spec:
        A[nm] = nc.dram_tensor(nm, shp, dt, kind="ExternalInput").ap()
    mid_spec = [
        ("qf", [D, T], BF16), ("kf", [256, T], BF16), ("vtm", [T, 256], BF16),
        ("x1loc", [T, D], F32), ("x1g", [8*T, D], F32), ("x1r", [T, D], F32),
        ("u1", [T, D], F32), ("qtf", [D, T], BF16), ("ktf", [256, T], BF16), ("vtf", [256, T], BF16),
        ("aofm_t", [D, T], BF16),
        ("h1", [T, D], F32), ("ln2tm", [T, D], BF16), ("ind", [T, 8], F32), ("route", [T, 4], F32),
        ("Rloc", [16, 8], F32), ("Rg", [128, 8], F32),
        ("dst1", [T, 1], I32), ("dst2", [T, 1], I32), ("pv", [T, 2], F32),
        ("bucket0", [8*LCAP0+1, D], BF16), ("fc2buf", [8*LCAP0+1, D], F32),
        ("x2loc", [T, D], F32), ("x2g", [8*T, D], F32), ("x2b", [T, D], F32),
        ("q1f", [D, T], BF16), ("k1f", [D, T], BF16), ("v1tm", [T, D], BF16),
        ("x3", [T, D], F32), ("u3", [T, D], F32), ("alpha", [T, 1], F32), ("dsti1", [T, 1], I32),
        ("bucket1", [8*LCAP1+1, D], BF16), ("fc2buf1", [8*LCAP1+1, D], F32),
    ]
    for nm, shp, dt in mid_spec:
        kind = "ExternalOutput" if nm in dbg else ("ExternalInput" if nm in ext_in else "Internal")
        A[nm] = nc.dram_tensor(nm, shp, dt, kind=kind).ap()
    A['out'] = nc.dram_tensor("out", [T, D], F32, kind="ExternalOutput").ap()

    gctx = ExitStack()
    sems = {k: gctx.enter_context(nc.semaphore(f"G{k}")) for k in ('g', 'h', 'cc', 't', 'v', 's', 'p', 'y')}
    Section.gseq[0] = Seq(nc, sems)
    nc._keepalive = gctx
    sections = [
        lambda sec: emit_s1_qkv0(sec, A),
        lambda sec: emit_s2_att0(sec, A),
        lambda sec: emit_reshard(sec, A, 'x1loc', 'x1g', 'x1r', 'RESHC', 256),
        lambda sec: emit_s4_ffn0a(sec, A),
        lambda sec: emit_s5_tiny(sec, A),
        lambda sec: emit_s6_ffn0b(sec, A),
        lambda sec: emit_s7_rgather(sec, A),
        lambda sec: emit_s8_disp(sec, A),
        lambda sec: emit_s9_moeffn(sec, A),
        lambda sec: emit_s10_ret0(sec, A),
        lambda sec: emit_reshard(sec, A, 'x2loc', 'x2g', 'x2b', 'RESHC2', 512),
        lambda sec: emit_s12_qkv2(sec, A),
        lambda sec: emit_s13_att1(sec, A),
        lambda sec: emit_s14_ffn1a(sec, A),
        lambda sec: emit_s15_ffn1b(sec, A),
        lambda sec: emit_s16_ret1(sec, A),
    ]
    nsec = min(upto, len(sections))
    for i, emit in enumerate(sections):
        if i >= upto or i < lo:
            continue
        sec = Section(nc)
        sec.final = (i == nsec - 1)
        emit(sec)
    return nc


def kernel(**inputs):
    nc = build_fused()
    in_maps = [host_inputs(inputs, c) for c in range(8)]
    res = run_bass_kernel_spmd(nc, in_maps, core_ids=list(range(8)))
    out = np.empty((16, 2048, D), np.float32)
    for c in range(8):
        out[2*c:2*c+2] = np.asarray(res.results[c]["out"]).reshape(2, 2048, D)
    return out
